# version 01
# baseline (speedup 1.0000x reference)
import numpy as np
from contextlib import ExitStack

ENGS = ("sync", "gpsimd", "vector", "tensor", "scalar")

import concourse.bass as bass
import concourse.mybir as mybir
from concourse.bass_utils import run_bass_kernel_spmd

F32 = mybir.dt.float32
BF16 = mybir.dt.bfloat16
AF = mybir.ActivationFunctionType
ALU = mybir.AluOpType

D = 2048
KC = 16
NT = 512
C = 128
NCH = NT // C
NTILES = 8
HA = 16
HB = 16
DK = 64
DV = 128
NEG = -1.0e30
EPS_RMS = float(np.finfo(np.float32).eps)

A_QF, A_KF, A_VF = 0, 8192, 16384
A_H = 0
A_QC, A_KC, A_VC = 0, 4096, 8192
A_GOUT = 16384
A_OF = 24576
ARENA_N = 32768


class Sem:
    __slots__ = ("h", "n", "name")

    def __init__(self, h, name):
        self.h = h
        self.n = 0
        self.name = name


class MK:
    PREFETCH = 2

    def __init__(self, n_sublayers=4, n_tiles=NTILES, tap=None, serialize=0):
        self.n_sublayers = n_sublayers
        self.n_tiles = n_tiles
        self.tap_point = tap
        self.serialize = serialize
        self.nc = bass.Bass("TRN2", target_bir_lowering=False)
        self.eng = None
        self.cur = None
        self._last = {}
        self.sems = {}

    def emit(self, engine, fn):
        self._last_inc[engine] = None
        if self.eng == engine:
            ins = fn(self.cur)
            self._last[engine] = ins
            return ins
        return None

    def inc(self, engine, semname, k=1):
        s = self.sems[semname]
        if self.eng == engine:
            self._last[engine].then_inc(s.h, k)
        s.n += k
        self._last_inc[engine] = (semname, s.n)
        return s.n

    def n_of(self, semname):
        return self.sems[semname].n

    def wait(self, engine, semname, val=None):
        s = self.sems[semname]
        v = s.n if val is None else val
        assert v <= s.n, f"wait on future count {semname} {v}>{s.n}"
        cache = self._wcache.setdefault(engine, {})
        if cache.get(semname, 0) >= v:
            return
        cache[semname] = v
        if self.eng == engine and v > 0:
            self.cur.wait_ge(s.h, v)

    ENG_SEM = {"vector": "ve", "scalar": "ac", "gpsimd": "po",
               "tensor": "pe"}

    def selfsync(self, engine):
        li = self._last_inc.get(engine)
        if li is not None:
            sn, n = li
        else:
            sn = self.ENG_SEM[engine]
            n = self.inc(engine, sn)
        if self.eng == engine:
            self.cur.wait_ge(self.sems[sn].h, n)
        cache = self._wcache.setdefault(engine, {})
        cache[sn] = max(cache.get(sn, 0), n)

    def bank(self, b16=False):
        lst = self.banks_b16 if b16 else self.banks_f32
        if b16:
            i = self.bi16
            self.bi16 = (i + 1) % len(lst)
        else:
            i = self.bi32
            self.bi32 = (i + 1) % len(lst)
        b = lst[i]
        if b[1] is not None:
            self.wait("tensor", b[1], b[2])
        return b

    def bank_release(self, b, semname):
        b[1] = semname
        b[2] = self.n_of(semname)

    def ar(self, off, n):
        return self.arena[:, off:off + n]

    def alloc(self, E):
        nc = self.nc
        self.x_in = nc.dram_tensor("x", [2, KC, 128, 2048], F32,
                                   kind="ExternalInput")
        self.x_out = nc.dram_tensor("out", [2, KC, 128, 2048], F32,
                                    kind="ExternalOutput")
        self.w_shapes = [("wq", 16, 16), ("wk", 16, 16), ("wv", 16, 16),
                         ("wg", 64, 16), ("wu", 64, 16), ("wd", 16, 64),
                         ("wqk", 16, 16), ("wvg", 32, 16), ("wo", 16, 16)]
        self.w_in = {n: nc.dram_tensor(n, [m, 128, k * 128], F32,
                                       kind="ExternalInput")
                     for n, m, k in self.w_shapes}
        self.w_scr = {n: nc.dram_tensor(n + "_s", [m, 128, k * 128], BF16,
                                        kind="Internal")
                      for n, m, k in self.w_shapes}
        self.wsm_in = {n: nc.dram_tensor(n, [128, KC * 16], F32,
                                         kind="ExternalInput")
                       for n in ("wal", "wbe", "wa", "wb")}
        self.cv_in = {n: nc.dram_tensor(n, [128, c * 4], F32,
                                        kind="ExternalInput")
                      for n, c in [("cq", 8), ("ck", 8), ("cv", 16)]}
        self.bias_in = {}
        for n in ("bq", "bk", "bv"):
            self.bias_in[n] = nc.dram_tensor(n, [128, 16], F32,
                                             kind="ExternalInput")
        for n in ("bal", "bbe", "bbb", "bbt", "bdt"):
            self.bias_in[n] = nc.dram_tensor(n, [16, 1], F32,
                                             kind="ExternalInput")
        self.alog_in = nc.dram_tensor("alog", [16, 1], F32,
                                      kind="ExternalInput")

        self.xf = E(nc.sbuf_tensor("xf", [128, KC * NT], F32))
        self.xb = E(nc.sbuf_tensor("xbr", [128, KC * NT], BF16))
        self.arena = E(nc.sbuf_tensor("arena", [128, ARENA_N], BF16))
        self.wbuf = [E(nc.sbuf_tensor(f"wbuf{i}", [128, 2048], BF16))
                     for i in range(3)]
        self.rows = E(nc.sbuf_tensor("rows", [1, 1536], F32))
        self.ones_col16 = E(nc.sbuf_tensor("ones_col16", [128, 1], BF16))
        self.ones_row32 = E(nc.sbuf_tensor("ones_row32", [1, 128], F32))
        self.r_b = E(nc.sbuf_tensor("r_b", [128, NT], F32))
        self.m_b = E(nc.sbuf_tensor("m_b", [128, NT], F32))
        self.ident16 = E(nc.sbuf_tensor("ident16", [128, 128], BF16))
        self.ident32 = E(nc.sbuf_tensor("ident32", [128, 128], F32))
        self.tril32 = E(nc.sbuf_tensor("tril32", [128, 128], F32))
        self.slow32 = E(nc.sbuf_tensor("slow32", [128, 128], F32))
        self.addUC = E(nc.sbuf_tensor("addUC", [128, 128], F32))
        self.addSL = E(nc.sbuf_tensor("addSL", [128, 128], F32))
        self.bcol = {n: E(nc.sbuf_tensor("bc_" + n, [128, 16], F32))
                     for n in ("bq", "bk", "bv")}
        self.bsm = {n: E(nc.sbuf_tensor("bs_" + n, [16, 1], F32))
                    for n in ("bal", "bbe", "bbb", "bbt", "bdt", "bb2")}
        self.wsm16 = {n: E(nc.sbuf_tensor("w16_" + n, [128, KC * 16], BF16))
                      for n in ("wal", "wbe", "wa", "wb")}
        self.cvw = {n: E(nc.sbuf_tensor("cvw_" + n, [128, c * 4], F32))
                    for n, c in [("cq", 8), ("ck", 8), ("cv", 16)]}
        self.negexpA = E(nc.sbuf_tensor("negexpA", [16, 1], F32))
        self.alog_sb = E(nc.sbuf_tensor("alog_sb", [16, 1], F32))
        self.stA = E(nc.sbuf_tensor("stA", [128, HA * 128], F32))
        self.stA16 = E(nc.sbuf_tensor("stA16", [128, HA * 128], BF16))
        self.stB = E(nc.sbuf_tensor("stB", [128, (HB // 2) * 128], F32))
        self.stB16 = E(nc.sbuf_tensor("stB16", [128, (HB // 2) * 128], BF16))
        self.colt = {n: E(nc.sbuf_tensor("colt_" + n, [128, 32], F32))
                     for n in ("la", "lb", "cla", "dtx", "etd", "ecd",
                               "edte", "ebt", "bias", "eb2", "g", "G", "dG",
                               "eG", "edG", "etdB", "Glb", "nbG")}
        self.za = E(nc.sbuf_tensor("za", [16, NT], F32))
        self.zb = E(nc.sbuf_tensor("zb", [16, NT], F32))
        self.loga = E(nc.sbuf_tensor("loga", [16, NT], F32))
        self.logb = E(nc.sbuf_tensor("logb", [16, NT], F32))
        self.zt1 = E(nc.sbuf_tensor("zt1", [16, NT], F32))
        self.zt2 = E(nc.sbuf_tensor("zt2", [16, NT], F32))
        self.gbuf = E(nc.sbuf_tensor("gbuf", [16, NT], F32))
        self.stage = [E(nc.sbuf_tensor(f"stage{i}", [128, 515], BF16))
                      for i in range(3)]
        self.xsq = [self.stage[0][:, 0:NT], self.stage[1][:, 0:NT]]
        self.tails = E(nc.sbuf_tensor("tails", [128, 96], F32))
        self.tb = {}
        for name, w, dt, nb in [
            ("ktok", 128, BF16, 2), ("vtok", 128, BF16, 2),
            ("kw", 128, BF16, 2), ("attns", 128, BF16, 2),
            ("DT", 128, BF16, 2), ("X2", 128, F32, 2),
            ("ytok", 128, BF16, 2), ("yg", 128, F32, 2),
            ("Mb", 128, BF16, 2), ("Mpr", 128, BF16, 2),
            ("Mng", 128, BF16, 2),
            ("pwA", 128, BF16, 2), ("pwAT", 128, BF16, 2),
            ("pwB", 128, BF16, 2), ("pwBT", 128, BF16, 2),
            ("tj", 128, BF16, 2), ("U16", 128, BF16, 2),
            ("KS", 128, BF16, 2), ("Ab", 128, BF16, 2),
            ("dgb", 128, BF16, 2), ("dgb2", 128, BF16, 2),
            ("o1", 128, F32, 2), ("on", 128, BF16, 2), ("osq", 128, BF16, 2),
            ("ktb", 64, BF16, 2), ("ktl", 64, BF16, 2),
            ("kq2", 128, BF16, 2), ("col", 1, F32, 6), ("sg", NT, BF16, 2),
            ("cvt", NT, F32, 2),
        ]:
            self.tb[name] = [E(nc.sbuf_tensor(f"t_{name}{i}", [128, w], dt))
                             for i in range(nb)]
        self.tb["rw"] = [E(nc.sbuf_tensor(f"t_rw{i}", [1, 256], F32))
                         for i in range(2)]
        self.CONSTS = [1e-5, 1e-6, 1e-24, 1.0, -1.0, -0.5,
                       EPS_RMS, 1.0 / DV, float(np.log(2.0)), 0.5, 2.0]
        self.cst = E(nc.sbuf_tensor("cst", [128, len(self.CONSTS)], F32))
        for i, v in enumerate(self.CONSTS):
            nc.const_aps.register(F32, v, self.cst[:, i:i + 1])
        self.banks_f32 = [[E(nc.psum_tensor(f"pf{i}", [128, 512], F32)),
                           None, 0] for i in range(5)]
        self.solve_bank = [E(nc.psum_tensor("psolve", [128, 512], F32)),
                           None, 0]
        self.banks_b16 = [[E(nc.psum_tensor(f"pb{i}", [128, 512], BF16)),
                           None, 0] for i in range(2)]
        for n in ("dS", "dA", "dG", "pe", "ve", "ac", "po"):
            self.sems[n] = Sem(E(nc.semaphore(n)), n)
        self.tap_t = None
        if self.tap_point is not None:
            self.tap_t = nc.dram_tensor("tap", [128, 8192], F32,
                                        kind="ExternalOutput")

    def weight_order_tile(self, sublayers):
        order = []
        if 1 in sublayers:
            for w in ("wq", "wk", "wv"):
                order += [(w, i) for i in range(16)]
        if 2 in sublayers:
            for i in range(64):
                order += [("wg", i), ("wu", i)]
            order += [("wd", i) for i in range(64)]
        if 3 in sublayers:
            order += [("wqk", i) for i in range(16)]
            order += [("wvg", i) for i in range(32)]
            order += [("wo", i) for i in range(16)]
        if 4 in sublayers:
            for i in range(64):
                order += [("wg", i), ("wu", i)]
            order += [("wd", i) for i in range(64)]
        return order

    def stream_init(self):
        subl = list(range(1, self.n_sublayers + 1))
        per_tile = self.weight_order_tile(subl)
        self.g_order = []
        for t in range(self.n_tiles):
            for (w, gi) in per_tile:
                self.g_order.append((t, w, gi))
        self.iptr = 0
        self.cptr = 0
        self.s_slot = 0
        self.s_rel = [None] * len(self.wbuf)
        self.comp = {}
        self.mirrored = set()

    def w_src(self, w, gi, first):
        base = self.w_in[w] if first else self.w_scr[w]
        if w == "wd":
            m, q = gi // 4, gi % 4
            return base[m, :, q * 2048:(q + 1) * 2048]
        return base[gi, :, :]

    def issue_upto(self, target):
        while self.iptr <= target and self.iptr < len(self.g_order):
            idx = self.iptr
            t, w, gi = self.g_order[idx]
            first = (t == 0)
            slot = self.s_slot
            self.s_slot = (slot + 1) % len(self.wbuf)
            rel = self.s_rel[slot]
            buf = self.wbuf[slot]
            q = "gpsimd" if first else "sync"
            sem = "dG" if first else "dS"
            if rel is not None:
                self.wait(q, rel[0], rel[1])
                if len(rel) == 4:
                    self.wait(q, rel[2], rel[3])
            src = self.w_src(w, gi, first)
            self.emit(q, lambda c, d_=buf, s_=src: c.dma_start(d_[:, :], s_))
            n = self.inc(q, sem, 16)
            mkey = (w, gi)
            do_mirror = first and mkey not in self.mirrored
            if do_mirror:
                self.mirrored.add(mkey)
                self.wait("scalar", sem, n)
                dst = self.w_src(w, gi, False)
                self.emit("scalar", lambda c, d_=dst, b_=buf:
                          c.dma_start(d_, b_[:, :]))
                self.inc("scalar", "dA", 16)
            self.comp[idx] = (sem, n, slot,
                              ("dA", self.n_of("dA")) if do_mirror else None)
            self.iptr += 1

    def next_group(self):
        idx = self.cptr
        self.cptr += 1
        self.issue_upto(idx + self.PREFETCH)
        sem, n, slot, mir = self.comp[idx]
        self.wait("tensor", sem, n)
        return self.wbuf[slot], slot, mir

    def group_done(self, slot, mir):
        rel = ("pe", self.n_of("pe"))
        if mir is not None:
            rel = rel + mir
        self.s_rel[slot] = rel

    def reset_pass(self, engine, cur):
        self.eng = engine
        self.cur = cur
        self._last = {}
        self._wcache = {}
        self._last_inc = {}
        self.tb_rel = {}
        for s in self.sems.values():
            s.n = 0
        for b in self.banks_f32 + self.banks_b16 + [self.solve_bank]:
            b[1] = None
            b[2] = 0
        self.bi32 = 0
        self.bi16 = 0
        self.stream_init()
        self.chunk_ctr = 0
        self.head_ctr = 0
        self.st_ready_a = [("po", 0)] * HA
        self.st_ready_b = [("po", 0)] * HB
        self.xver = ("po", 0)

    def tbw(self, name, hp, engine):
        rel = self.tb_rel.get((name, hp))
        if rel is not None:
            self.wait(engine, rel[0], rel[1])
        return self.tb[name][hp]

    def tbr(self, name, hp, sem):
        self.tb_rel[(name, hp)] = (sem, self.n_of(sem))

    def barrier(self):
        for e in ENGS:
            for s in ("pe", "ve", "ac", "po", "dS", "dA", "dG"):
                self.wait(e, s)

    def maybe_tap(self, name, ap):
        if self.tap_point != name or getattr(self, "_tapped", None) == name:
            return
        self._tapped = name
        for s in ("pe", "ve", "ac", "po"):
            self.wait("gpsimd", s)
        P, W = ap.shape
        self.emit("gpsimd", lambda c: c.dma_start(
            self.tap_t[0:P, 0:W], ap, allow_slow_non_contiguous=True))
        self.inc("gpsimd", "dG", 16)
        for e in ("gpsimd", "vector", "scalar", "tensor", "sync"):
            self.wait(e, "dG")

    def program(self):
        self._tapped = None
        self.preamble()
        for t in range(self.n_tiles):
            self.tile(t)
        self.wait("scalar", "dA")
        self.wait("sync", "dS")

    def preamble(self):
        gp = lambda fn: self.emit("gpsimd", fn)
        for tn, cmp_, fill, cm, stp in [
            (self.ident16, ALU.not_equal, 1.0, 1, -1),
            (self.ident32, ALU.not_equal, 1.0, 1, -1),
            (self.tril32, ALU.is_gt, 1.0, 1, -1),
            (self.slow32, ALU.is_ge, 1.0, -1, 1),
            (self.addUC, ALU.is_ge, NEG, -1, 1),
            (self.addSL, ALU.is_gt, NEG, 1, -1),
        ]:
            gp(lambda c, t_=tn: c.memset(t_[:, :], 0.0))
            gp(lambda c, t_=tn, c_=cmp_, f=fill, m=cm, s=stp:
               c.affine_select(
                   out=t_[:, :], in_=t_[:, :], compare_op=c_, fill=f,
                   base=0, pattern=[[s, 128]], channel_multiplier=m))
        gp(lambda c: c.memset(self.ones_col16[:, :], 1.0))
        gp(lambda c: c.memset(self.ones_row32[:, :], 1.0))
        gp(lambda c: c.memset(self.stA[:, :], 0.0))
        gp(lambda c: c.memset(self.stA16[:, :], 0.0))
        gp(lambda c: c.memset(self.stB[:, :], 0.0))
        gp(lambda c: c.memset(self.stB16[:, :], 0.0))
        for i, v in enumerate(self.CONSTS):
            gp(lambda c, i_=i, v_=v: c.memset(self.cst[:, i_:i_ + 1], v_))
        self.inc("gpsimd", "po")
        self.emit("vector", lambda c: c.memset(self.tails[:, :], 0.0))
        self.inc("vector", "ve")
        self.po_consts = self.n_of("po")
        for n in ("wal", "wbe", "wa", "wb"):
            self.emit("gpsimd", lambda c, n_=n: c.dma_start(
                self.wsm16[n_][:, :], self.wsm_in[n_][:, :]))
            self.inc("gpsimd", "dG", 16)
        self.dg_small = self.n_of("dG")
        for n in ("bq", "bk", "bv"):
            self.emit("sync", lambda c, n_=n: c.dma_start(
                self.bcol[n_][:, :], self.bias_in[n_][:, :]))
            self.inc("sync", "dS", 16)
        for n in ("bal", "bbe", "bbb", "bbt", "bdt"):
            self.emit("sync", lambda c, n_=n: c.dma_start(
                self.bsm[n_][:, :], self.bias_in[n_][:, :]))
            self.inc("sync", "dS", 16)
        for n in ("cq", "ck", "cv"):
            self.emit("sync", lambda c, n_=n: c.dma_start(
                self.cvw[n_][:, :], self.cv_in[n_][:, :]))
            self.inc("sync", "dS", 16)
        self.emit("sync", lambda c: c.dma_start(self.alog_sb[:, :],
                                                self.alog_in[:, :]))
        self.inc("sync", "dS", 16)
        self.ds_small = self.n_of("dS")
        self.wait("scalar", "dS", self.ds_small)
        self.wait("scalar", "po", self.po_consts)
        self.emit("scalar", lambda c: c.activation(
            self.negexpA[:, :], self.alog_sb[:, :], AF.Exp))
        na = self.inc("scalar", "ac")
        self.wait("vector", "ac", na)
        self.emit("vector", lambda c: c.tensor_scalar(
            self.negexpA[:, :], self.negexpA[:, :], -1.0, None, ALU.mult))
        self.wait("vector", "dS", self.ds_small)
        self.emit("vector", lambda c: c.tensor_tensor(
            self.bsm["bb2"][:, :], self.bsm["bbb"][:, :],
            self.bsm["bbt"][:, :], ALU.add))
        self.negA_ready = self.inc("vector", "ve")
        self.emit("sync", lambda c: c.dma_start(
            self.xf[:, :].rearrange("p (k t) -> p k t", t=NT),
            self.x_in[0, :, :, 0:NT].rearrange("k p t -> p k t")))
        self.inc("sync", "dS", 16)
        self.xver = ("dS", self.n_of("dS"))

    def ln(self):
        xsem, xn = self.xver
        pb = self.bank()
        pbq = self.bank()
        pe_kc = []
        for kc in range(KC):
            self.wait("gpsimd", xsem, xn)
            self.emit("gpsimd", lambda c, k=kc: c.tensor_copy(
                self.xb[:, k * NT:(k + 1) * NT],
                self.xf[:, k * NT:(k + 1) * NT]))
            npo = self.inc("gpsimd", "po")
            self.wait("scalar", xsem, xn)
            if kc >= 2:
                self.wait("scalar", "pe", pe_kc[kc - 2])
            self.emit("scalar", lambda c, k=kc: c.activation(
                self.xsq[k % 2], self.xf[:, k * NT:(k + 1) * NT],
                AF.Square))
            nac = self.inc("scalar", "ac")
            self.wait("tensor", "po", npo)
            self.wait("tensor", "ac", nac)
            self.emit("tensor", lambda c, k=kc: c.matmul(
                pb[0][0:1, 0:NT], self.ones_col16[:, :],
                self.xb[:, k * NT:(k + 1) * NT],
                start=(k == 0), stop=False, skip_group_check=True))
            self.emit("tensor", lambda c, k=kc: c.matmul(
                pbq[0][0:1, 0:NT], self.ones_col16[:, :],
                self.xsq[kc % 2], start=(k == 0),
                stop=(k == KC - 1), skip_group_check=True))
            pe_kc.append(self.inc("tensor", "pe"))
        pe_stats = self.n_of("pe")
        R = self.rows
        self.wait("vector", "pe", pe_stats)
        self.emit("vector", lambda c: c.tensor_scalar(
            R[:, 0:512], pb[0][0:1, 0:NT], 1.0 / D, None, ALU.mult))
        self.emit("vector", lambda c: c.tensor_scalar(
            R[:, 1024:1536], pbq[0][0:1, 0:NT], 1.0 / D, None,
            ALU.mult))
        self.emit("vector", lambda c: c.tensor_tensor(
            R[:, 512:1024], R[:, 0:512], R[:, 0:512], ALU.mult))
        self.emit("vector", lambda c: c.tensor_tensor(
            R[:, 1024:1536], R[:, 1024:1536], R[:, 512:1024],
            ALU.subtract))
        nve = self.inc("vector", "ve")
        self.bank_release(pb, "ve")
        self.bank_release(pbq, "ve")
        self.wait("scalar", "ve", nve)
        self.emit("scalar", lambda c: c.activation(
            R[:, 512:1024], R[:, 1024:1536], AF.Ln, bias=1e-5))
        self.emit("scalar", lambda c: c.activation(
            R[:, 512:1024], R[:, 512:1024], AF.Exp, scale=-0.5))
        r_ready = self.inc("scalar", "ac")
        pb2 = self.bank()
        pb3 = self.bank()
        self.wait("tensor", "ac", r_ready)
        self.emit("tensor", lambda c: c.matmul(
            pb2[0][:, 0:NT], self.ones_row32[0:1, :], R[0:1, 0:512],
            start=True, stop=True))
        self.emit("tensor", lambda c: c.matmul(
            pb3[0][:, 0:NT], self.ones_row32[0:1, :], R[0:1, 512:1024],
            start=True, stop=True))
        nrb = self.inc("tensor", "pe")
        self.wait("vector", "pe", nrb)
        self.wait("vector", "po")
        self.emit("vector", lambda c: c.tensor_copy(
            self.m_b[:, :], pb2[0][:, 0:NT]))
        self.emit("vector", lambda c: c.tensor_copy(
            self.r_b[:, :], pb3[0][:, 0:NT]))
        rb_ready = self.inc("vector", "ve")
        self.bank_release(pb2, "ve")
        self.bank_release(pb3, "ve")
        for kc in range(KC):
            engn = "vector" if kc % 2 == 0 else "gpsimd"
            self.wait(engn, "pe", pe_stats)
            if engn == "gpsimd":
                self.wait(engn, "ve", rb_ready)
            self.emit(engn, lambda c, k=kc: c.tensor_tensor(
                self.xb[:, k * NT:(k + 1) * NT],
                self.xf[:, k * NT:(k + 1) * NT], self.m_b[:, :],
                ALU.subtract))
            self.emit(engn, lambda c, k=kc: c.tensor_tensor(
                self.xb[:, k * NT:(k + 1) * NT],
                self.xb[:, k * NT:(k + 1) * NT], self.r_b[:, :], ALU.mult))
            self.inc(engn, "ve" if engn == "vector" else "po")
        self.xr_ve = self.n_of("ve")
        self.xr_po = self.n_of("po")
        self.maybe_tap("ln_xb", self.xb[:, :])
        self.maybe_tap("ln_rows", self.rows[0:1, :])

    def proj(self, n_m, wname, evac):
        for m in range(n_m):
            buf, slot, mir = self.next_group()
            pbk = self.bank()
            self.wait("tensor", "ve", self.xr_ve)
            self.wait("tensor", "po", self.xr_po)
            for k in range(KC):
                self.emit("tensor", lambda c, k_=k: c.matmul(
                    pbk[0][:, 0:NT], buf[:, k_ * 128:(k_ + 1) * 128],
                    self.xb[:, k_ * NT:(k_ + 1) * NT], start=(k_ == 0),
                    stop=(k_ == KC - 1), skip_group_check=True))
            self.inc("tensor", "pe")
            self.group_done(slot, mir)
            evac(m, pbk)

    def proj_small(self, wname, out_t, bias_name):
        pbk = self.bank()
        self.wait("tensor", "ve", self.xr_ve)
        self.wait("tensor", "po", self.xr_po)
        self.wait("tensor", "dG", self.dg_small)
        for k in range(KC):
            self.emit("tensor", lambda c, k_=k: c.matmul(
                pbk[0][0:16, 0:NT],
                self.wsm16[wname][:, k_ * 16:(k_ + 1) * 16],
                self.xb[:, k_ * NT:(k_ + 1) * NT],
                start=(k_ == 0), stop=(k_ == KC - 1),
                skip_group_check=True))
        self.inc("tensor", "pe")
        npe = self.n_of("pe")
        self.wait("vector", "pe", npe)
        self.wait("vector", "dS", self.ds_small)
        self.emit("vector", lambda c: c.tensor_scalar(
            out_t[:, :], pbk[0][0:16, 0:NT], self.bsm[bias_name][:, 0:1],
            None, ALU.add))
        self.inc("vector", "ve")
        self.bank_release(pbk, "ve")
        return self.n_of("ve")

    def tile(self, t):
        seq = t // 4
        tt = t % 4
        tok0 = tt * NT
        if tt == 0 and t > 0:
            npe = self.n_of("pe")
            self.wait("gpsimd", "pe", npe)
            self.emit("gpsimd", lambda c: c.memset(self.stA[:, :], 0.0))
            self.emit("gpsimd", lambda c: c.memset(self.stA16[:, :], 0.0))
            self.emit("gpsimd", lambda c: c.memset(self.stB[:, :], 0.0))
            self.emit("gpsimd", lambda c: c.memset(self.stB16[:, :], 0.0))
            npo = self.inc("gpsimd", "po")
            self.wait("vector", "pe", npe)
            self.emit("vector", lambda c: c.memset(self.tails[:, :], 0.0))
            self.inc("vector", "ve")
            self.st_ready_a = [("po", npo)] * HA
            self.st_ready_b = [("po", npo)] * HB
        elif t == 0:
            self.st_ready_a = [("po", self.po_consts)] * HA
            self.st_ready_b = [("po", self.po_consts)] * HB

        if self.n_sublayers >= 1:
            self.ln()
            self.attn_a(t)
        self.maybe_tap("x_after1", self.xf[:, :])
        if self.n_sublayers >= 2:
            self.ln()
            self.ffn()
        self.maybe_tap("x_after2", self.xf[:, :])
        if self.n_sublayers >= 3:
            self.ln()
            self.attn_b(t)
        if self.n_sublayers >= 4:
            self.ln()
            self.ffn()

        xsem, xn = self.xver
        self.wait("scalar", xsem, xn)
        self.emit("scalar", lambda c, s_=seq, t0=tok0: c.dma_start(
            self.x_out[s_, :, :, t0:t0 + NT].rearrange("k p t -> p k t"),
            self.xf[:, :].rearrange("p (k t) -> p k t", t=NT)))
        nst = self.inc("scalar", "dA", 16)
        if t + 1 < self.n_tiles:
            s2, t2 = (t + 1) // 4, (t + 1) % 4
            self.wait("sync", "dA", nst)
            self.emit("sync", lambda c, s_=s2, t0=t2 * NT: c.dma_start(
                self.xf[:, :].rearrange("p (k t) -> p k t", t=NT),
                self.x_in[s_, :, :, t0:t0 + NT].rearrange("k p t -> p k t")))
            self.inc("sync", "dS", 16)
            self.xver = ("dS", self.n_of("dS"))

    def attn_a(self, t):
        def mk_evac(base, bname):
            def evac(m, pbk):
                npe = self.n_of("pe")
                self.wait("vector", "pe", npe)
                self.wait("vector", "dS", self.ds_small)
                self.emit("vector", lambda c, m_=m: c.tensor_scalar(
                    self.ar(base + m_ * NT, NT), pbk[0][:, 0:NT],
                    self.bcol[bname][:, m_:m_ + 1], None, ALU.add))
                self.inc("vector", "ve")
                self.bank_release(pbk, "ve")
            return evac
        self.proj(16, "wq", mk_evac(A_QF, "bq"))
        self.proj(16, "wk", mk_evac(A_KF, "bk"))
        self.proj(16, "wv", mk_evac(A_VF, "bv"))
        qkv_ve = self.n_of("ve")
        self.maybe_tap("qf", self.ar(A_QF, 8192))
        self.maybe_tap("kf", self.ar(A_KF, 8192))
        self.maybe_tap("vf", self.ar(A_VF, 8192))
        za_ve = self.proj_small("wal", self.za, "bal")
        zb_ve = self.proj_small("wbe", self.zb, "bbe")
        self.wait("scalar", "ve", za_ve)
        self.emit("scalar", lambda c: c.activation(
            self.zt1[:, :], self.za[:, :], AF.Exp, scale=-1.0))
        n1 = self.inc("scalar", "ac")
        self.wait("vector", "ac", n1)
        self.emit("vector", lambda c: c.tensor_scalar(
            self.zt1[:, :], self.zt1[:, :], 1.0, None, ALU.add))
        self.emit("vector", lambda c: c.reciprocal(self.zt1[:, :],
                                                   self.zt1[:, :]))
        n2 = self.inc("vector", "ve")
        self.wait("scalar", "ve", n2)
        self.emit("scalar", lambda c: c.activation(
            self.loga[:, :], self.zt1[:, :], AF.Ln, bias=1e-6))
        self.emit("scalar", lambda c: c.activation(
            self.zt1[:, :], self.zb[:, :], AF.Abs))
        self.emit("scalar", lambda c: c.activation(
            self.zt1[:, :], self.zt1[:, :], AF.Exp, scale=-1.0))
        self.emit("scalar", lambda c: c.activation(
            self.zt1[:, :], self.zt1[:, :], AF.Ln, bias=1.0))
        n4 = self.inc("scalar", "ac")
        self.wait("vector", "ac", n4)
        self.wait("vector", "ve", zb_ve)
        self.emit("vector", lambda c: c.tensor_scalar(
            self.zt2[:, :], self.zb[:, :], 0.0, None, ALU.max))
        self.emit("vector", lambda c: c.tensor_tensor(
            self.zt2[:, :], self.zt2[:, :], self.zt1[:, :], ALU.add))
        n5 = self.inc("vector", "ve")
        self.wait("scalar", "ve", n5)
        self.emit("scalar", lambda c: c.activation(
            self.logb[:, :], self.zt2[:, :], AF.Ln))
        ab_ready = self.inc("scalar", "ac")
        self.maybe_tap("loga", self.loga[:, :])
        self.maybe_tap("logb", self.logb[:, :])
        for cch in range(NCH):
            self.attn_a_chunk(t, cch, qkv_ve, ab_ready)
            self.maybe_tap(f"stA_c{cch}", self.stA[:, :])
        self.maybe_tap("stA", self.stA[:, :])

    def attn_a_chunk(self, t, cch, qkv_ve, ab_ready):
        tb = self.tb
        p = self.chunk_ctr % 2
        self.chunk_ctr += 1
        coff = cch * C
        csl = lambda n: self.colt[n][:, p * 16:(p + 1) * 16]
        pbt = self.bank()
        self.wait("tensor", "ac", ab_ready)
        self.wait("tensor", "po", self.po_consts)
        self.emit("tensor", lambda c: c.transpose(
            pbt[0][0:128, 0:16], self.loga[:, coff:coff + C],
            self.ident32[0:16, 0:16]))
        self.emit("tensor", lambda c: c.transpose(
            pbt[0][0:128, 16:32], self.logb[:, coff:coff + C],
            self.ident32[0:16, 0:16]))
        ntr = self.inc("tensor", "pe")
        self.wait("vector", "pe", ntr)
        self.emit("vector", lambda c: c.tensor_copy(csl("la"),
                                                    pbt[0][:, 0:16]))
        self.emit("vector", lambda c: c.tensor_copy(csl("lb"),
                                                    pbt[0][:, 16:32]))
        nla = self.inc("vector", "ve")
        self.bank_release(pbt, "ve")
        pbc = self.bank()
        self.wait("tensor", "ve", nla)
        self.emit("tensor", lambda c: c.matmul(
            pbc[0][0:128, 0:16], self.tril32[:, :], csl("la"),
            start=True, stop=False, skip_group_check=True))
        self.emit("tensor", lambda c: c.matmul(
            pbc[0][0:128, 16:32], self.slow32[:, :], csl("la"),
            start=False, stop=True, skip_group_check=True))
        ncl = self.inc("tensor", "pe")
        self.wait("vector", "pe", ncl)
        self.emit("vector", lambda c: c.tensor_copy(csl("cla"),
                                                    pbc[0][:, 0:16]))
        self.emit("vector", lambda c: c.tensor_copy(csl("dtx"),
                                                    pbc[0][:, 16:32]))
        self.selfsync("vector")
        self.emit("vector", lambda c: c.tensor_tensor(
            csl("etd"), csl("cla"), csl("dtx"), ALU.add))
        self.emit("vector", lambda c: c.tensor_tensor(
            csl("bias"), csl("lb"), csl("cla"), ALU.subtract))
        ncl2 = self.inc("vector", "ve")
        self.bank_release(pbc, "ve")
        self.wait("scalar", "ve", ncl2)
        self.emit("scalar", lambda c: c.activation(
            csl("ecd"), csl("cla"), AF.Exp))
        self.emit("scalar", lambda c: c.activation(
            csl("edte"), csl("dtx"), AF.Exp))
        self.emit("scalar", lambda c: c.activation(
            csl("etd"), csl("etd"), AF.Exp))
        self.emit("scalar", lambda c: c.activation(
            csl("ebt"), csl("lb"), AF.Exp))
        self.selfsync("scalar")
        self.emit("scalar", lambda c: c.activation(
            csl("eb2"), csl("edte"), AF.Copy))
        prep_ac0 = self.inc("scalar", "ac")
        self.wait("vector", "ac", prep_ac0)
        self.emit("vector", lambda c: c.tensor_tensor(
            csl("eb2"), csl("eb2"), csl("ebt"), ALU.mult))
        eb2_ve = self.inc("vector", "ve")
        prep_ac = prep_ac0
        for h in range(HA):
            self.attn_a_head(t, cch, h, p, qkv_ve, nla, prep_ac, eb2_ve)

    def attn_a_head(self, t, cch, h, p, qkv_ve, nla, prep_ac, eb2_ve):
        tb = self.tb
        self.head_ctr += 1
        hp = self.head_ctr % 2
        coff = cch * C
        one = lambda n: self.colt[n][:, p * 16 + h:p * 16 + h + 1]
        qsl = self.ar(A_QF + h * NT + coff, C)
        ksl = self.ar(A_KF + h * NT + coff, C)
        vsl = self.ar(A_VF + h * NT + coff, C)
        pb1 = self.bank(b16=True)
        self.wait("tensor", "ve", qkv_ve)
        self.emit("tensor", lambda c: c.transpose(
            pb1[0][:, 0:128], ksl, self.ident16[:, :]))
        self.emit("tensor", lambda c: c.transpose(
            pb1[0][:, 128:256], vsl, self.ident16[:, :]))
        ntr = self.inc("tensor", "pe")
        self.wait("vector", "pe", ntr)
        self.tbw("ktok", hp, "vector")
        self.tbw("vtok", hp, "vector")
        self.emit("vector", lambda c: c.tensor_copy(
            tb["ktok"][hp][:, :], pb1[0][:, 0:128]))
        self.emit("vector", lambda c: c.tensor_copy(
            tb["vtok"][hp][:, :], pb1[0][:, 128:256]))
        ktv = self.inc("vector", "ve")
        self.bank_release(pb1, "ve")
        self.wait("scalar", "ve", ktv)
        self.emit("scalar", lambda c: c.activation(
            tb["osq"][hp][:, :], tb["ktok"][hp][:, :], AF.Square,
            accum_out=tb["col"][0][:, 0:1]))
        self.selfsync("scalar")
        self.emit("scalar", lambda c: c.activation(
            tb["col"][1][:, 0:1], tb["col"][0][:, 0:1], AF.Ln, bias=1e-24))
        self.selfsync("scalar")
        self.emit("scalar", lambda c: c.activation(
            tb["col"][2][:, 0:1], tb["col"][1][:, 0:1], AF.Exp, scale=-0.5))
        nk1 = self.inc("scalar", "ac")
        self.wait("vector", "ac", max(nk1, prep_ac))
        self.emit("vector", lambda c: c.scalar_tensor_tensor(
            tb["col"][3][:, 0:1], tb["col"][1][:, 0:1], -0.5,
            one("bias"), ALU.mult, ALU.add))
        nbias = self.inc("vector", "ve")
        pbr = self.bank()
        self.wait("tensor", "ve", nla)
        self.emit("tensor", lambda c: c.matmul(
            pbr[0][0:1, 0:128], self.colt["la"][:, p * 16 + h:p * 16 + h + 1],
            self.tril32[:, :], start=True, stop=True))
        nrow = self.inc("tensor", "pe")
        self.wait("vector", "pe", nrow)
        self.tbw("rw", hp, "vector")
        self.emit("vector", lambda c: c.tensor_copy(
            tb["rw"][hp][0:1, 0:128], pbr[0][0:1, 0:128]))
        nrow2 = self.inc("vector", "ve")
        self.bank_release(pbr, "ve")
        pbx = self.bank()
        self.wait("tensor", "ve", nrow2)
        self.emit("tensor", lambda c: c.matmul(
            pbx[0][:, 0:128], self.ones_row32[0:1, :],
            tb["rw"][hp][0:1, 0:128], start=True, stop=True))
        nX = self.inc("tensor", "pe")
        self.tbr("rw", hp, "pe")
        self.wait("vector", "pe", nX)
        self.tbw("X2", hp, "vector")
        self.emit("vector", lambda c: c.tensor_tensor(
            tb["X2"][hp][:, :], pbx[0][:, 0:128], self.addUC[:, :], ALU.add))
        nX2 = self.inc("vector", "ve")
        self.bank_release(pbx, "ve")
        self.wait("scalar", "ve", max(nX2, nbias))
        self.tbw("DT", hp, "scalar")
        self.emit("scalar", lambda c: c.activation(
            tb["DT"][hp][:, :], tb["X2"][hp][:, :], AF.Exp,
            bias=tb["col"][3][:, 0:1]))
        nDT = self.inc("scalar", "ac")
        self.tbr("X2", hp, "ac")
        pba = self.bank()
        self.wait("tensor", "ve", qkv_ve)
        self.emit("tensor", lambda c: c.matmul(
            pba[0][:, 0:128], ksl, qsl, start=True, stop=True))
        nat = self.inc("tensor", "pe")
        self.wait("vector", "pe", nat)
        self.wait("vector", "ac", nDT)
        self.tbw("attns", hp, "vector")
        self.emit("vector", lambda c: c.tensor_tensor(
            tb["attns"][hp][:, :], pba[0][:, 0:128], tb["DT"][hp][:, :],
            ALU.mult))
        nats = self.inc("vector", "ve")
        self.tbr("DT", hp, "ve")
        self.bank_release(pba, "ve")
        pyg = self.bank()
        sra = self.st_ready_a[h]
        self.wait("tensor", sra[0], sra[1])
        self.emit("tensor", lambda c: c.matmul(
            pyg[0][:, 0:128], qsl, self.stA16[:, h * 128:(h + 1) * 128],
            start=True, stop=True))
        nyg = self.inc("tensor", "pe")
        self.wait("scalar", "pe", nyg)
        self.wait("scalar", "ac", prep_ac)
        self.tbw("yg", hp, "scalar")
        self.emit("scalar", lambda c: c.activation(
            tb["yg"][hp][:, :], pyg[0][:, 0:128], AF.Copy,
            scale=one("ecd")))
        nyg2 = self.inc("scalar", "ac")
        self.bank_release(pyg, "ac")
        pyl = self.bank()
        self.wait("tensor", "ve", nats)
        self.emit("tensor", lambda c: c.matmul(
            pyl[0][:, 0:128], tb["attns"][hp][:, :], tb["vtok"][hp][:, :],
            start=True, stop=True))
        nyl = self.inc("tensor", "pe")
        self.tbr("attns", hp, "pe")
        self.wait("vector", "pe", nyl)
        self.wait("vector", "ac", nyg2)
        self.tbw("ytok", hp, "vector")
        self.emit("vector", lambda c: c.tensor_tensor(
            tb["ytok"][hp][:, :], tb["yg"][hp][:, :], pyl[0][:, 0:128],
            ALU.add))
        nyt = self.inc("vector", "ve")
        self.tbr("yg", hp, "ve")
        self.bank_release(pyl, "ve")
        pyt = self.bank(b16=True)
        self.wait("tensor", "ve", nyt)
        self.emit("tensor", lambda c: c.transpose(
            pyt[0][:, 0:128], tb["ytok"][hp][:, :], self.ident16[:, :]))
        nyT = self.inc("tensor", "pe")
        self.tbr("ytok", hp, "pe")
        xslice = self.xf[:, h * NT + coff:h * NT + coff + C]
        self.wait("vector", "pe", nyT)
        self.emit("vector", lambda c: c.tensor_tensor(
            xslice, xslice, pyt[0][:, 0:128], ALU.add))
        nadd = self.inc("vector", "ve")
        self.bank_release(pyt, "ve")
        self.wait("vector", "ac", max(nk1, prep_ac))
        self.wait("vector", "ve", eb2_ve)
        self.tbw("kw", hp, "vector")
        self.emit("vector", lambda c: c.tensor_scalar(
            tb["kw"][hp][:, :], tb["ktok"][hp][:, :], tb["col"][2][:, 0:1],
            one("eb2"), ALU.mult, ALU.mult))
        nkw = self.inc("vector", "ve")
        self.tbr("ktok", hp, "ve")
        if cch == 0 and h == 0:
            self.maybe_tap("edte_t", self.colt["edte"][:, :])
            self.maybe_tap("ebt_t", self.colt["ebt"][:, :])
            self.maybe_tap("cla_t", self.colt["cla"][:, :])
            self.maybe_tap("dtx_t", self.colt["dtx"][:, :])
            self.maybe_tap("ktok0", tb["ktok"][hp][:, :])
            self.maybe_tap("vtok0", tb["vtok"][hp][:, :])
            self.maybe_tap("kw0", tb["kw"][hp][:, :])
            self.maybe_tap("col5", tb["col"][5][:, 0:1])
            self.maybe_tap("col2", tb["col"][2][:, 0:1])
        pdel = self.bank()
        self.wait("tensor", "ve", nkw)
        self.emit("tensor", lambda c: c.matmul(
            pdel[0][:, 0:128], tb["kw"][hp][:, :], tb["vtok"][hp][:, :],
            start=True, stop=True))
        ndel = self.inc("tensor", "pe")
        self.tbr("kw", hp, "pe")
        self.tbr("vtok", hp, "pe")
        stsl = self.stA[:, h * 128:(h + 1) * 128]
        st16 = self.stA16[:, h * 128:(h + 1) * 128]
        self.wait("vector", "pe", ndel)
        self.wait("vector", "ac", prep_ac)
        self.emit("vector", lambda c: c.tensor_scalar(
            stsl, stsl, one("etd"), None, ALU.mult))
        self.emit("vector", lambda c: c.tensor_tensor(
            stsl, stsl, pdel[0][:, 0:128], ALU.add))
        nst = self.inc("vector", "ve")
        self.bank_release(pdel, "ve")
        self.wait("scalar", "ve", nst)
        self.emit("scalar", lambda c: c.copy(st16, stsl))
        ncast = self.inc("scalar", "ac")
        self.st_ready_a[h] = ("ac", ncast)
        if cch == NCH - 1 and h == HA - 1:
            self.xver = ("ve", nadd)

    def ffn(self):
        tb = self.tb
        sg_used = [0, 0]
        H_ve = []
        for hc in range(64):
            bufg, slg, mirg = self.next_group()
            pg = self.bank()
            self.wait("tensor", "ve", self.xr_ve)
            self.wait("tensor", "po", self.xr_po)
            for k in range(KC):
                self.emit("tensor", lambda c, k_=k: c.matmul(
                    pg[0][:, 0:NT], bufg[:, k_ * 128:(k_ + 1) * 128],
                    self.xb[:, k_ * NT:(k_ + 1) * NT], start=(k_ == 0),
                    stop=(k_ == KC - 1), skip_group_check=True))
            ng = self.inc("tensor", "pe")
            self.group_done(slg, mirg)
            bufu, slu, miru = self.next_group()
            pu = self.bank()
            for k in range(KC):
                self.emit("tensor", lambda c, k_=k: c.matmul(
                    pu[0][:, 0:NT], bufu[:, k_ * 128:(k_ + 1) * 128],
                    self.xb[:, k_ * NT:(k_ + 1) * NT], start=(k_ == 0),
                    stop=(k_ == KC - 1), skip_group_check=True))
            nu = self.inc("tensor", "pe")
            self.group_done(slu, miru)
            sp = hc % 2
            self.wait("scalar", "pe", ng)
            if sg_used[sp]:
                self.wait("scalar", "ve", sg_used[sp])
            self.emit("scalar", lambda c, s_=sp: c.activation(
                tb["sg"][s_][:, :], pg[0][:, 0:NT], AF.Silu))
            nsg = self.inc("scalar", "ac")
            self.bank_release(pg, "ac")
            self.wait("vector", "pe", nu)
            self.wait("vector", "ac", nsg)
            self.emit("vector", lambda c, m_=hc, s_=sp: c.tensor_tensor(
                self.ar(A_H + m_ * NT, NT), tb["sg"][s_][:, :],
                pu[0][:, 0:NT], ALU.mult))
            nh = self.inc("vector", "ve")
            self.bank_release(pu, "ve")
            sg_used[sp] = nh
            H_ve.append(nh)
        last_add = None
        for m in range(KC):
            pd = self.bank()
            for quarter in range(4):
                buf, slot, mir = self.next_group()
                self.wait("tensor", "ve", H_ve[quarter * 16 + 15])
                for j in range(16):
                    k = quarter * 16 + j
                    self.emit("tensor", lambda c, j_=j, k_=k: c.matmul(
                        pd[0][:, 0:NT], buf[:, j_ * 128:(j_ + 1) * 128],
                        self.ar(A_H + k_ * NT, NT), start=(k_ == 0),
                        stop=(k_ == 63), skip_group_check=True))
                self.inc("tensor", "pe")
                self.group_done(slot, mir)
            nd = self.n_of("pe")
            xsl = self.xf[:, m * NT:(m + 1) * NT]
            self.wait("vector", "pe", nd)
            self.emit("vector", lambda c, x_=xsl: c.tensor_tensor(
                x_, x_, pd[0][:, 0:NT], ALU.add))
            last_add = self.inc("vector", "ve")
            self.bank_release(pd, "ve")
        self.xver = ("ve", last_add)

    def attn_b(self, t):
        tb = self.tb
        stage_rel = [None, None, None]
        scr_rel = [0, 0]
        self.stg_i = 0

        def conv_one(m, pbk, which):
            cw, obase, ch, engn = which
            sl = self.stg_i % 3
            sc = self.stg_i % 2
            self.stg_i += 1
            npe = self.n_of("pe")
            self.wait("vector", "pe", npe)
            if stage_rel[sl] is not None:
                self.wait("vector", stage_rel[sl][0], stage_rel[sl][1])
            self.emit("vector", lambda c: c.tensor_copy(
                self.stage[sl][:, 3:515], pbk[0][:, 0:NT]))
            nev = self.inc("vector", "ve")
            self.bank_release(pbk, "ve")
            toff = {"cq": 0, "ck": 24, "cv": 48}[cw] + ch * 3
            self.wait("vector", "dS", self.ds_small)
            self.emit("vector", lambda c: c.tensor_copy(
                self.stage[sl][:, 0:3], self.tails[:, toff:toff + 3]))
            self.selfsync("vector")
            if scr_rel[sc]:
                self.wait("vector", "ac", scr_rel[sc])
            scr = tb["cvt"][sc]
            wc = lambda j: self.cvw[cw][:, ch * 4 + j:ch * 4 + j + 1]
            self.emit("vector", lambda c: c.tensor_scalar(
                scr[:, :], self.stage[sl][:, 0:512], wc(0), None, ALU.mult))
            for j in (1, 2, 3):
                self.emit("vector", lambda c, j_=j: c.scalar_tensor_tensor(
                    scr[:, :], self.stage[sl][:, j_:j_ + 512], wc(j_),
                    scr[:, :], ALU.mult, ALU.add))
            self.emit("vector", lambda c: c.tensor_copy(
                self.tails[:, toff:toff + 3], self.stage[sl][:, 512:515]))
            ncv = self.inc("vector", "ve")
            stage_rel[sl] = ("ve", ncv)
            self.wait("scalar", "ve", ncv)
            self.emit("scalar", lambda c, ob=obase, ch_=ch: c.activation(
                self.ar(ob + ch_ * 512, 512), scr[:, :], AF.Silu))
            scr_rel[sc] = self.inc("scalar", "ac")

        def evac_qk(m, pbk):
            if m < 8:
                conv_one(m, pbk, ("cq", A_QC, m, "vector"))
            else:
                conv_one(m, pbk, ("ck", A_KC, m - 8, "vector"))

        def evac_vg(m, pbk):
            if m < 16:
                conv_one(m, pbk, ("cv", A_VC, m, "vector"))
            else:
                npe = self.n_of("pe")
                self.wait("vector", "pe", npe)
                self.emit("vector", lambda c, m_=m: c.tensor_copy(
                    self.ar(A_GOUT + (m_ - 16) * NT, NT), pbk[0][:, 0:NT]))
                self.inc("vector", "ve")
                self.bank_release(pbk, "ve")

        self.proj(16, "wqk", evac_qk)
        self.proj(32, "wvg", evac_vg)
        conv_ac = self.n_of("ac")
        self.maybe_tap("qc", self.ar(A_QC, 4096))
        self.maybe_tap("kc", self.ar(A_KC, 4096))
        self.maybe_tap("vc", self.ar(A_VC, 8192))
        self.maybe_tap("gout", self.ar(A_GOUT, 8192))
        zb_ve = self.proj_small("wb", self.zb, "bb2")
        za_ve = self.proj_small("wa", self.za, "bdt")
        self.wait("scalar", "ve", za_ve)
        self.emit("scalar", lambda c: c.activation(
            self.zt1[:, :], self.za[:, :], AF.Abs))
        self.emit("scalar", lambda c: c.activation(
            self.zt1[:, :], self.zt1[:, :], AF.Exp, scale=-1.0))
        self.emit("scalar", lambda c: c.activation(
            self.zt1[:, :], self.zt1[:, :], AF.Ln, bias=1.0))
        n1 = self.inc("scalar", "ac")
        self.wait("vector", "ac", n1)
        self.wait("vector", "ve", self.negA_ready)
        self.emit("vector", lambda c: c.tensor_scalar(
            self.zt2[:, :], self.za[:, :], 0.0, None, ALU.max))
        self.emit("vector", lambda c: c.tensor_tensor(
            self.zt2[:, :], self.zt2[:, :], self.zt1[:, :], ALU.add))
        self.emit("vector", lambda c: c.tensor_scalar(
            self.gbuf[:, :], self.zt2[:, :], self.negexpA[0:16, 0:1], None,
            ALU.mult))
        g_ve = self.inc("vector", "ve")
        self.wait("scalar", "ve", g_ve)
        self.emit("scalar", lambda c: c.activation(
            self.zt1[:, :], self.zb[:, :], AF.Abs))
        self.emit("scalar", lambda c: c.activation(
            self.zt1[:, :], self.zt1[:, :], AF.Exp, scale=-1.0))
        self.emit("scalar", lambda c: c.activation(
            self.zt1[:, :], self.zt1[:, :], AF.Ln, bias=1.0))
        n2 = self.inc("scalar", "ac")
        self.wait("vector", "ac", n2)
        self.emit("vector", lambda c: c.tensor_scalar(
            self.zt2[:, :], self.zb[:, :], -1.0, 0.0, ALU.mult, ALU.max))
        self.emit("vector", lambda c: c.tensor_tensor(
            self.zt2[:, :], self.zt2[:, :], self.zt1[:, :], ALU.add))
        self.emit("vector", lambda c: c.tensor_scalar(
            self.logb[:, :], self.zt2[:, :], -1.0, float(np.log(2.0)),
            ALU.mult, ALU.add))
        logb_ve = self.inc("vector", "ve")
        self.maybe_tap("gbuf", self.gbuf[:, :])
        self.maybe_tap("logbB", self.logb[:, :])
        for cch in range(NCH):
            self.attn_b_chunk(t, cch, conv_ac, g_ve, logb_ve)
            self.maybe_tap(f"stB_c{cch}", self.stB[:, :])
        self.maybe_tap("of", self.ar(A_OF, 8192))
        self.attn_b_out()

    def attn_b_chunk(self, t, cch, conv_ac, g_ve, logb_ve):
        tb = self.tb
        p = self.chunk_ctr % 2
        self.chunk_ctr += 1
        coff = cch * C
        csl = lambda n: self.colt[n][:, p * 16:(p + 1) * 16]
        pbt = self.bank()
        self.wait("tensor", "ve", max(g_ve, logb_ve))
        self.wait("tensor", "po", self.po_consts)
        self.emit("tensor", lambda c: c.transpose(
            pbt[0][0:128, 0:16], self.gbuf[:, coff:coff + C],
            self.ident32[0:16, 0:16]))
        self.emit("tensor", lambda c: c.transpose(
            pbt[0][0:128, 16:32], self.logb[:, coff:coff + C],
            self.ident32[0:16, 0:16]))
        ntr = self.inc("tensor", "pe")
        self.wait("vector", "pe", ntr)
        self.emit("vector", lambda c: c.tensor_copy(csl("g"),
                                                    pbt[0][:, 0:16]))
        self.emit("vector", lambda c: c.tensor_copy(csl("lb"),
                                                    pbt[0][:, 16:32]))
        ng = self.inc("vector", "ve")
        self.bank_release(pbt, "ve")
        pbc = self.bank()
        self.wait("tensor", "ve", ng)
        self.emit("tensor", lambda c: c.matmul(
            pbc[0][0:128, 0:16], self.tril32[:, :], csl("g"),
            start=True, stop=False, skip_group_check=True))
        self.emit("tensor", lambda c: c.matmul(
            pbc[0][0:128, 16:32], self.slow32[:, :], csl("g"),
            start=False, stop=True, skip_group_check=True))
        ncl = self.inc("tensor", "pe")
        self.wait("vector", "pe", ncl)
        self.emit("vector", lambda c: c.tensor_copy(csl("G"),
                                                    pbc[0][:, 0:16]))
        self.emit("vector", lambda c: c.tensor_copy(csl("dG"),
                                                    pbc[0][:, 16:32]))
        self.selfsync("vector")
        self.emit("vector", lambda c: c.tensor_tensor(
            csl("etdB"), csl("G"), csl("dG"), ALU.add))
        self.emit("vector", lambda c: c.tensor_tensor(
            csl("Glb"), csl("G"), csl("lb"), ALU.add))
        nG = self.inc("vector", "ve")
        self.bank_release(pbc, "ve")
        self.wait("scalar", "ve", nG)
        self.emit("scalar", lambda c: c.activation(
            csl("eG"), csl("G"), AF.Exp))
        self.emit("scalar", lambda c: c.activation(
            csl("edG"), csl("dG"), AF.Exp))
        self.emit("scalar", lambda c: c.activation(
            csl("etdB"), csl("etdB"), AF.Exp))
        self.emit("scalar", lambda c: c.activation(
            csl("ebt"), csl("lb"), AF.Exp))
        prep_ac = self.inc("scalar", "ac")
        if self.serialize >= 1:
            self.barrier()
        self.wait("vector", "ac", prep_ac)
        self.emit("vector", lambda c: c.tensor_tensor(
            csl("nbG"), csl("ebt"), csl("eG"), ALU.mult))
        self.selfsync("vector")
        self.emit("vector", lambda c: c.tensor_scalar(
            csl("nbG"), csl("nbG"), -1.0, None, ALU.mult))
        nbG_ve = self.inc("vector", "ve")
        for h in range(HB):
            if self.serialize >= 2:
                self.barrier()
            self.attn_b_head(t, cch, h, p, conv_ac, prep_ac, nG, nbG_ve)

    def attn_b_head(self, t, cch, h, p, conv_ac, prep_ac, nG, nbG_ve):
        tb = self.tb
        self.head_ctr += 1
        hp = self.head_ctr % 2
        coff = cch * C
        one = lambda n: self.colt[n][:, p * 16 + h:p * 16 + h + 1]
        pbase = 64 * (h % 2)
        qsl = self.arena[pbase:pbase + 64,
                         A_QC + (h // 2) * NT + coff:
                         A_QC + (h // 2) * NT + coff + C]
        ksl = self.arena[pbase:pbase + 64,
                         A_KC + (h // 2) * NT + coff:
                         A_KC + (h // 2) * NT + coff + C]
        vsl = self.ar(A_VC + h * NT + coff, C)
        stB_sl = self.stB[pbase:pbase + 64,
                          (h // 2) * 128:(h // 2 + 1) * 128]
        stB16_sl = self.stB16[pbase:pbase + 64,
                              (h // 2) * 128:(h // 2 + 1) * 128]
        idsl = self.ident16[pbase:pbase + 64, pbase:pbase + 64]
        pkt = self.bank(b16=True)
        self.wait("tensor", "ac", conv_ac)
        self.emit("tensor", lambda c: c.transpose(
            pkt[0][:, 0:64], ksl, idsl))
        nkt = self.inc("tensor", "pe")
        self.wait("vector", "pe", nkt)
        self.tbw("ktb", hp, "vector")
        self.emit("vector", lambda c: c.tensor_copy(
            tb["ktb"][hp][:, 0:64], pkt[0][:, 0:64]))
        nktv = self.inc("vector", "ve")
        self.bank_release(pkt, "ve")
        self.wait("scalar", "ve", nktv)
        self.emit("scalar", lambda c: c.activation(
            tb["kq2"][hp][:, 0:64], tb["ktb"][hp][:, 0:64], AF.Square,
            accum_out=tb["col"][0][:, 0:1]))
        self.selfsync("scalar")
        self.emit("scalar", lambda c: c.activation(
            tb["col"][1][:, 0:1], tb["col"][0][:, 0:1], AF.Ln, bias=1e-6))
        self.selfsync("scalar")
        self.emit("scalar", lambda c: c.activation(
            tb["col"][2][:, 0:1], tb["col"][1][:, 0:1], AF.Exp, scale=-0.5))
        nkin = self.inc("scalar", "ac")
        self.tbw("kq2", hp, "scalar")
        self.emit("scalar", lambda c: c.activation(
            tb["kq2"][hp][pbase:pbase + 64, 0:128], ksl, AF.Square))
        nksq = self.inc("scalar", "ac")
        prow = self.bank()
        self.wait("tensor", "ve", nG)
        self.emit("tensor", lambda c: c.matmul(
            prow[0][0:1, 0:128], self.colt["g"][:, p * 16 + h:p * 16 + h + 1],
            self.tril32[:, :], start=True, stop=True))
        self.wait("tensor", "ac", nksq)
        self.emit("tensor", lambda c: c.matmul(
            prow[0][0:1, 128:256], self.ones_col16[pbase:pbase + 64, 0:1],
            tb["kq2"][hp][pbase:pbase + 64, 0:128], start=True, stop=True,
            skip_group_check=True))
        nrow = self.inc("tensor", "pe")
        self.tbr("kq2", hp, "pe")
        self.wait("vector", "pe", nrow)
        self.tbw("rw", hp, "vector")
        self.emit("vector", lambda c: c.tensor_copy(
            tb["rw"][hp][0:1, 0:128], prow[0][0:1, 0:128]))
        self.emit("vector", lambda c: c.tensor_copy(
            tb["rw"][hp][0:1, 128:256], prow[0][0:1, 128:256]))
        nrows = self.inc("vector", "ve")
        self.bank_release(prow, "ve")
        self.wait("scalar", "ve", nrows)
        self.emit("scalar", lambda c: c.activation(
            tb["rw"][hp][0:1, 128:256], tb["rw"][hp][0:1, 128:256], AF.Ln,
            bias=1e-6))
        nlnk = self.inc("scalar", "ac")
        self.wait("vector", "ac", nlnk)
        self.emit("vector", lambda c: c.scalar_tensor_tensor(
            tb["rw"][hp][0:1, 128:256], tb["rw"][hp][0:1, 128:256], -0.5,
            tb["rw"][hp][0:1, 0:128], ALU.mult, ALU.subtract))
        nMrow = self.inc("vector", "ve")
        pkk = self.bank()
        self.wait("tensor", "ac", conv_ac)
        self.emit("tensor", lambda c: c.matmul(
            pkk[0][:, 0:128], ksl, ksl, start=True, stop=True,
            skip_group_check=True))
        self.emit("tensor", lambda c: c.matmul(
            pkk[0][:, 128:256], ksl, qsl, start=True, stop=True,
            skip_group_check=True))
        nkk = self.inc("tensor", "pe")
        self.wait("vector", "ac", max(nkin, prep_ac))
        self.emit("vector", lambda c: c.scalar_tensor_tensor(
            tb["col"][3][:, 0:1], tb["col"][1][:, 0:1], -0.5,
            one("Glb"), ALU.mult, ALU.add))
        nMb = self.inc("vector", "ve")
        pX = self.bank()
        self.wait("tensor", "ve", nMrow)
        self.emit("tensor", lambda c: c.matmul(
            pX[0][:, 0:128], self.ones_row32[0:1, :],
            tb["rw"][hp][0:1, 128:256], start=True, stop=True))
        nX = self.inc("tensor", "pe")
        self.wait("vector", "pe", nX)
        self.tbw("X2", hp, "vector")
        self.emit("vector", lambda c: c.tensor_tensor(
            tb["X2"][hp][:, :], pX[0][:, 0:128], self.addSL[:, :], ALU.add))
        nX2 = self.inc("vector", "ve")
        self.bank_release(pX, "ve")
        self.wait("scalar", "ve", max(nX2, nMb))
        self.tbw("DT", hp, "scalar")
        self.emit("scalar", lambda c: c.activation(
            tb["DT"][hp][:, :], tb["X2"][hp][:, :], AF.Exp,
            bias=tb["col"][3][:, 0:1]))
        nMexp = self.inc("scalar", "ac")
        self.tbr("X2", hp, "ac")
        self.wait("vector", "pe", nkk)
        self.wait("vector", "ac", nMexp)
        self.tbw("Mb", hp, "vector")
        self.emit("vector", lambda c: c.tensor_tensor(
            tb["Mb"][hp][:, :], pkk[0][:, 0:128], tb["DT"][hp][:, :],
            ALU.mult))
        nM = self.inc("vector", "ve")
        self.wait("vector", "ac", max(nkin, prep_ac))
        self.selfsync("vector")
        self.emit("vector", lambda c: c.scalar_tensor_tensor(
            tb["col"][4][:, 0:1], tb["col"][1][:, 0:1], -0.5,
            one("G"), ALU.mult, ALU.subtract))
        nAb = self.inc("vector", "ve")
        pXa = self.bank()
        self.wait("tensor", "ve", nrows)
        self.emit("tensor", lambda c: c.matmul(
            pXa[0][:, 0:128], self.ones_row32[0:1, :],
            tb["rw"][hp][0:1, 0:128], start=True, stop=True))
        nXa = self.inc("tensor", "pe")
        self.tbr("rw", hp, "pe")
        self.wait("vector", "pe", nXa)
        self.tbw("X2", hp, "vector")
        self.emit("vector", lambda c: c.tensor_tensor(
            tb["X2"][hp][:, :], pXa[0][:, 0:128], self.addUC[:, :], ALU.add))
        nXa2 = self.inc("vector", "ve")
        self.bank_release(pXa, "ve")
        self.wait("scalar", "ve", max(nXa2, nAb))
        self.tbw("DT", hp, "scalar")
        self.emit("scalar", lambda c: c.activation(
            tb["DT"][hp][:, :], tb["X2"][hp][:, :], AF.Exp,
            bias=tb["col"][4][:, 0:1]))
        nAexp = self.inc("scalar", "ac")
        self.tbr("X2", hp, "ac")
        self.wait("vector", "ac", nAexp)
        self.tbw("Ab", hp, "vector")
        self.emit("vector", lambda c: c.tensor_tensor(
            tb["Ab"][hp][:, :], pkk[0][:, 128:256], tb["DT"][hp][:, :],
            ALU.mult))
        nAbar = self.inc("vector", "ve")
        self.tbr("DT", hp, "ve")
        self.bank_release(pkk, "ve")
        pmt = self.bank(b16=True)
        self.wait("tensor", "ve", nM)
        self.emit("tensor", lambda c: c.transpose(
            pmt[0][:, 0:128], tb["Mb"][hp][:, :], self.ident16[:, :]))
        nmt = self.inc("tensor", "pe")
        self.wait("vector", "pe", nmt)
        self.tbw("Mpr", hp, "vector")
        self.tbw("Mng", hp, "vector")
        self.emit("vector", lambda c: c.tensor_copy(
            tb["Mpr"][hp][:, :], pmt[0][:, 0:128]))
        self.emit("vector", lambda c: c.tensor_scalar(
            tb["Mng"][hp][:, :], pmt[0][:, 0:128], -1.0, None, ALU.mult))
        nmpr = self.inc("vector", "ve")
        self.bank_release(pmt, "ve")
        pks = self.bank()
        srb = self.st_ready_b[h]
        self.wait("tensor", srb[0], srb[1])
        self.wait("tensor", "ac", conv_ac)
        self.emit("tensor", lambda c: c.matmul(
            pks[0][:, 0:128], ksl, stB16_sl, start=True, stop=True))
        nks = self.inc("tensor", "pe")
        self.wait("vector", "pe", nks)
        self.tbw("KS", hp, "vector")
        self.emit("vector", lambda c: c.tensor_copy(
            tb["KS"][hp][:, :], pks[0][:, 0:128]))
        nksv = self.inc("vector", "ve")
        self.bank_release(pks, "ve")
        pvt = self.bank(b16=True)
        self.wait("tensor", "ac", conv_ac)
        self.emit("tensor", lambda c: c.transpose(
            pvt[0][:, 0:128], vsl, self.ident16[:, :]))
        nvt = self.inc("tensor", "pe")
        self.wait("vector", "pe", nvt)
        self.tbw("vtok", hp, "vector")
        self.emit("vector", lambda c: c.tensor_copy(
            tb["vtok"][hp][:, :], pvt[0][:, 0:128]))
        nvtv = self.inc("vector", "ve")
        self.bank_release(pvt, "ve")
        self.wait("vector", "ac", max(prep_ac, nkin))
        self.wait("vector", "ve", nbG_ve)
        self.wait("vector", "po", self.po_consts)
        self.tbw("dgb", hp, "vector")
        self.tbw("dgb2", hp, "vector")
        self.emit("vector", lambda c: c.tensor_scalar(
            tb["dgb"][hp][:, :], self.ident16[:, :], one("ebt"), None,
            ALU.mult))
        self.emit("vector", lambda c: c.tensor_scalar(
            tb["dgb2"][hp][:, :], self.ident16[:, :], one("nbG"),
            tb["col"][2][:, 0:1], ALU.mult, ALU.mult))
        ndg = self.inc("vector", "ve")
        pR = self.solve_bank
        if pR[1] is not None:
            self.wait("tensor", pR[1], pR[2])
        self.wait("tensor", "ve", max(ndg, nvtv, nksv))
        self.emit("tensor", lambda c: c.matmul(
            pR[0][:, 0:128], tb["dgb"][hp][:, :], tb["vtok"][hp][:, :],
            start=True, stop=False, skip_group_check=True))
        self.emit("tensor", lambda c: c.matmul(
            pR[0][:, 0:128], tb["dgb2"][hp][:, :], tb["KS"][hp][:, :],
            start=False, stop=True, skip_group_check=True))
        nR = self.inc("tensor", "pe")
        self.tbr("KS", hp, "pe")
        self.tbr("vtok", hp, "pe")
        self.tbr("dgb", hp, "pe")
        self.tbr("dgb2", hp, "pe")
        self.wait("vector", "pe", nR)
        self.tbw("tj", hp, "vector")
        self.emit("vector", lambda c: c.tensor_copy(
            tb["tj"][hp][:, :], pR[0][:, 0:128]))
        ntj = self.inc("vector", "ve")
        prime_list = [tb["Mng"][hp]]
        curP, curPT = tb["Mb"][hp], tb["Mpr"][hp]
        sq_ve = nmpr
        bufs = [(tb["pwA"][hp], tb["pwAT"][hp]),
                (tb["pwB"][hp], tb["pwBT"][hp])]
        for lvl in range(1, 7):
            dst, dstT = bufs[lvl % 2]
            psq = self.bank()
            self.wait("tensor", "ve", sq_ve)
            if lvl < 6:
                self.emit("tensor", lambda c, a=curPT, b=curP: c.matmul(
                    psq[0][:, 0:128], a[:, 0:128], b[:, 0:128],
                    start=True, stop=False, skip_group_check=True))
                self.emit("tensor", lambda c, a=curP, b=curPT: c.matmul(
                    psq[0][:, 128:256], a[:, 0:128], b[:, 0:128],
                    start=True, stop=True, skip_group_check=True))
            else:
                self.emit("tensor", lambda c, a=curP, b=curPT: c.matmul(
                    psq[0][:, 0:128], a[:, 0:128], b[:, 0:128],
                    start=True, stop=True, skip_group_check=True))
            nsq = self.inc("tensor", "pe")
            self.wait("vector", "pe", nsq)
            dnames = {id(self.tb["pwA"][hp]): ("pwA", hp),
                      id(self.tb["pwAT"][hp]): ("pwAT", hp),
                      id(self.tb["pwB"][hp]): ("pwB", hp),
                      id(self.tb["pwBT"][hp]): ("pwBT", hp)}
            if lvl < 6:
                self.tbw(*dnames[id(dst)][:1], hp, "vector") if False else None
                self.tbw(dnames[id(dst)][0], hp, "vector")
                self.tbw(dnames[id(dstT)][0], hp, "vector")
                self.emit("vector", lambda c, d=dst: c.tensor_copy(
                    d[:, 0:128], psq[0][:, 0:128]))
                self.emit("vector", lambda c, d=dstT: c.tensor_copy(
                    d[:, 0:128], psq[0][:, 128:256]))
            else:
                self.tbw(dnames[id(dstT)][0], hp, "vector")
                self.emit("vector", lambda c, d=dstT: c.tensor_copy(
                    d[:, 0:128], psq[0][:, 0:128]))
            sq_ve = self.inc("vector", "ve")
            self.bank_release(psq, "ve")
            prime_list.append(dstT)
            curP, curPT = dst, dstT
        app_ve = ntj
        for lvl in range(6, 0, -1):
            Pp = prime_list[lvl]
            self.wait("tensor", "ve", app_ve)
            self.emit("tensor", lambda c, a=Pp: c.matmul(
                pR[0][:, 0:128], a[:, 0:128], tb["tj"][hp][:, :],
                start=False, stop=False, skip_group_check=True))
            napp = self.inc("tensor", "pe")
            self.wait("vector", "pe", napp)
            self.emit("vector", lambda c: c.tensor_copy(
                tb["tj"][hp][:, :], pR[0][:, 0:128]))
            app_ve = self.inc("vector", "ve")
        self.wait("tensor", "ve", app_ve)
        self.emit("tensor", lambda c: c.matmul(
            pR[0][:, 0:128], tb["Mng"][hp][:, :], tb["tj"][hp][:, :],
            start=False, stop=True, skip_group_check=True))
        nU = self.inc("tensor", "pe")
        for nm in ("Mb", "Mpr", "Mng", "pwA", "pwAT", "pwB", "pwBT", "tj"):
            self.tbr(nm, hp, "pe")
        self.wait("vector", "pe", nU)
        self.tbw("U16", hp, "vector")
        self.emit("vector", lambda c: c.tensor_copy(
            tb["U16"][hp][:, :], pR[0][:, 0:128]))
        nU16 = self.inc("vector", "ve")
        self.bank_release(pR, "ve")
        if cch == 1 and h == 0:
            self.maybe_tap("c1U", tb["U16"][hp][:, :])
            self.maybe_tap("c1KS", tb["KS"][hp][:, :])
        if cch == 0 and h == 0:
            self.maybe_tap("bU16", tb["U16"][hp][:, :])
            self.maybe_tap("bMb", tb["Mb"][hp][:, :])
            self.maybe_tap("bMpr", tb["Mpr"][hp][:, :])
            self.maybe_tap("bAb", tb["Ab"][hp][:, :])
            self.maybe_tap("bKS", tb["KS"][hp][:, :])
            self.maybe_tap("bktb", tb["ktb"][hp][:, :])
            self.maybe_tap("bvtok", tb["vtok"][hp][:, :])
            self.maybe_tap("bdgb", tb["dgb"][hp][:, :])
            self.maybe_tap("bdgb2", tb["dgb2"][hp][:, :])
        pO = self.bank()
        self.wait("tensor", srb[0], srb[1])
        self.emit("tensor", lambda c: c.matmul(
            pO[0][:, 0:128], qsl, stB16_sl, start=True, stop=True))
        nqs = self.inc("tensor", "pe")
        self.wait("scalar", "pe", nqs)
        self.wait("scalar", "ac", prep_ac)
        self.tbw("o1", hp, "scalar")
        self.emit("scalar", lambda c: c.activation(
            tb["o1"][hp][:, :], pO[0][:, 0:128], AF.Copy, scale=one("eG")))
        no1 = self.inc("scalar", "ac")
        self.bank_release(pO, "ac")
        pO2 = self.bank()
        self.wait("tensor", "ve", max(nAbar, nU16))
        self.emit("tensor", lambda c: c.matmul(
            pO2[0][:, 0:128], tb["Ab"][hp][:, :], tb["U16"][hp][:, :],
            start=True, stop=True))
        nO2 = self.inc("tensor", "pe")
        self.tbr("Ab", hp, "pe")
        self.wait("scalar", "ac", nksq)
        self.tbw("osq", hp, "scalar")
        self.emit("scalar", lambda c: c.activation(
            tb["osq"][hp][pbase:pbase + 64, 0:128], qsl, AF.Square))
        nqsq = self.inc("scalar", "ac")
        peps = self.bank()
        self.wait("tensor", "ac", nqsq)
        self.emit("tensor", lambda c: c.matmul(
            peps[0][:, 0:1], tb["osq"][hp][pbase:pbase + 64, 0:128],
            self.ones_col16[pbase:pbase + 64, 0:1], start=True, stop=True))
        neps = self.inc("tensor", "pe")
        self.tbr("osq", hp, "pe")
        self.wait("vector", "pe", neps)
        self.emit("vector", lambda c: c.tensor_scalar(
            tb["col"][3][:, 0:1], peps[0][:, 0:1], EPS_RMS * DK,
            EPS_RMS * DK * 1e-6, ALU.mult, ALU.add))
        nepsv = self.inc("vector", "ve")
        self.bank_release(peps, "ve")
        self.wait("vector", "pe", nO2)
        self.wait("vector", "ac", no1)
        self.emit("vector", lambda c: c.tensor_tensor(
            tb["o1"][hp][:, :], tb["o1"][hp][:, :], pO2[0][:, 0:128],
            ALU.add))
        nOf = self.inc("vector", "ve")
        self.bank_release(pO2, "ve")
        if cch == 1 and h == 0:
            self.maybe_tap("c1o1", tb["o1"][hp][:, :])
        if cch == 0 and h == 0:
            self.maybe_tap("bo1", tb["o1"][hp][:, :])
        self.wait("scalar", "ve", nOf)
        self.tbw("osq", hp, "scalar")
        self.emit("scalar", lambda c: c.activation(
            tb["osq"][hp][:, :], tb["o1"][hp][:, :], AF.Square,
            accum_out=tb["col"][0][:, 0:1]))
        self.selfsync("scalar")
        self.wait("scalar", "ve", nepsv)
        self.emit("scalar", lambda c: c.activation(
            tb["col"][1][:, 0:1], tb["col"][0][:, 0:1], AF.Ln,
            bias=tb["col"][3][:, 0:1], scale=1.0 / DV))
        self.selfsync("scalar")
        self.emit("scalar", lambda c: c.activation(
            tb["col"][5][:, 0:1], tb["col"][1][:, 0:1], AF.Exp, scale=-0.5))
        nrms = self.inc("scalar", "ac")
        self.wait("vector", "ac", nrms)
        self.tbw("on", hp, "vector")
        self.emit("vector", lambda c: c.tensor_scalar(
            tb["on"][hp][:, :], tb["o1"][hp][:, :], tb["col"][5][:, 0:1],
            None, ALU.mult))
        non = self.inc("vector", "ve")
        self.tbr("o1", hp, "ve")
        if cch == 0 and h == 0:
            self.maybe_tap("bon", tb["on"][hp][:, :])
        pot = self.bank(b16=True)
        self.wait("tensor", "ve", non)
        self.emit("tensor", lambda c: c.transpose(
            pot[0][:, 0:128], tb["on"][hp][:, :], self.ident16[:, :]))
        noT = self.inc("tensor", "pe")
        self.tbr("on", hp, "pe")
        self.wait("vector", "pe", noT)
        self.emit("vector", lambda c: c.tensor_copy(
            self.ar(A_OF + h * NT + coff, C), pot[0][:, 0:128]))
        self.inc("vector", "ve")
        self.bank_release(pot, "ve")
        self.wait("vector", "ac", max(nkin, prep_ac))
        self.tbw("ktl", hp, "vector")
        self.emit("vector", lambda c: c.tensor_scalar(
            tb["ktl"][hp][:, 0:64], tb["ktb"][hp][:, 0:64],
            tb["col"][2][:, 0:1], one("edG"), ALU.mult, ALU.mult))
        nktl = self.inc("vector", "ve")
        self.tbr("ktb", hp, "ve")
        pdS = self.bank()
        self.wait("tensor", "ve", max(nktl, nU16))
        self.emit("tensor", lambda c: c.matmul(
            pdS[0][pbase:pbase + 64, 0:128], tb["ktl"][hp][:, 0:64],
            tb["U16"][hp][:, :], start=True, stop=True,
            tile_position=(0, pbase)))
        ndS = self.inc("tensor", "pe")
        self.tbr("ktl", hp, "pe")
        self.tbr("U16", hp, "pe")
        self.wait("vector", "pe", ndS)
        self.wait("vector", "ac", prep_ac)
        self.emit("vector", lambda c: c.tensor_scalar(
            stB_sl, stB_sl,
            self.colt["etdB"][pbase:pbase + 64,
                              p * 16 + h:p * 16 + h + 1],
            None, ALU.mult))
        self.emit("vector", lambda c: c.tensor_tensor(
            stB_sl, stB_sl, pdS[0][pbase:pbase + 64, 0:128], ALU.add))
        nst = self.inc("vector", "ve")
        self.bank_release(pdS, "ve")
        self.wait("scalar", "ve", nst)
        self.emit("scalar", lambda c: c.copy(stB16_sl, stB_sl))
        ncast = self.inc("scalar", "ac")
        self.st_ready_b[h] = ("ac", ncast)

    def attn_b_out(self):
        of_ve = self.n_of("ve")
        for m in range(16):
            self.wait("scalar", "ve", of_ve)
            self.emit("scalar", lambda c, m_=m: c.activation(
                self.ar(A_GOUT + m_ * NT, NT), self.ar(A_GOUT + m_ * NT, NT),
                AF.Silu))
            nsl = self.inc("scalar", "ac")
            self.wait("vector", "ac", nsl)
            self.emit("vector", lambda c, m_=m: c.tensor_tensor(
                self.ar(A_OF + m_ * NT, NT), self.ar(A_OF + m_ * NT, NT),
                self.ar(A_GOUT + m_ * NT, NT), ALU.mult))
            self.inc("vector", "ve")
        og_ve = self.n_of("ve")
        last_add = None
        for m in range(16):
            buf, slot, mir = self.next_group()
            pbk = self.bank()
            self.wait("tensor", "ve", og_ve)
            for k in range(KC):
                self.emit("tensor", lambda c, k_=k: c.matmul(
                    pbk[0][:, 0:NT], buf[:, k_ * 128:(k_ + 1) * 128],
                    self.ar(A_OF + k_ * NT, NT), start=(k_ == 0),
                    stop=(k_ == KC - 1), skip_group_check=True))
            self.inc("tensor", "pe")
            self.group_done(slot, mir)
            nd = self.n_of("pe")
            xsl = self.xf[:, m * NT:(m + 1) * NT]
            self.wait("vector", "pe", nd)
            self.emit("vector", lambda c, x_=xsl: c.tensor_tensor(
                x_, x_, pbk[0][:, 0:NT], ALU.add))
            last_add = self.inc("vector", "ve")
            self.bank_release(pbk, "ve")
        self.xver = ("ve", last_add)

    def build(self):
        st = ExitStack()
        with st:
            self.alloc(st.enter_context)
            with self.nc.Block() as block:
                @block.sync
                def _(cur):
                    self.reset_pass("sync", cur)
                    self.program()

                @block.gpsimd
                def _(cur):
                    self.reset_pass("gpsimd", cur)
                    self.program()

                @block.vector
                def _(cur):
                    self.reset_pass("vector", cur)
                    self.program()

                @block.tensor
                def _(cur):
                    self.reset_pass("tensor", cur)
                    self.program()

                @block.scalar
                def _(cur):
                    self.reset_pass("scalar", cur)
                    self.program()
        return self.nc


def arrange_w(wT, nm, nk):
    K_in, M_out = wT.shape
    assert K_in == nk * 128 and M_out == nm * 128
    return np.ascontiguousarray(
        wT.reshape(nk, 128, nm, 128).transpose(2, 1, 0, 3)
        .reshape(nm, 128, nk * 128))


def arrange_small(wT):
    return np.ascontiguousarray(
        wT.reshape(16, 128, 16).transpose(1, 0, 2).reshape(128, 256))


def arrange_conv(cw):
    ch = cw.shape[0]
    nch = ch // 128
    return np.ascontiguousarray(
        cw[:, 0, :].reshape(nch, 128, 4).transpose(1, 0, 2)
        .reshape(128, nch * 4))


def make_inputs(core, full):
    f = lambda k: np.asarray(full[k], np.float32)
    x = f("L0_attn_a_x")[2 * core:2 * core + 2]
    xa = np.ascontiguousarray(x.transpose(0, 2, 1).reshape(2, 16, 128, 2048))
    d = {
        "x": xa,
        "wq": arrange_w(f("L0_attn_a_Wq"), 16, 16),
        "wk": arrange_w(f("L0_attn_a_Wk"), 16, 16),
        "wv": arrange_w(f("L0_attn_a_Wv"), 16, 16),
        "wg": arrange_w(f("L0_ffn_w_gate").T, 64, 16),
        "wu": arrange_w(f("L0_ffn_w_up").T, 64, 16),
        "wd": arrange_w(f("L0_ffn_w_down").T, 16, 64),
        "wqk": arrange_w(f("L0_attn_b_qk_w").T, 16, 16),
        "wvg": arrange_w(f("L0_attn_b_vg_w").T, 32, 16),
        "wo": arrange_w(f("L0_attn_b_o_w").T, 16, 16),
        "wal": arrange_small(f("L0_attn_a_Walpha")),
        "wbe": arrange_small(f("L0_attn_a_Wbeta")),
        "wa": arrange_small(f("L0_attn_b_a_w").T),
        "wb": arrange_small(f("L0_attn_b_b_w").T),
        "cq": arrange_conv(f("L0_attn_b_q_conv_w")),
        "ck": arrange_conv(f("L0_attn_b_k_conv_w")),
        "cv": arrange_conv(f("L0_attn_b_v_conv_w")),
        "bq": np.ascontiguousarray(f("L0_attn_a_bq").reshape(16, 128).T),
        "bk": np.ascontiguousarray(f("L0_attn_a_bk").reshape(16, 128).T),
        "bv": np.ascontiguousarray(f("L0_attn_a_bv").reshape(16, 128).T),
        "bal": f("L0_attn_a_balpha").reshape(16, 1),
        "bbe": f("L0_attn_a_bbeta").reshape(16, 1),
        "bbb": f("L0_attn_b_b_bias").reshape(16, 1),
        "bbt": f("L0_attn_b_beta_bias").reshape(16, 1),
        "bdt": f("L0_attn_b_dt_bias").reshape(16, 1),
        "alog": f("L0_attn_b_A_log").reshape(16, 1),
    }
    return {k: np.ascontiguousarray(v) for k, v in d.items()}


_NC_CACHE = {}


def get_nc(n_sublayers=4, n_tiles=NTILES):
    key = (n_sublayers, n_tiles)
    if key not in _NC_CACHE:
        _NC_CACHE[key] = MK(n_sublayers, n_tiles).build()
    return _NC_CACHE[key]


def kernel(**inputs):
    nc = get_nc()
    in_maps = [make_inputs(c, inputs) for c in range(8)]
    res = run_bass_kernel_spmd(nc, in_maps, core_ids=list(range(8)))
    out = np.empty((16, 2048, 2048), np.float32)
    for c in range(8):
        o = np.asarray(res.results[c]["out"])
        out[2 * c:2 * c + 2] = o.transpose(0, 3, 1, 2).reshape(2, 2048, 2048)
    return out


# version 02
# speedup vs baseline: 1.2875x; 1.2875x over previous
import numpy as np
from contextlib import ExitStack

ENGS = ("sync", "gpsimd", "vector", "tensor", "scalar")

import concourse.bass as bass
import concourse.mybir as mybir
from concourse.bass_utils import run_bass_kernel_spmd

F32 = mybir.dt.float32
BF16 = mybir.dt.bfloat16
AF = mybir.ActivationFunctionType
ALU = mybir.AluOpType

D = 2048
KC = 16
NT = 512
C = 128
NCH = NT // C
NTILES = 8
HA = 16
HB = 16
DK = 64
DV = 128
NEG = -1.0e30
EPS_RMS = float(np.finfo(np.float32).eps)

A_QF, A_KF, A_VF = 0, 8192, 16384
A_H = 0
A_QC, A_KC, A_VC = 0, 4096, 8192
A_GOUT = 16384
A_OF = 24576
ARENA_N = 32768


class Sem:
    __slots__ = ("h", "n", "name")

    def __init__(self, h, name):
        self.h = h
        self.n = 0
        self.name = name


class MK:
    PREFETCH = 2

    def __init__(self, n_sublayers=4, n_tiles=NTILES, tap=None, serialize=0):
        self.n_sublayers = n_sublayers
        self.n_tiles = n_tiles
        self.tap_point = tap
        self.serialize = serialize
        self.nc = bass.Bass("TRN2", target_bir_lowering=False)
        self.eng = None
        self.cur = None
        self._last = {}
        self.sems = {}

    def emit(self, engine, fn):
        self._last_inc[engine] = None
        if self.eng == engine:
            ins = fn(self.cur)
            self._last[engine] = ins
            return ins
        return None

    def inc(self, engine, semname, k=1):
        s = self.sems[semname]
        if self.eng == engine:
            self._last[engine].then_inc(s.h, k)
        s.n += k
        self._last_inc[engine] = (semname, s.n)
        return s.n

    def n_of(self, semname):
        return self.sems[semname].n

    def wait(self, engine, semname, val=None):
        s = self.sems[semname]
        v = s.n if val is None else val
        assert v <= s.n, f"wait on future count {semname} {v}>{s.n}"
        cache = self._wcache.setdefault(engine, {})
        if cache.get(semname, 0) >= v:
            return
        cache[semname] = v
        if self.eng == engine and v > 0:
            self.cur.wait_ge(s.h, v)

    ENG_SEM = {"vector": "ve", "scalar": "ac", "gpsimd": "po",
               "tensor": "pe"}

    def selfsync(self, engine):
        li = self._last_inc.get(engine)
        if li is not None:
            sn, n = li
        else:
            sn = self.ENG_SEM[engine]
            n = self.inc(engine, sn)
        if self.eng == engine:
            self.cur.wait_ge(self.sems[sn].h, n)
        cache = self._wcache.setdefault(engine, {})
        cache[sn] = max(cache.get(sn, 0), n)

    def bank(self, b16=False):
        lst = self.banks_b16 if b16 else self.banks_f32
        n = len(lst)
        i0 = self.bi16 if b16 else self.bi32
        for k in range(n):
            i = (i0 + k) % n
            if not lst[i][3]:
                break
        else:
            raise RuntimeError("no free PSUM bank (all held)")
        if b16:
            self.bi16 = (i + 1) % n
        else:
            self.bi32 = (i + 1) % n
        b = lst[i]
        b[3] = True
        if b[1] is not None:
            self.wait("tensor", b[1], b[2])
        return b

    def bank_release(self, b, semname):
        b[1] = semname
        b[2] = self.n_of(semname)
        b[3] = False

    def ar(self, off, n):
        return self.arena[:, off:off + n]

    def alloc(self, E):
        nc = self.nc
        self.x_in = nc.dram_tensor("x", [2, KC, 128, 2048], F32,
                                   kind="ExternalInput")
        self.x_out = nc.dram_tensor("out", [2, KC, 128, 2048], F32,
                                    kind="ExternalOutput")
        self.w_shapes = [("wq", 16, 16), ("wk", 16, 16), ("wv", 16, 16),
                         ("wg", 64, 16), ("wu", 64, 16), ("wd", 16, 64),
                         ("wqk", 16, 16), ("wvg", 32, 16), ("wo", 16, 16)]
        self.w_in = {n: nc.dram_tensor(n, [m, 128, k * 128], F32,
                                       kind="ExternalInput")
                     for n, m, k in self.w_shapes}
        self.w_scr = {n: nc.dram_tensor(n + "_s", [m, 128, k * 128], BF16,
                                        kind="Internal")
                      for n, m, k in self.w_shapes}
        self.wsm_in = {n: nc.dram_tensor(n, [128, KC * 16], F32,
                                         kind="ExternalInput")
                       for n in ("wal", "wbe", "wa", "wb")}
        self.cv_in = {n: nc.dram_tensor(n, [128, c * 4], F32,
                                        kind="ExternalInput")
                      for n, c in [("cq", 8), ("ck", 8), ("cv", 16)]}
        self.bias_in = {}
        for n in ("bq", "bk", "bv"):
            self.bias_in[n] = nc.dram_tensor(n, [128, 16], F32,
                                             kind="ExternalInput")
        for n in ("bal", "bbe", "bbb", "bbt", "bdt"):
            self.bias_in[n] = nc.dram_tensor(n, [16, 1], F32,
                                             kind="ExternalInput")
        self.alog_in = nc.dram_tensor("alog", [16, 1], F32,
                                      kind="ExternalInput")

        self.xf = E(nc.sbuf_tensor("xf", [128, KC * NT], F32))
        self.xb = E(nc.sbuf_tensor("xbr", [128, KC * NT], BF16))
        self.arena = E(nc.sbuf_tensor("arena", [128, ARENA_N], BF16))
        self.wbuf = [E(nc.sbuf_tensor(f"wbuf{i}", [128, 2048], BF16))
                     for i in range(3)]
        self.rows = E(nc.sbuf_tensor("rows", [1, 1536], F32))
        self.ones_col16 = E(nc.sbuf_tensor("ones_col16", [128, 1], BF16))
        self.ones_row32 = E(nc.sbuf_tensor("ones_row32", [1, 128], F32))
        self.r_b = E(nc.sbuf_tensor("r_b", [128, NT], F32))
        self.m_b = E(nc.sbuf_tensor("m_b", [128, NT], F32))
        self.ident16 = E(nc.sbuf_tensor("ident16", [128, 128], BF16))
        self.ident32 = E(nc.sbuf_tensor("ident32", [128, 128], F32))
        self.tril32 = E(nc.sbuf_tensor("tril32", [128, 128], F32))
        self.slow32 = E(nc.sbuf_tensor("slow32", [128, 128], F32))
        self.addUC = E(nc.sbuf_tensor("addUC", [128, 128], F32))
        self.addSL = E(nc.sbuf_tensor("addSL", [128, 128], F32))
        self.bcol = {n: E(nc.sbuf_tensor("bc_" + n, [128, 16], F32))
                     for n in ("bq", "bk", "bv")}
        self.bsm = {n: E(nc.sbuf_tensor("bs_" + n, [16, 1], F32))
                    for n in ("bal", "bbe", "bbb", "bbt", "bdt", "bb2")}
        self.wsm16 = {n: E(nc.sbuf_tensor("w16_" + n, [128, KC * 16], BF16))
                      for n in ("wal", "wbe", "wa", "wb")}
        self.cvw = {n: E(nc.sbuf_tensor("cvw_" + n, [128, c * 4], F32))
                    for n, c in [("cq", 8), ("ck", 8), ("cv", 16)]}
        self.negexpA = E(nc.sbuf_tensor("negexpA", [16, 1], F32))
        self.alog_sb = E(nc.sbuf_tensor("alog_sb", [16, 1], F32))
        self.stA = E(nc.sbuf_tensor("stA", [128, HA * 128], F32))
        self.stA16 = E(nc.sbuf_tensor("stA16", [128, HA * 128], BF16))
        self.stB = E(nc.sbuf_tensor("stB", [128, (HB // 2) * 128], F32))
        self.stB16 = E(nc.sbuf_tensor("stB16", [128, (HB // 2) * 128], BF16))
        self.colt = {n: E(nc.sbuf_tensor("colt_" + n, [128, 32], F32))
                     for n in ("la", "lb", "cla", "dtx", "etd", "ecd",
                               "edte", "ebt", "bias", "eb2", "g", "G", "dG",
                               "eG", "edG", "etdB", "Glb", "nbG")}
        self.za = E(nc.sbuf_tensor("za", [16, NT], F32))
        self.zb = E(nc.sbuf_tensor("zb", [16, NT], F32))
        self.loga = E(nc.sbuf_tensor("loga", [16, NT], F32))
        self.logb = E(nc.sbuf_tensor("logb", [16, NT], F32))
        self.zt1 = E(nc.sbuf_tensor("zt1", [16, NT], F32))
        self.zt2 = E(nc.sbuf_tensor("zt2", [16, NT], F32))
        self.gbuf = E(nc.sbuf_tensor("gbuf", [16, NT], F32))
        self.stage = [E(nc.sbuf_tensor(f"stage{i}", [128, 515], BF16))
                      for i in range(3)]
        self.xsq = [self.stage[0][:, 0:NT], self.stage[1][:, 0:NT]]
        self.tails = E(nc.sbuf_tensor("tails", [128, 96], F32))
        self.tb = {}
        for name, w, dt, nb in [
            ("ktok", 128, BF16, 2), ("vtok", 128, BF16, 2),
            ("kw", 128, BF16, 2), ("attns", 128, BF16, 2),
            ("DT", 128, BF16, 2), ("X2", 128, F32, 2),
            ("ytok", 128, BF16, 2), ("yg", 128, F32, 2),
            ("Mb", 128, BF16, 2), ("Mpr", 128, BF16, 2),
            ("Mng", 128, BF16, 2),
            ("pwA", 128, BF16, 2), ("pwAT", 128, BF16, 2),
            ("pwB", 128, BF16, 2), ("pwBT", 128, BF16, 2),
            ("tj", 128, BF16, 2), ("U16", 128, BF16, 2),
            ("KS", 128, BF16, 2), ("Ab", 128, BF16, 2),
            ("dgb", 128, BF16, 2), ("dgb2", 128, BF16, 2),
            ("o1", 128, F32, 2), ("on", 128, BF16, 2), ("osq", 128, BF16, 2),
            ("ktb", 64, BF16, 2), ("ktl", 64, BF16, 2),
            ("kq2", 128, BF16, 2), ("col", 1, F32, 12), ("sg", NT, BF16, 2),
            ("cvt", NT, F32, 2),
        ]:
            self.tb[name] = [E(nc.sbuf_tensor(f"t_{name}{i}", [128, w], dt))
                             for i in range(nb)]
        self.tb["rw"] = [E(nc.sbuf_tensor(f"t_rw{i}", [1, 256], F32))
                         for i in range(2)]
        self.CONSTS = [1e-5, 1e-6, 1e-24, 1.0, -1.0, -0.5,
                       EPS_RMS, 1.0 / DV, float(np.log(2.0)), 0.5, 2.0]
        self.cst = E(nc.sbuf_tensor("cst", [128, len(self.CONSTS)], F32))
        for i, v in enumerate(self.CONSTS):
            nc.const_aps.register(F32, v, self.cst[:, i:i + 1])
        self.banks_f32 = [[E(nc.psum_tensor(f"pf{i}", [128, 512], F32)),
                           None, 0, False] for i in range(4)]
        self.solve_bank = [[E(nc.psum_tensor(f"psolve{i}", [128, 512], F32)),
                            None, 0, False] for i in range(2)]
        self.banks_b16 = [[E(nc.psum_tensor(f"pb{i}", [128, 512], BF16)),
                           None, 0, False] for i in range(2)]
        for n in ("dS", "dA", "dG", "pe", "ve", "ac", "po"):
            self.sems[n] = Sem(E(nc.semaphore(n)), n)
        self.tap_t = None
        if self.tap_point is not None:
            self.tap_t = nc.dram_tensor("tap", [128, 8192], F32,
                                        kind="ExternalOutput")

    def weight_order_tile(self, sublayers):
        order = []
        if 1 in sublayers:
            for w in ("wq", "wk", "wv"):
                order += [(w, i) for i in range(16)]
        if 2 in sublayers:
            for i in range(64):
                order += [("wg", i), ("wu", i)]
            order += [("wd", i) for i in range(64)]
        if 3 in sublayers:
            order += [("wqk", i) for i in range(16)]
            order += [("wvg", i) for i in range(32)]
            order += [("wo", i) for i in range(16)]
        if 4 in sublayers:
            for i in range(64):
                order += [("wg", i), ("wu", i)]
            order += [("wd", i) for i in range(64)]
        return order

    def stream_init(self):
        subl = list(range(1, self.n_sublayers + 1))
        per_tile = self.weight_order_tile(subl)
        self.g_order = []
        for t in range(self.n_tiles):
            for (w, gi) in per_tile:
                self.g_order.append((t, w, gi))
        self.iptr = 0
        self.cptr = 0
        self.s_slot = 0
        self.s_rel = [None] * len(self.wbuf)
        self.comp = {}
        self.mirrored = set()

    def w_src(self, w, gi, first):
        base = self.w_in[w] if first else self.w_scr[w]
        if w == "wd":
            m, q = gi // 4, gi % 4
            return base[m, :, q * 2048:(q + 1) * 2048]
        return base[gi, :, :]

    def issue_upto(self, target):
        while self.iptr <= target and self.iptr < len(self.g_order):
            idx = self.iptr
            t, w, gi = self.g_order[idx]
            first = (t == 0)
            slot = self.s_slot
            self.s_slot = (slot + 1) % len(self.wbuf)
            rel = self.s_rel[slot]
            buf = self.wbuf[slot]
            q = "gpsimd" if first else "sync"
            sem = "dG" if first else "dS"
            if rel is not None:
                self.wait(q, rel[0], rel[1])
                if len(rel) == 4:
                    self.wait(q, rel[2], rel[3])
            src = self.w_src(w, gi, first)
            self.emit(q, lambda c, d_=buf, s_=src: c.dma_start(d_[:, :], s_))
            n = self.inc(q, sem, 16)
            mkey = (w, gi)
            do_mirror = first and mkey not in self.mirrored
            if do_mirror:
                self.mirrored.add(mkey)
                self.wait("scalar", sem, n)
                dst = self.w_src(w, gi, False)
                self.emit("scalar", lambda c, d_=dst, b_=buf:
                          c.dma_start(d_, b_[:, :]))
                self.inc("scalar", "dA", 16)
            self.comp[idx] = (sem, n, slot,
                              ("dA", self.n_of("dA")) if do_mirror else None)
            self.iptr += 1

    def next_group(self):
        idx = self.cptr
        self.cptr += 1
        self.issue_upto(idx + self.PREFETCH)
        sem, n, slot, mir = self.comp[idx]
        self.wait("tensor", sem, n)
        return self.wbuf[slot], slot, mir

    def group_done(self, slot, mir):
        rel = ("pe", self.n_of("pe"))
        if mir is not None:
            rel = rel + mir
        self.s_rel[slot] = rel

    def reset_pass(self, engine, cur):
        self.eng = engine
        self.cur = cur
        self._last = {}
        self._wcache = {}
        self._last_inc = {}
        self.tb_rel = {}
        for s in self.sems.values():
            s.n = 0
        for b in self.banks_f32 + self.banks_b16 + self.solve_bank:
            b[1] = None
            b[2] = 0
            b[3] = False
        self.bi32 = 0
        self.bi16 = 0
        self.stream_init()
        self.chunk_ctr = 0
        self.head_ctr = 0
        self.st_ready_a = [("po", 0)] * HA
        self.st_ready_b = [("po", 0)] * HB
        self.xver = ("po", 0)

    def col(self, hp, i):
        return self.tb["col"][hp * 6 + i][:, 0:1]

    def tbw(self, name, hp, engine):
        rel = self.tb_rel.get((name, hp))
        if rel is not None:
            self.wait(engine, rel[0], rel[1])
        return self.tb[name][hp]

    def tbr(self, name, hp, sem):
        self.tb_rel[(name, hp)] = (sem, self.n_of(sem))

    def drive(self, *gens):
        gens = list(gens)
        while gens:
            done = []
            for g in gens:
                try:
                    next(g)
                except StopIteration:
                    done.append(g)
            for g in done:
                gens.remove(g)

    def barrier(self):
        for e in ENGS:
            for s in ("pe", "ve", "ac", "po", "dS", "dA", "dG"):
                self.wait(e, s)

    def maybe_tap(self, name, ap):
        if self.tap_point != name or getattr(self, "_tapped", None) == name:
            return
        self._tapped = name
        for s in ("pe", "ve", "ac", "po"):
            self.wait("gpsimd", s)
        P, W = ap.shape
        self.emit("gpsimd", lambda c: c.dma_start(
            self.tap_t[0:P, 0:W], ap, allow_slow_non_contiguous=True))
        self.inc("gpsimd", "dG", 16)
        for e in ("gpsimd", "vector", "scalar", "tensor", "sync"):
            self.wait(e, "dG")

    def program(self):
        self._tapped = None
        self.preamble()
        for t in range(self.n_tiles):
            self.tile(t)
        self.wait("scalar", "dA")
        self.wait("sync", "dS")

    def preamble(self):
        gp = lambda fn: self.emit("gpsimd", fn)
        for tn, cmp_, fill, cm, stp in [
            (self.ident16, ALU.not_equal, 1.0, 1, -1),
            (self.ident32, ALU.not_equal, 1.0, 1, -1),
            (self.tril32, ALU.is_gt, 1.0, 1, -1),
            (self.slow32, ALU.is_ge, 1.0, -1, 1),
            (self.addUC, ALU.is_ge, NEG, -1, 1),
            (self.addSL, ALU.is_gt, NEG, 1, -1),
        ]:
            gp(lambda c, t_=tn: c.memset(t_[:, :], 0.0))
            gp(lambda c, t_=tn, c_=cmp_, f=fill, m=cm, s=stp:
               c.affine_select(
                   out=t_[:, :], in_=t_[:, :], compare_op=c_, fill=f,
                   base=0, pattern=[[s, 128]], channel_multiplier=m))
        gp(lambda c: c.memset(self.ones_col16[:, :], 1.0))
        gp(lambda c: c.memset(self.ones_row32[:, :], 1.0))
        gp(lambda c: c.memset(self.stA[:, :], 0.0))
        gp(lambda c: c.memset(self.stA16[:, :], 0.0))
        gp(lambda c: c.memset(self.stB[:, :], 0.0))
        gp(lambda c: c.memset(self.stB16[:, :], 0.0))
        for i, v in enumerate(self.CONSTS):
            gp(lambda c, i_=i, v_=v: c.memset(self.cst[:, i_:i_ + 1], v_))
        self.inc("gpsimd", "po")
        self.emit("vector", lambda c: c.memset(self.tails[:, :], 0.0))
        self.inc("vector", "ve")
        self.po_consts = self.n_of("po")
        for n in ("wal", "wbe", "wa", "wb"):
            self.emit("gpsimd", lambda c, n_=n: c.dma_start(
                self.wsm16[n_][:, :], self.wsm_in[n_][:, :]))
            self.inc("gpsimd", "dG", 16)
        self.dg_small = self.n_of("dG")
        for n in ("bq", "bk", "bv"):
            self.emit("sync", lambda c, n_=n: c.dma_start(
                self.bcol[n_][:, :], self.bias_in[n_][:, :]))
            self.inc("sync", "dS", 16)
        for n in ("bal", "bbe", "bbb", "bbt", "bdt"):
            self.emit("sync", lambda c, n_=n: c.dma_start(
                self.bsm[n_][:, :], self.bias_in[n_][:, :]))
            self.inc("sync", "dS", 16)
        for n in ("cq", "ck", "cv"):
            self.emit("sync", lambda c, n_=n: c.dma_start(
                self.cvw[n_][:, :], self.cv_in[n_][:, :]))
            self.inc("sync", "dS", 16)
        self.emit("sync", lambda c: c.dma_start(self.alog_sb[:, :],
                                                self.alog_in[:, :]))
        self.inc("sync", "dS", 16)
        self.ds_small = self.n_of("dS")
        self.wait("scalar", "dS", self.ds_small)
        self.wait("scalar", "po", self.po_consts)
        self.emit("scalar", lambda c: c.activation(
            self.negexpA[:, :], self.alog_sb[:, :], AF.Exp))
        na = self.inc("scalar", "ac")
        self.wait("vector", "ac", na)
        self.emit("vector", lambda c: c.tensor_scalar(
            self.negexpA[:, :], self.negexpA[:, :], -1.0, None, ALU.mult))
        self.wait("vector", "dS", self.ds_small)
        self.emit("vector", lambda c: c.tensor_tensor(
            self.bsm["bb2"][:, :], self.bsm["bbb"][:, :],
            self.bsm["bbt"][:, :], ALU.add))
        self.negA_ready = self.inc("vector", "ve")
        self.emit("sync", lambda c: c.dma_start(
            self.xf[:, :].rearrange("p (k t) -> p k t", t=NT),
            self.x_in[0, :, :, 0:NT].rearrange("k p t -> p k t")))
        self.inc("sync", "dS", 16)
        self.xver = ("dS", self.n_of("dS"))

    def ln(self):
        xsem, xn = self.xver
        pb = self.bank()
        pbq = self.bank()
        pe_kc = []
        for kc in range(KC):
            self.wait("gpsimd", xsem, xn)
            self.emit("gpsimd", lambda c, k=kc: c.tensor_copy(
                self.xb[:, k * NT:(k + 1) * NT],
                self.xf[:, k * NT:(k + 1) * NT]))
            npo = self.inc("gpsimd", "po")
            self.wait("scalar", xsem, xn)
            if kc >= 2:
                self.wait("scalar", "pe", pe_kc[kc - 2])
            self.emit("scalar", lambda c, k=kc: c.activation(
                self.xsq[k % 2], self.xf[:, k * NT:(k + 1) * NT],
                AF.Square))
            nac = self.inc("scalar", "ac")
            self.wait("tensor", "po", npo)
            self.wait("tensor", "ac", nac)
            self.emit("tensor", lambda c, k=kc: c.matmul(
                pb[0][0:1, 0:NT], self.ones_col16[:, :],
                self.xb[:, k * NT:(k + 1) * NT],
                start=(k == 0), stop=False, skip_group_check=True))
            self.emit("tensor", lambda c, k=kc: c.matmul(
                pbq[0][0:1, 0:NT], self.ones_col16[:, :],
                self.xsq[kc % 2], start=(k == 0),
                stop=(k == KC - 1), skip_group_check=True))
            pe_kc.append(self.inc("tensor", "pe"))
        pe_stats = self.n_of("pe")
        R = self.rows
        self.wait("vector", "pe", pe_stats)
        self.emit("vector", lambda c: c.tensor_scalar(
            R[:, 0:512], pb[0][0:1, 0:NT], 1.0 / D, None, ALU.mult))
        self.emit("vector", lambda c: c.tensor_scalar(
            R[:, 1024:1536], pbq[0][0:1, 0:NT], 1.0 / D, None,
            ALU.mult))
        self.emit("vector", lambda c: c.tensor_tensor(
            R[:, 512:1024], R[:, 0:512], R[:, 0:512], ALU.mult))
        self.emit("vector", lambda c: c.tensor_tensor(
            R[:, 1024:1536], R[:, 1024:1536], R[:, 512:1024],
            ALU.subtract))
        nve = self.inc("vector", "ve")
        self.bank_release(pb, "ve")
        self.bank_release(pbq, "ve")
        self.wait("scalar", "ve", nve)
        self.emit("scalar", lambda c: c.activation(
            R[:, 512:1024], R[:, 1024:1536], AF.Ln, bias=1e-5))
        self.emit("scalar", lambda c: c.activation(
            R[:, 512:1024], R[:, 512:1024], AF.Exp, scale=-0.5))
        r_ready = self.inc("scalar", "ac")
        pb2 = self.bank()
        pb3 = self.bank()
        self.wait("tensor", "ac", r_ready)
        self.emit("tensor", lambda c: c.matmul(
            pb2[0][:, 0:NT], self.ones_row32[0:1, :], R[0:1, 0:512],
            start=True, stop=True))
        self.emit("tensor", lambda c: c.matmul(
            pb3[0][:, 0:NT], self.ones_row32[0:1, :], R[0:1, 512:1024],
            start=True, stop=True))
        nrb = self.inc("tensor", "pe")
        self.wait("vector", "pe", nrb)
        self.wait("vector", "po")
        self.emit("vector", lambda c: c.tensor_copy(
            self.m_b[:, :], pb2[0][:, 0:NT]))
        self.emit("vector", lambda c: c.tensor_copy(
            self.r_b[:, :], pb3[0][:, 0:NT]))
        rb_ready = self.inc("vector", "ve")
        self.bank_release(pb2, "ve")
        self.bank_release(pb3, "ve")
        for kc in range(KC):
            engn = "vector" if kc % 2 == 0 else "gpsimd"
            self.wait(engn, "pe", pe_stats)
            if engn == "gpsimd":
                self.wait(engn, "ve", rb_ready)
            self.emit(engn, lambda c, k=kc: c.tensor_tensor(
                self.xb[:, k * NT:(k + 1) * NT],
                self.xf[:, k * NT:(k + 1) * NT], self.m_b[:, :],
                ALU.subtract))
            self.emit(engn, lambda c, k=kc: c.tensor_tensor(
                self.xb[:, k * NT:(k + 1) * NT],
                self.xb[:, k * NT:(k + 1) * NT], self.r_b[:, :], ALU.mult))
            self.inc(engn, "ve" if engn == "vector" else "po")
        self.xr_ve = self.n_of("ve")
        self.xr_po = self.n_of("po")
        self.maybe_tap("ln_xb", self.xb[:, :])
        self.maybe_tap("ln_rows", self.rows[0:1, :])

    def proj(self, n_m, wname, evac):
        for m in range(n_m):
            buf, slot, mir = self.next_group()
            pbk = self.bank()
            self.wait("tensor", "ve", self.xr_ve)
            self.wait("tensor", "po", self.xr_po)
            for k in range(KC):
                self.emit("tensor", lambda c, k_=k: c.matmul(
                    pbk[0][:, 0:NT], buf[:, k_ * 128:(k_ + 1) * 128],
                    self.xb[:, k_ * NT:(k_ + 1) * NT], start=(k_ == 0),
                    stop=(k_ == KC - 1), skip_group_check=True))
            self.inc("tensor", "pe")
            self.group_done(slot, mir)
            evac(m, pbk)

    def proj_small(self, wname, out_t, bias_name):
        pbk = self.bank()
        self.wait("tensor", "ve", self.xr_ve)
        self.wait("tensor", "po", self.xr_po)
        self.wait("tensor", "dG", self.dg_small)
        for k in range(KC):
            self.emit("tensor", lambda c, k_=k: c.matmul(
                pbk[0][0:16, 0:NT],
                self.wsm16[wname][:, k_ * 16:(k_ + 1) * 16],
                self.xb[:, k_ * NT:(k_ + 1) * NT],
                start=(k_ == 0), stop=(k_ == KC - 1),
                skip_group_check=True))
        self.inc("tensor", "pe")
        npe = self.n_of("pe")
        self.wait("vector", "pe", npe)
        self.wait("vector", "dS", self.ds_small)
        self.emit("vector", lambda c: c.tensor_scalar(
            out_t[:, :], pbk[0][0:16, 0:NT], self.bsm[bias_name][:, 0:1],
            None, ALU.add))
        self.inc("vector", "ve")
        self.bank_release(pbk, "ve")
        return self.n_of("ve")

    def tile(self, t):
        seq = t // 4
        tt = t % 4
        tok0 = tt * NT
        if tt == 0 and t > 0:
            npe = self.n_of("pe")
            self.wait("gpsimd", "pe", npe)
            self.emit("gpsimd", lambda c: c.memset(self.stA[:, :], 0.0))
            self.emit("gpsimd", lambda c: c.memset(self.stA16[:, :], 0.0))
            self.emit("gpsimd", lambda c: c.memset(self.stB[:, :], 0.0))
            self.emit("gpsimd", lambda c: c.memset(self.stB16[:, :], 0.0))
            npo = self.inc("gpsimd", "po")
            self.wait("vector", "pe", npe)
            self.emit("vector", lambda c: c.memset(self.tails[:, :], 0.0))
            self.inc("vector", "ve")
            self.st_ready_a = [("po", npo)] * HA
            self.st_ready_b = [("po", npo)] * HB
        elif t == 0:
            self.st_ready_a = [("po", self.po_consts)] * HA
            self.st_ready_b = [("po", self.po_consts)] * HB

        if self.n_sublayers >= 1:
            self.ln()
            self.attn_a(t)
        self.maybe_tap("x_after1", self.xf[:, :])
        if self.n_sublayers >= 2:
            self.ln()
            self.ffn()
        self.maybe_tap("x_after2", self.xf[:, :])
        if self.n_sublayers >= 3:
            self.ln()
            self.attn_b(t)
        if self.n_sublayers >= 4:
            self.ln()
            self.ffn()

        xsem, xn = self.xver
        self.wait("scalar", xsem, xn)
        self.emit("scalar", lambda c, s_=seq, t0=tok0: c.dma_start(
            self.x_out[s_, :, :, t0:t0 + NT].rearrange("k p t -> p k t"),
            self.xf[:, :].rearrange("p (k t) -> p k t", t=NT)))
        nst = self.inc("scalar", "dA", 16)
        if t + 1 < self.n_tiles:
            s2, t2 = (t + 1) // 4, (t + 1) % 4
            self.wait("sync", "dA", nst)
            self.emit("sync", lambda c, s_=s2, t0=t2 * NT: c.dma_start(
                self.xf[:, :].rearrange("p (k t) -> p k t", t=NT),
                self.x_in[s_, :, :, t0:t0 + NT].rearrange("k p t -> p k t")))
            self.inc("sync", "dS", 16)
            self.xver = ("dS", self.n_of("dS"))

    def attn_a(self, t):
        def mk_evac(base, bname):
            def evac(m, pbk):
                npe = self.n_of("pe")
                self.wait("vector", "pe", npe)
                self.wait("vector", "dS", self.ds_small)
                self.emit("vector", lambda c, m_=m: c.tensor_scalar(
                    self.ar(base + m_ * NT, NT), pbk[0][:, 0:NT],
                    self.bcol[bname][:, m_:m_ + 1], None, ALU.add))
                self.inc("vector", "ve")
                self.bank_release(pbk, "ve")
            return evac
        self.proj(16, "wq", mk_evac(A_QF, "bq"))
        self.proj(16, "wk", mk_evac(A_KF, "bk"))
        self.proj(16, "wv", mk_evac(A_VF, "bv"))
        qkv_ve = self.n_of("ve")
        self.maybe_tap("qf", self.ar(A_QF, 8192))
        self.maybe_tap("kf", self.ar(A_KF, 8192))
        self.maybe_tap("vf", self.ar(A_VF, 8192))
        za_ve = self.proj_small("wal", self.za, "bal")
        zb_ve = self.proj_small("wbe", self.zb, "bbe")
        self.wait("scalar", "ve", za_ve)
        self.emit("scalar", lambda c: c.activation(
            self.zt1[:, :], self.za[:, :], AF.Exp, scale=-1.0))
        n1 = self.inc("scalar", "ac")
        self.wait("vector", "ac", n1)
        self.emit("vector", lambda c: c.tensor_scalar(
            self.zt1[:, :], self.zt1[:, :], 1.0, None, ALU.add))
        self.emit("vector", lambda c: c.reciprocal(self.zt1[:, :],
                                                   self.zt1[:, :]))
        n2 = self.inc("vector", "ve")
        self.wait("scalar", "ve", n2)
        self.emit("scalar", lambda c: c.activation(
            self.loga[:, :], self.zt1[:, :], AF.Ln, bias=1e-6))
        self.emit("scalar", lambda c: c.activation(
            self.zt1[:, :], self.zb[:, :], AF.Abs))
        self.emit("scalar", lambda c: c.activation(
            self.zt1[:, :], self.zt1[:, :], AF.Exp, scale=-1.0))
        self.emit("scalar", lambda c: c.activation(
            self.zt1[:, :], self.zt1[:, :], AF.Ln, bias=1.0))
        n4 = self.inc("scalar", "ac")
        self.wait("vector", "ac", n4)
        self.wait("vector", "ve", zb_ve)
        self.emit("vector", lambda c: c.tensor_scalar(
            self.zt2[:, :], self.zb[:, :], 0.0, None, ALU.max))
        self.emit("vector", lambda c: c.tensor_tensor(
            self.zt2[:, :], self.zt2[:, :], self.zt1[:, :], ALU.add))
        n5 = self.inc("vector", "ve")
        self.wait("scalar", "ve", n5)
        self.emit("scalar", lambda c: c.activation(
            self.logb[:, :], self.zt2[:, :], AF.Ln))
        ab_ready = self.inc("scalar", "ac")
        self.maybe_tap("loga", self.loga[:, :])
        self.maybe_tap("logb", self.logb[:, :])
        for cch in range(NCH):
            self.attn_a_chunk(t, cch, qkv_ve, ab_ready)
            self.maybe_tap(f"stA_c{cch}", self.stA[:, :])
        self.maybe_tap("stA", self.stA[:, :])

    def attn_a_chunk(self, t, cch, qkv_ve, ab_ready):
        tb = self.tb
        p = self.chunk_ctr % 2
        self.chunk_ctr += 1
        coff = cch * C
        csl = lambda n: self.colt[n][:, p * 16:(p + 1) * 16]
        pbt = self.bank()
        self.wait("tensor", "ac", ab_ready)
        self.wait("tensor", "po", self.po_consts)
        self.emit("tensor", lambda c: c.transpose(
            pbt[0][0:128, 0:16], self.loga[:, coff:coff + C],
            self.ident32[0:16, 0:16]))
        self.emit("tensor", lambda c: c.transpose(
            pbt[0][0:128, 16:32], self.logb[:, coff:coff + C],
            self.ident32[0:16, 0:16]))
        ntr = self.inc("tensor", "pe")
        self.wait("vector", "pe", ntr)
        self.emit("vector", lambda c: c.tensor_copy(csl("la"),
                                                    pbt[0][:, 0:16]))
        self.emit("vector", lambda c: c.tensor_copy(csl("lb"),
                                                    pbt[0][:, 16:32]))
        nla = self.inc("vector", "ve")
        self.bank_release(pbt, "ve")
        pbc = self.bank()
        self.wait("tensor", "ve", nla)
        self.emit("tensor", lambda c: c.matmul(
            pbc[0][0:128, 0:16], self.tril32[:, :], csl("la"),
            start=True, stop=False, skip_group_check=True))
        self.emit("tensor", lambda c: c.matmul(
            pbc[0][0:128, 16:32], self.slow32[:, :], csl("la"),
            start=False, stop=True, skip_group_check=True))
        ncl = self.inc("tensor", "pe")
        self.wait("vector", "pe", ncl)
        self.emit("vector", lambda c: c.tensor_copy(csl("cla"),
                                                    pbc[0][:, 0:16]))
        self.emit("vector", lambda c: c.tensor_copy(csl("dtx"),
                                                    pbc[0][:, 16:32]))
        self.selfsync("vector")
        self.emit("vector", lambda c: c.tensor_tensor(
            csl("etd"), csl("cla"), csl("dtx"), ALU.add))
        self.emit("vector", lambda c: c.tensor_tensor(
            csl("bias"), csl("lb"), csl("cla"), ALU.subtract))
        ncl2 = self.inc("vector", "ve")
        self.bank_release(pbc, "ve")
        self.wait("scalar", "ve", ncl2)
        self.emit("scalar", lambda c: c.activation(
            csl("ecd"), csl("cla"), AF.Exp))
        self.emit("scalar", lambda c: c.activation(
            csl("edte"), csl("dtx"), AF.Exp))
        self.emit("scalar", lambda c: c.activation(
            csl("etd"), csl("etd"), AF.Exp))
        self.emit("scalar", lambda c: c.activation(
            csl("ebt"), csl("lb"), AF.Exp))
        self.selfsync("scalar")
        self.emit("scalar", lambda c: c.activation(
            csl("eb2"), csl("edte"), AF.Copy))
        prep_ac0 = self.inc("scalar", "ac")
        self.wait("vector", "ac", prep_ac0)
        self.emit("vector", lambda c: c.tensor_tensor(
            csl("eb2"), csl("eb2"), csl("ebt"), ALU.mult))
        eb2_ve = self.inc("vector", "ve")
        prep_ac = prep_ac0
        for h0 in range(0, HA, 2):
            g1 = self.attn_a_head(t, cch, h0, p, qkv_ve, nla, prep_ac,
                                  eb2_ve)
            g2 = self.attn_a_head(t, cch, h0 + 1, p, qkv_ve, nla, prep_ac,
                                  eb2_ve)
            self.drive(g1, g2)

    def attn_a_head(self, t, cch, h, p, qkv_ve, nla, prep_ac, eb2_ve):
        tb = self.tb
        hp = h % 2
        coff = cch * C
        one = lambda n: self.colt[n][:, p * 16 + h:p * 16 + h + 1]
        qsl = self.ar(A_QF + h * NT + coff, C)
        ksl = self.ar(A_KF + h * NT + coff, C)
        vsl = self.ar(A_VF + h * NT + coff, C)
        pb1 = self.bank(b16=True)
        self.wait("tensor", "ve", qkv_ve)
        self.emit("tensor", lambda c: c.transpose(
            pb1[0][:, 0:128], ksl, self.ident16[:, :]))
        self.emit("tensor", lambda c: c.transpose(
            pb1[0][:, 128:256], vsl, self.ident16[:, :]))
        ntr = self.inc("tensor", "pe")
        self.wait("vector", "pe", ntr)
        self.tbw("ktok", hp, "vector")
        yield
        self.tbw("vtok", hp, "vector")
        self.emit("vector", lambda c: c.tensor_copy(
            tb["ktok"][hp][:, :], pb1[0][:, 0:128]))
        self.emit("vector", lambda c: c.tensor_copy(
            tb["vtok"][hp][:, :], pb1[0][:, 128:256]))
        ktv = self.inc("vector", "ve")
        self.bank_release(pb1, "ve")
        yield
        self.wait("scalar", "ve", ktv)
        self.emit("scalar", lambda c: c.activation(
            tb["osq"][hp][:, :], tb["ktok"][hp][:, :], AF.Square,
            accum_out=self.col(hp, 0)))
        self.selfsync("scalar")
        self.emit("scalar", lambda c: c.activation(
            self.col(hp, 1), self.col(hp, 0), AF.Ln, bias=1e-24))
        self.selfsync("scalar")
        self.emit("scalar", lambda c: c.activation(
            self.col(hp, 2), self.col(hp, 1), AF.Exp, scale=-0.5))
        nk1 = self.inc("scalar", "ac")
        yield
        self.wait("vector", "ac", max(nk1, prep_ac))
        self.emit("vector", lambda c: c.scalar_tensor_tensor(
            self.col(hp, 3), self.col(hp, 1), -0.5,
            one("bias"), ALU.mult, ALU.add))
        nbias = self.inc("vector", "ve")
        yield
        pbr = self.bank()
        self.wait("tensor", "ve", nla)
        self.emit("tensor", lambda c: c.matmul(
            pbr[0][0:1, 0:128], self.colt["la"][:, p * 16 + h:p * 16 + h + 1],
            self.tril32[:, :], start=True, stop=True))
        nrow = self.inc("tensor", "pe")
        self.wait("vector", "pe", nrow)
        self.tbw("rw", hp, "vector")
        self.emit("vector", lambda c: c.tensor_copy(
            tb["rw"][hp][0:1, 0:128], pbr[0][0:1, 0:128]))
        nrow2 = self.inc("vector", "ve")
        self.bank_release(pbr, "ve")
        yield
        pbx = self.bank()
        self.wait("tensor", "ve", nrow2)
        self.emit("tensor", lambda c: c.matmul(
            pbx[0][:, 0:128], self.ones_row32[0:1, :],
            tb["rw"][hp][0:1, 0:128], start=True, stop=True))
        nX = self.inc("tensor", "pe")
        self.tbr("rw", hp, "pe")
        self.wait("vector", "pe", nX)
        self.tbw("X2", hp, "vector")
        self.emit("vector", lambda c: c.tensor_tensor(
            tb["X2"][hp][:, :], pbx[0][:, 0:128], self.addUC[:, :], ALU.add))
        nX2 = self.inc("vector", "ve")
        self.bank_release(pbx, "ve")
        yield
        self.wait("scalar", "ve", max(nX2, nbias))
        self.tbw("DT", hp, "scalar")
        self.emit("scalar", lambda c: c.activation(
            tb["DT"][hp][:, :], tb["X2"][hp][:, :], AF.Exp,
            bias=self.col(hp, 3)))
        nDT = self.inc("scalar", "ac")
        self.tbr("X2", hp, "ac")
        yield
        pba = self.bank()
        self.wait("tensor", "ve", qkv_ve)
        self.emit("tensor", lambda c: c.matmul(
            pba[0][:, 0:128], ksl, qsl, start=True, stop=True))
        nat = self.inc("tensor", "pe")
        self.wait("vector", "pe", nat)
        self.wait("vector", "ac", nDT)
        self.tbw("attns", hp, "vector")
        self.emit("vector", lambda c: c.tensor_tensor(
            tb["attns"][hp][:, :], pba[0][:, 0:128], tb["DT"][hp][:, :],
            ALU.mult))
        nats = self.inc("vector", "ve")
        self.tbr("DT", hp, "ve")
        yield
        self.bank_release(pba, "ve")
        pyg = self.bank()
        sra = self.st_ready_a[h]
        self.wait("tensor", sra[0], sra[1])
        self.emit("tensor", lambda c: c.matmul(
            pyg[0][:, 0:128], qsl, self.stA16[:, h * 128:(h + 1) * 128],
            start=True, stop=True))
        nyg = self.inc("tensor", "pe")
        self.wait("scalar", "pe", nyg)
        self.wait("scalar", "ac", prep_ac)
        self.tbw("yg", hp, "scalar")
        self.emit("scalar", lambda c: c.activation(
            tb["yg"][hp][:, :], pyg[0][:, 0:128], AF.Copy,
            scale=one("ecd")))
        nyg2 = self.inc("scalar", "ac")
        self.bank_release(pyg, "ac")
        yield
        pyl = self.bank()
        self.wait("tensor", "ve", nats)
        self.emit("tensor", lambda c: c.matmul(
            pyl[0][:, 0:128], tb["attns"][hp][:, :], tb["vtok"][hp][:, :],
            start=True, stop=True))
        nyl = self.inc("tensor", "pe")
        self.tbr("attns", hp, "pe")
        self.wait("vector", "pe", nyl)
        self.wait("vector", "ac", nyg2)
        self.tbw("ytok", hp, "vector")
        self.emit("vector", lambda c: c.tensor_tensor(
            tb["ytok"][hp][:, :], tb["yg"][hp][:, :], pyl[0][:, 0:128],
            ALU.add))
        nyt = self.inc("vector", "ve")
        self.tbr("yg", hp, "ve")
        yield
        self.bank_release(pyl, "ve")
        pyt = self.bank(b16=True)
        self.wait("tensor", "ve", nyt)
        self.emit("tensor", lambda c: c.transpose(
            pyt[0][:, 0:128], tb["ytok"][hp][:, :], self.ident16[:, :]))
        nyT = self.inc("tensor", "pe")
        self.tbr("ytok", hp, "pe")
        xslice = self.xf[:, h * NT + coff:h * NT + coff + C]
        self.wait("vector", "pe", nyT)
        self.emit("vector", lambda c: c.tensor_tensor(
            xslice, xslice, pyt[0][:, 0:128], ALU.add))
        nadd = self.inc("vector", "ve")
        self.bank_release(pyt, "ve")
        yield
        self.wait("vector", "ac", max(nk1, prep_ac))
        self.wait("vector", "ve", eb2_ve)
        self.tbw("kw", hp, "vector")
        self.emit("vector", lambda c: c.tensor_scalar(
            tb["kw"][hp][:, :], tb["ktok"][hp][:, :], self.col(hp, 2),
            one("eb2"), ALU.mult, ALU.mult))
        nkw = self.inc("vector", "ve")
        self.tbr("ktok", hp, "ve")
        yield
        if cch == 0 and h == 0:
            self.maybe_tap("edte_t", self.colt["edte"][:, :])
            self.maybe_tap("ebt_t", self.colt["ebt"][:, :])
            self.maybe_tap("cla_t", self.colt["cla"][:, :])
            self.maybe_tap("dtx_t", self.colt["dtx"][:, :])
            self.maybe_tap("ktok0", tb["ktok"][hp][:, :])
            self.maybe_tap("vtok0", tb["vtok"][hp][:, :])
            self.maybe_tap("kw0", tb["kw"][hp][:, :])
            self.maybe_tap("col5", self.col(hp, 5))
            self.maybe_tap("col2", self.col(hp, 2))
        pdel = self.bank()
        self.wait("tensor", "ve", nkw)
        self.emit("tensor", lambda c: c.matmul(
            pdel[0][:, 0:128], tb["kw"][hp][:, :], tb["vtok"][hp][:, :],
            start=True, stop=True))
        ndel = self.inc("tensor", "pe")
        self.tbr("kw", hp, "pe")
        self.tbr("vtok", hp, "pe")
        stsl = self.stA[:, h * 128:(h + 1) * 128]
        st16 = self.stA16[:, h * 128:(h + 1) * 128]
        self.wait("vector", "pe", ndel)
        self.wait("vector", "ac", prep_ac)
        self.emit("vector", lambda c: c.tensor_scalar(
            stsl, stsl, one("etd"), None, ALU.mult))
        self.emit("vector", lambda c: c.tensor_tensor(
            stsl, stsl, pdel[0][:, 0:128], ALU.add))
        nst = self.inc("vector", "ve")
        self.bank_release(pdel, "ve")
        yield
        self.wait("scalar", "ve", nst)
        self.emit("scalar", lambda c: c.copy(st16, stsl))
        ncast = self.inc("scalar", "ac")
        self.st_ready_a[h] = ("ac", ncast)
        if cch == NCH - 1 and h == HA - 1:
            self.xver = ("ve", nadd)

    def ffn(self):
        tb = self.tb
        sg_used = [0, 0]
        H_ve = []
        for hc in range(64):
            bufg, slg, mirg = self.next_group()
            pg = self.bank()
            self.wait("tensor", "ve", self.xr_ve)
            self.wait("tensor", "po", self.xr_po)
            for k in range(KC):
                self.emit("tensor", lambda c, k_=k: c.matmul(
                    pg[0][:, 0:NT], bufg[:, k_ * 128:(k_ + 1) * 128],
                    self.xb[:, k_ * NT:(k_ + 1) * NT], start=(k_ == 0),
                    stop=(k_ == KC - 1), skip_group_check=True))
            ng = self.inc("tensor", "pe")
            self.group_done(slg, mirg)
            bufu, slu, miru = self.next_group()
            pu = self.bank()
            for k in range(KC):
                self.emit("tensor", lambda c, k_=k: c.matmul(
                    pu[0][:, 0:NT], bufu[:, k_ * 128:(k_ + 1) * 128],
                    self.xb[:, k_ * NT:(k_ + 1) * NT], start=(k_ == 0),
                    stop=(k_ == KC - 1), skip_group_check=True))
            nu = self.inc("tensor", "pe")
            self.group_done(slu, miru)
            sp = hc % 2
            self.wait("scalar", "pe", ng)
            if sg_used[sp]:
                self.wait("scalar", "ve", sg_used[sp])
            self.emit("scalar", lambda c, s_=sp: c.activation(
                tb["sg"][s_][:, :], pg[0][:, 0:NT], AF.Silu))
            nsg = self.inc("scalar", "ac")
            self.bank_release(pg, "ac")
            self.wait("vector", "pe", nu)
            self.wait("vector", "ac", nsg)
            self.emit("vector", lambda c, m_=hc, s_=sp: c.tensor_tensor(
                self.ar(A_H + m_ * NT, NT), tb["sg"][s_][:, :],
                pu[0][:, 0:NT], ALU.mult))
            nh = self.inc("vector", "ve")
            self.bank_release(pu, "ve")
            sg_used[sp] = nh
            H_ve.append(nh)
        last_add = None
        for m in range(KC):
            pd = self.bank()
            for quarter in range(4):
                buf, slot, mir = self.next_group()
                self.wait("tensor", "ve", H_ve[quarter * 16 + 15])
                for j in range(16):
                    k = quarter * 16 + j
                    self.emit("tensor", lambda c, j_=j, k_=k: c.matmul(
                        pd[0][:, 0:NT], buf[:, j_ * 128:(j_ + 1) * 128],
                        self.ar(A_H + k_ * NT, NT), start=(k_ == 0),
                        stop=(k_ == 63), skip_group_check=True))
                self.inc("tensor", "pe")
                self.group_done(slot, mir)
            nd = self.n_of("pe")
            xsl = self.xf[:, m * NT:(m + 1) * NT]
            self.wait("vector", "pe", nd)
            self.emit("vector", lambda c, x_=xsl: c.tensor_tensor(
                x_, x_, pd[0][:, 0:NT], ALU.add))
            last_add = self.inc("vector", "ve")
            self.bank_release(pd, "ve")
        self.xver = ("ve", last_add)

    def attn_b(self, t):
        tb = self.tb
        stage_rel = [None, None, None]
        scr_rel = [0, 0]
        self.stg_i = 0

        def conv_one(m, pbk, which):
            cw, obase, ch, engn = which
            sl = self.stg_i % 3
            sc = self.stg_i % 2
            self.stg_i += 1
            npe = self.n_of("pe")
            self.wait("vector", "pe", npe)
            if stage_rel[sl] is not None:
                self.wait("vector", stage_rel[sl][0], stage_rel[sl][1])
            self.emit("vector", lambda c: c.tensor_copy(
                self.stage[sl][:, 3:515], pbk[0][:, 0:NT]))
            nev = self.inc("vector", "ve")
            self.bank_release(pbk, "ve")
            toff = {"cq": 0, "ck": 24, "cv": 48}[cw] + ch * 3
            self.wait("vector", "dS", self.ds_small)
            self.emit("vector", lambda c: c.tensor_copy(
                self.stage[sl][:, 0:3], self.tails[:, toff:toff + 3]))
            self.selfsync("vector")
            if scr_rel[sc]:
                self.wait("vector", "ac", scr_rel[sc])
            scr = tb["cvt"][sc]
            wc = lambda j: self.cvw[cw][:, ch * 4 + j:ch * 4 + j + 1]
            self.emit("vector", lambda c: c.tensor_scalar(
                scr[:, :], self.stage[sl][:, 0:512], wc(0), None, ALU.mult))
            for j in (1, 2, 3):
                self.emit("vector", lambda c, j_=j: c.scalar_tensor_tensor(
                    scr[:, :], self.stage[sl][:, j_:j_ + 512], wc(j_),
                    scr[:, :], ALU.mult, ALU.add))
            self.emit("vector", lambda c: c.tensor_copy(
                self.tails[:, toff:toff + 3], self.stage[sl][:, 512:515]))
            ncv = self.inc("vector", "ve")
            stage_rel[sl] = ("ve", ncv)
            self.wait("scalar", "ve", ncv)
            self.emit("scalar", lambda c, ob=obase, ch_=ch: c.activation(
                self.ar(ob + ch_ * 512, 512), scr[:, :], AF.Silu))
            scr_rel[sc] = self.inc("scalar", "ac")

        def evac_qk(m, pbk):
            if m < 8:
                conv_one(m, pbk, ("cq", A_QC, m, "vector"))
            else:
                conv_one(m, pbk, ("ck", A_KC, m - 8, "vector"))

        def evac_vg(m, pbk):
            if m < 16:
                conv_one(m, pbk, ("cv", A_VC, m, "vector"))
            else:
                npe = self.n_of("pe")
                self.wait("vector", "pe", npe)
                self.emit("vector", lambda c, m_=m: c.tensor_copy(
                    self.ar(A_GOUT + (m_ - 16) * NT, NT), pbk[0][:, 0:NT]))
                self.inc("vector", "ve")
                self.bank_release(pbk, "ve")

        self.proj(16, "wqk", evac_qk)
        self.proj(32, "wvg", evac_vg)
        conv_ac = self.n_of("ac")
        self.maybe_tap("qc", self.ar(A_QC, 4096))
        self.maybe_tap("kc", self.ar(A_KC, 4096))
        self.maybe_tap("vc", self.ar(A_VC, 8192))
        self.maybe_tap("gout", self.ar(A_GOUT, 8192))
        zb_ve = self.proj_small("wb", self.zb, "bb2")
        za_ve = self.proj_small("wa", self.za, "bdt")
        self.wait("scalar", "ve", za_ve)
        self.emit("scalar", lambda c: c.activation(
            self.zt1[:, :], self.za[:, :], AF.Abs))
        self.emit("scalar", lambda c: c.activation(
            self.zt1[:, :], self.zt1[:, :], AF.Exp, scale=-1.0))
        self.emit("scalar", lambda c: c.activation(
            self.zt1[:, :], self.zt1[:, :], AF.Ln, bias=1.0))
        n1 = self.inc("scalar", "ac")
        self.wait("vector", "ac", n1)
        self.wait("vector", "ve", self.negA_ready)
        self.emit("vector", lambda c: c.tensor_scalar(
            self.zt2[:, :], self.za[:, :], 0.0, None, ALU.max))
        self.emit("vector", lambda c: c.tensor_tensor(
            self.zt2[:, :], self.zt2[:, :], self.zt1[:, :], ALU.add))
        self.emit("vector", lambda c: c.tensor_scalar(
            self.gbuf[:, :], self.zt2[:, :], self.negexpA[0:16, 0:1], None,
            ALU.mult))
        g_ve = self.inc("vector", "ve")
        self.wait("scalar", "ve", g_ve)
        self.emit("scalar", lambda c: c.activation(
            self.zt1[:, :], self.zb[:, :], AF.Abs))
        self.emit("scalar", lambda c: c.activation(
            self.zt1[:, :], self.zt1[:, :], AF.Exp, scale=-1.0))
        self.emit("scalar", lambda c: c.activation(
            self.zt1[:, :], self.zt1[:, :], AF.Ln, bias=1.0))
        n2 = self.inc("scalar", "ac")
        self.wait("vector", "ac", n2)
        self.emit("vector", lambda c: c.tensor_scalar(
            self.zt2[:, :], self.zb[:, :], -1.0, 0.0, ALU.mult, ALU.max))
        self.emit("vector", lambda c: c.tensor_tensor(
            self.zt2[:, :], self.zt2[:, :], self.zt1[:, :], ALU.add))
        self.emit("vector", lambda c: c.tensor_scalar(
            self.logb[:, :], self.zt2[:, :], -1.0, float(np.log(2.0)),
            ALU.mult, ALU.add))
        logb_ve = self.inc("vector", "ve")
        self.maybe_tap("gbuf", self.gbuf[:, :])
        self.maybe_tap("logbB", self.logb[:, :])
        for cch in range(NCH):
            self.attn_b_chunk(t, cch, conv_ac, g_ve, logb_ve)
            self.maybe_tap(f"stB_c{cch}", self.stB[:, :])
        self.maybe_tap("of", self.ar(A_OF, 8192))
        self.attn_b_out()

    def attn_b_chunk(self, t, cch, conv_ac, g_ve, logb_ve):
        tb = self.tb
        p = self.chunk_ctr % 2
        self.chunk_ctr += 1
        coff = cch * C
        csl = lambda n: self.colt[n][:, p * 16:(p + 1) * 16]
        pbt = self.bank()
        self.wait("tensor", "ve", max(g_ve, logb_ve))
        self.wait("tensor", "po", self.po_consts)
        self.emit("tensor", lambda c: c.transpose(
            pbt[0][0:128, 0:16], self.gbuf[:, coff:coff + C],
            self.ident32[0:16, 0:16]))
        self.emit("tensor", lambda c: c.transpose(
            pbt[0][0:128, 16:32], self.logb[:, coff:coff + C],
            self.ident32[0:16, 0:16]))
        ntr = self.inc("tensor", "pe")
        self.wait("vector", "pe", ntr)
        self.emit("vector", lambda c: c.tensor_copy(csl("g"),
                                                    pbt[0][:, 0:16]))
        self.emit("vector", lambda c: c.tensor_copy(csl("lb"),
                                                    pbt[0][:, 16:32]))
        ng = self.inc("vector", "ve")
        self.bank_release(pbt, "ve")
        pbc = self.bank()
        self.wait("tensor", "ve", ng)
        self.emit("tensor", lambda c: c.matmul(
            pbc[0][0:128, 0:16], self.tril32[:, :], csl("g"),
            start=True, stop=False, skip_group_check=True))
        self.emit("tensor", lambda c: c.matmul(
            pbc[0][0:128, 16:32], self.slow32[:, :], csl("g"),
            start=False, stop=True, skip_group_check=True))
        ncl = self.inc("tensor", "pe")
        self.wait("vector", "pe", ncl)
        self.emit("vector", lambda c: c.tensor_copy(csl("G"),
                                                    pbc[0][:, 0:16]))
        self.emit("vector", lambda c: c.tensor_copy(csl("dG"),
                                                    pbc[0][:, 16:32]))
        self.selfsync("vector")
        self.emit("vector", lambda c: c.tensor_tensor(
            csl("etdB"), csl("G"), csl("dG"), ALU.add))
        self.emit("vector", lambda c: c.tensor_tensor(
            csl("Glb"), csl("G"), csl("lb"), ALU.add))
        nG = self.inc("vector", "ve")
        self.bank_release(pbc, "ve")
        self.wait("scalar", "ve", nG)
        self.emit("scalar", lambda c: c.activation(
            csl("eG"), csl("G"), AF.Exp))
        self.emit("scalar", lambda c: c.activation(
            csl("edG"), csl("dG"), AF.Exp))
        self.emit("scalar", lambda c: c.activation(
            csl("etdB"), csl("etdB"), AF.Exp))
        self.emit("scalar", lambda c: c.activation(
            csl("ebt"), csl("lb"), AF.Exp))
        prep_ac = self.inc("scalar", "ac")
        if self.serialize >= 1:
            self.barrier()
        self.wait("vector", "ac", prep_ac)
        self.emit("vector", lambda c: c.tensor_tensor(
            csl("nbG"), csl("ebt"), csl("eG"), ALU.mult))
        self.selfsync("vector")
        self.emit("vector", lambda c: c.tensor_scalar(
            csl("nbG"), csl("nbG"), -1.0, None, ALU.mult))
        nbG_ve = self.inc("vector", "ve")
        for h0 in range(0, HB, 2):
            g1 = self.attn_b_head(t, cch, h0, p, conv_ac, prep_ac, nG,
                                  nbG_ve)
            g2 = self.attn_b_head(t, cch, h0 + 1, p, conv_ac, prep_ac, nG,
                                  nbG_ve)
            self.drive(g1, g2)

    def attn_b_head(self, t, cch, h, p, conv_ac, prep_ac, nG, nbG_ve):
        tb = self.tb
        hp = h % 2
        coff = cch * C
        one = lambda n: self.colt[n][:, p * 16 + h:p * 16 + h + 1]
        pbase = 64 * (h % 2)
        qsl = self.arena[pbase:pbase + 64,
                         A_QC + (h // 2) * NT + coff:
                         A_QC + (h // 2) * NT + coff + C]
        ksl = self.arena[pbase:pbase + 64,
                         A_KC + (h // 2) * NT + coff:
                         A_KC + (h // 2) * NT + coff + C]
        vsl = self.ar(A_VC + h * NT + coff, C)
        stB_sl = self.stB[pbase:pbase + 64,
                          (h // 2) * 128:(h // 2 + 1) * 128]
        stB16_sl = self.stB16[pbase:pbase + 64,
                              (h // 2) * 128:(h // 2 + 1) * 128]
        idsl = self.ident16[pbase:pbase + 64, pbase:pbase + 64]
        pkt = self.bank(b16=True)
        self.wait("tensor", "ac", conv_ac)
        self.emit("tensor", lambda c: c.transpose(
            pkt[0][:, 0:64], ksl, idsl))
        nkt = self.inc("tensor", "pe")
        self.wait("vector", "pe", nkt)
        self.tbw("ktb", hp, "vector")
        self.emit("vector", lambda c: c.tensor_copy(
            tb["ktb"][hp][:, 0:64], pkt[0][:, 0:64]))
        nktv = self.inc("vector", "ve")
        self.bank_release(pkt, "ve")
        yield
        self.wait("scalar", "ve", nktv)
        self.emit("scalar", lambda c: c.activation(
            tb["kq2"][hp][:, 0:64], tb["ktb"][hp][:, 0:64], AF.Square,
            accum_out=self.col(hp, 0)))
        self.selfsync("scalar")
        self.emit("scalar", lambda c: c.activation(
            self.col(hp, 1), self.col(hp, 0), AF.Ln, bias=1e-6))
        self.selfsync("scalar")
        self.emit("scalar", lambda c: c.activation(
            self.col(hp, 2), self.col(hp, 1), AF.Exp, scale=-0.5))
        nkin = self.inc("scalar", "ac")
        self.tbw("kq2", hp, "scalar")
        yield
        self.emit("scalar", lambda c: c.activation(
            tb["kq2"][hp][pbase:pbase + 64, 0:128], ksl, AF.Square))
        nksq = self.inc("scalar", "ac")
        yield
        prow = self.bank()
        self.wait("tensor", "ve", nG)
        self.emit("tensor", lambda c: c.matmul(
            prow[0][0:1, 0:128], self.colt["g"][:, p * 16 + h:p * 16 + h + 1],
            self.tril32[:, :], start=True, stop=True))
        self.wait("tensor", "ac", nksq)
        self.emit("tensor", lambda c: c.matmul(
            prow[0][0:1, 128:256], self.ones_col16[pbase:pbase + 64, 0:1],
            tb["kq2"][hp][pbase:pbase + 64, 0:128], start=True, stop=True,
            skip_group_check=True))
        nrow = self.inc("tensor", "pe")
        self.tbr("kq2", hp, "pe")
        self.wait("vector", "pe", nrow)
        self.tbw("rw", hp, "vector")
        self.emit("vector", lambda c: c.tensor_copy(
            tb["rw"][hp][0:1, 0:128], prow[0][0:1, 0:128]))
        self.emit("vector", lambda c: c.tensor_copy(
            tb["rw"][hp][0:1, 128:256], prow[0][0:1, 128:256]))
        nrows = self.inc("vector", "ve")
        self.bank_release(prow, "ve")
        yield
        self.wait("scalar", "ve", nrows)
        self.emit("scalar", lambda c: c.activation(
            tb["rw"][hp][0:1, 128:256], tb["rw"][hp][0:1, 128:256], AF.Ln,
            bias=1e-6))
        nlnk = self.inc("scalar", "ac")
        self.wait("vector", "ac", nlnk)
        self.emit("vector", lambda c: c.scalar_tensor_tensor(
            tb["rw"][hp][0:1, 128:256], tb["rw"][hp][0:1, 128:256], -0.5,
            tb["rw"][hp][0:1, 0:128], ALU.mult, ALU.subtract))
        nMrow = self.inc("vector", "ve")
        yield
        pkk = self.bank()
        self.wait("tensor", "ac", conv_ac)
        self.emit("tensor", lambda c: c.matmul(
            pkk[0][:, 0:128], ksl, ksl, start=True, stop=True,
            skip_group_check=True))
        self.emit("tensor", lambda c: c.matmul(
            pkk[0][:, 128:256], ksl, qsl, start=True, stop=True,
            skip_group_check=True))
        nkk = self.inc("tensor", "pe")
        yield
        self.wait("vector", "ac", max(nkin, prep_ac))
        self.emit("vector", lambda c: c.scalar_tensor_tensor(
            self.col(hp, 3), self.col(hp, 1), -0.5,
            one("Glb"), ALU.mult, ALU.add))
        nMb = self.inc("vector", "ve")
        yield
        pX = self.bank()
        self.wait("tensor", "ve", nMrow)
        self.emit("tensor", lambda c: c.matmul(
            pX[0][:, 0:128], self.ones_row32[0:1, :],
            tb["rw"][hp][0:1, 128:256], start=True, stop=True))
        nX = self.inc("tensor", "pe")
        self.wait("vector", "pe", nX)
        self.tbw("X2", hp, "vector")
        self.emit("vector", lambda c: c.tensor_tensor(
            tb["X2"][hp][:, :], pX[0][:, 0:128], self.addSL[:, :], ALU.add))
        nX2 = self.inc("vector", "ve")
        self.bank_release(pX, "ve")
        self.wait("scalar", "ve", max(nX2, nMb))
        self.tbw("DT", hp, "scalar")
        self.emit("scalar", lambda c: c.activation(
            tb["DT"][hp][:, :], tb["X2"][hp][:, :], AF.Exp,
            bias=self.col(hp, 3)))
        nMexp = self.inc("scalar", "ac")
        self.tbr("X2", hp, "ac")
        yield
        self.wait("vector", "pe", nkk)
        self.wait("vector", "ac", nMexp)
        self.tbw("Mb", hp, "vector")
        self.emit("vector", lambda c: c.tensor_tensor(
            tb["Mb"][hp][:, :], pkk[0][:, 0:128], tb["DT"][hp][:, :],
            ALU.mult))
        nM = self.inc("vector", "ve")
        yield
        self.wait("vector", "ac", max(nkin, prep_ac))
        self.selfsync("vector")
        self.emit("vector", lambda c: c.scalar_tensor_tensor(
            self.col(hp, 4), self.col(hp, 1), -0.5,
            one("G"), ALU.mult, ALU.subtract))
        nAb = self.inc("vector", "ve")
        yield
        pXa = self.bank()
        self.wait("tensor", "ve", nrows)
        self.emit("tensor", lambda c: c.matmul(
            pXa[0][:, 0:128], self.ones_row32[0:1, :],
            tb["rw"][hp][0:1, 0:128], start=True, stop=True))
        nXa = self.inc("tensor", "pe")
        self.tbr("rw", hp, "pe")
        self.wait("vector", "pe", nXa)
        self.tbw("X2", hp, "vector")
        self.emit("vector", lambda c: c.tensor_tensor(
            tb["X2"][hp][:, :], pXa[0][:, 0:128], self.addUC[:, :], ALU.add))
        nXa2 = self.inc("vector", "ve")
        self.bank_release(pXa, "ve")
        self.wait("scalar", "ve", max(nXa2, nAb))
        self.tbw("DT", hp, "scalar")
        self.emit("scalar", lambda c: c.activation(
            tb["DT"][hp][:, :], tb["X2"][hp][:, :], AF.Exp,
            bias=self.col(hp, 4)))
        nAexp = self.inc("scalar", "ac")
        self.tbr("X2", hp, "ac")
        yield
        self.wait("vector", "ac", nAexp)
        self.tbw("Ab", hp, "vector")
        self.emit("vector", lambda c: c.tensor_tensor(
            tb["Ab"][hp][:, :], pkk[0][:, 128:256], tb["DT"][hp][:, :],
            ALU.mult))
        nAbar = self.inc("vector", "ve")
        self.tbr("DT", hp, "ve")
        yield
        self.bank_release(pkk, "ve")
        pmt = self.bank(b16=True)
        self.wait("tensor", "ve", nM)
        self.emit("tensor", lambda c: c.transpose(
            pmt[0][:, 0:128], tb["Mb"][hp][:, :], self.ident16[:, :]))
        nmt = self.inc("tensor", "pe")
        self.wait("vector", "pe", nmt)
        self.tbw("Mpr", hp, "vector")
        self.tbw("Mng", hp, "vector")
        self.emit("vector", lambda c: c.tensor_copy(
            tb["Mpr"][hp][:, :], pmt[0][:, 0:128]))
        self.emit("vector", lambda c: c.tensor_scalar(
            tb["Mng"][hp][:, :], pmt[0][:, 0:128], -1.0, None, ALU.mult))
        nmpr = self.inc("vector", "ve")
        self.bank_release(pmt, "ve")
        yield
        pks = self.bank()
        srb = self.st_ready_b[h]
        self.wait("tensor", srb[0], srb[1])
        self.wait("tensor", "ac", conv_ac)
        self.emit("tensor", lambda c: c.matmul(
            pks[0][:, 0:128], ksl, stB16_sl, start=True, stop=True))
        nks = self.inc("tensor", "pe")
        self.wait("vector", "pe", nks)
        self.tbw("KS", hp, "vector")
        self.emit("vector", lambda c: c.tensor_copy(
            tb["KS"][hp][:, :], pks[0][:, 0:128]))
        nksv = self.inc("vector", "ve")
        self.bank_release(pks, "ve")
        yield
        pvt = self.bank(b16=True)
        self.wait("tensor", "ac", conv_ac)
        self.emit("tensor", lambda c: c.transpose(
            pvt[0][:, 0:128], vsl, self.ident16[:, :]))
        nvt = self.inc("tensor", "pe")
        self.wait("vector", "pe", nvt)
        self.tbw("vtok", hp, "vector")
        self.emit("vector", lambda c: c.tensor_copy(
            tb["vtok"][hp][:, :], pvt[0][:, 0:128]))
        nvtv = self.inc("vector", "ve")
        self.bank_release(pvt, "ve")
        yield
        self.wait("vector", "ac", max(prep_ac, nkin))
        self.wait("vector", "ve", nbG_ve)
        self.wait("vector", "po", self.po_consts)
        self.tbw("dgb", hp, "vector")
        self.tbw("dgb2", hp, "vector")
        self.emit("vector", lambda c: c.tensor_scalar(
            tb["dgb"][hp][:, :], self.ident16[:, :], one("ebt"), None,
            ALU.mult))
        self.emit("vector", lambda c: c.tensor_scalar(
            tb["dgb2"][hp][:, :], self.ident16[:, :], one("nbG"),
            self.col(hp, 2), ALU.mult, ALU.mult))
        ndg = self.inc("vector", "ve")
        yield
        pR = self.solve_bank[hp]
        if pR[1] is not None:
            self.wait("tensor", pR[1], pR[2])
        self.wait("tensor", "ve", max(ndg, nvtv, nksv))
        self.emit("tensor", lambda c: c.matmul(
            pR[0][:, 0:128], tb["dgb"][hp][:, :], tb["vtok"][hp][:, :],
            start=True, stop=False, skip_group_check=True))
        self.emit("tensor", lambda c: c.matmul(
            pR[0][:, 0:128], tb["dgb2"][hp][:, :], tb["KS"][hp][:, :],
            start=False, stop=True, skip_group_check=True))
        nR = self.inc("tensor", "pe")
        self.tbr("KS", hp, "pe")
        self.tbr("vtok", hp, "pe")
        self.tbr("dgb", hp, "pe")
        self.tbr("dgb2", hp, "pe")
        self.wait("vector", "pe", nR)
        self.tbw("tj", hp, "vector")
        self.emit("vector", lambda c: c.tensor_copy(
            tb["tj"][hp][:, :], pR[0][:, 0:128]))
        ntj = self.inc("vector", "ve")
        yield
        prime_list = [tb["Mng"][hp]]
        curP, curPT = tb["Mb"][hp], tb["Mpr"][hp]
        sq_ve = nmpr
        bufs = [(tb["pwA"][hp], tb["pwAT"][hp]),
                (tb["pwB"][hp], tb["pwBT"][hp])]
        for lvl in range(1, 7):
            dst, dstT = bufs[lvl % 2]
            psq = self.bank()
            self.wait("tensor", "ve", sq_ve)
            if lvl < 6:
                self.emit("tensor", lambda c, a=curPT, b=curP: c.matmul(
                    psq[0][:, 0:128], a[:, 0:128], b[:, 0:128],
                    start=True, stop=False, skip_group_check=True))
                self.emit("tensor", lambda c, a=curP, b=curPT: c.matmul(
                    psq[0][:, 128:256], a[:, 0:128], b[:, 0:128],
                    start=True, stop=True, skip_group_check=True))
            else:
                self.emit("tensor", lambda c, a=curP, b=curPT: c.matmul(
                    psq[0][:, 0:128], a[:, 0:128], b[:, 0:128],
                    start=True, stop=True, skip_group_check=True))
            nsq = self.inc("tensor", "pe")
            self.wait("vector", "pe", nsq)
            dnames = {id(self.tb["pwA"][hp]): ("pwA", hp),
                      id(self.tb["pwAT"][hp]): ("pwAT", hp),
                      id(self.tb["pwB"][hp]): ("pwB", hp),
                      id(self.tb["pwBT"][hp]): ("pwBT", hp)}
            if lvl < 6:
                self.tbw(*dnames[id(dst)][:1], hp, "vector") if False else None
                self.tbw(dnames[id(dst)][0], hp, "vector")
                self.tbw(dnames[id(dstT)][0], hp, "vector")
                self.emit("vector", lambda c, d=dst: c.tensor_copy(
                    d[:, 0:128], psq[0][:, 0:128]))
                self.emit("vector", lambda c, d=dstT: c.tensor_copy(
                    d[:, 0:128], psq[0][:, 128:256]))
            else:
                self.tbw(dnames[id(dstT)][0], hp, "vector")
                self.emit("vector", lambda c, d=dstT: c.tensor_copy(
                    d[:, 0:128], psq[0][:, 0:128]))
            sq_ve = self.inc("vector", "ve")
            self.bank_release(psq, "ve")
            prime_list.append(dstT)
            curP, curPT = dst, dstT
            yield
        app_ve = ntj
        for lvl in range(6, 0, -1):
            Pp = prime_list[lvl]
            self.wait("tensor", "ve", app_ve)
            self.emit("tensor", lambda c, a=Pp: c.matmul(
                pR[0][:, 0:128], a[:, 0:128], tb["tj"][hp][:, :],
                start=False, stop=False, skip_group_check=True))
            napp = self.inc("tensor", "pe")
            self.wait("vector", "pe", napp)
            self.emit("vector", lambda c: c.tensor_copy(
                tb["tj"][hp][:, :], pR[0][:, 0:128]))
            app_ve = self.inc("vector", "ve")
            yield
        self.wait("tensor", "ve", app_ve)
        self.emit("tensor", lambda c: c.matmul(
            pR[0][:, 0:128], tb["Mng"][hp][:, :], tb["tj"][hp][:, :],
            start=False, stop=True, skip_group_check=True))
        nU = self.inc("tensor", "pe")
        for nm in ("Mb", "Mpr", "Mng", "pwA", "pwAT", "pwB", "pwBT", "tj"):
            self.tbr(nm, hp, "pe")
        self.wait("vector", "pe", nU)
        self.tbw("U16", hp, "vector")
        self.emit("vector", lambda c: c.tensor_copy(
            tb["U16"][hp][:, :], pR[0][:, 0:128]))
        nU16 = self.inc("vector", "ve")
        yield
        self.bank_release(pR, "ve")
        if cch == 1 and h == 0:
            self.maybe_tap("c1U", tb["U16"][hp][:, :])
            self.maybe_tap("c1KS", tb["KS"][hp][:, :])
        if cch == 0 and h == 0:
            self.maybe_tap("bU16", tb["U16"][hp][:, :])
            self.maybe_tap("bMb", tb["Mb"][hp][:, :])
            self.maybe_tap("bMpr", tb["Mpr"][hp][:, :])
            self.maybe_tap("bAb", tb["Ab"][hp][:, :])
            self.maybe_tap("bKS", tb["KS"][hp][:, :])
            self.maybe_tap("bktb", tb["ktb"][hp][:, :])
            self.maybe_tap("bvtok", tb["vtok"][hp][:, :])
            self.maybe_tap("bdgb", tb["dgb"][hp][:, :])
            self.maybe_tap("bdgb2", tb["dgb2"][hp][:, :])
        pO = self.bank()
        self.wait("tensor", srb[0], srb[1])
        self.emit("tensor", lambda c: c.matmul(
            pO[0][:, 0:128], qsl, stB16_sl, start=True, stop=True))
        nqs = self.inc("tensor", "pe")
        self.wait("scalar", "pe", nqs)
        self.wait("scalar", "ac", prep_ac)
        self.tbw("o1", hp, "scalar")
        self.emit("scalar", lambda c: c.activation(
            tb["o1"][hp][:, :], pO[0][:, 0:128], AF.Copy, scale=one("eG")))
        no1 = self.inc("scalar", "ac")
        self.bank_release(pO, "ac")
        yield
        pO2 = self.bank()
        self.wait("tensor", "ve", max(nAbar, nU16))
        self.emit("tensor", lambda c: c.matmul(
            pO2[0][:, 0:128], tb["Ab"][hp][:, :], tb["U16"][hp][:, :],
            start=True, stop=True))
        nO2 = self.inc("tensor", "pe")
        self.tbr("Ab", hp, "pe")
        self.wait("scalar", "ac", nksq)
        self.tbw("osq", hp, "scalar")
        self.emit("scalar", lambda c: c.activation(
            tb["osq"][hp][pbase:pbase + 64, 0:128], qsl, AF.Square))
        nqsq = self.inc("scalar", "ac")
        peps = self.bank()
        self.wait("tensor", "ac", nqsq)
        self.emit("tensor", lambda c: c.matmul(
            peps[0][:, 0:1], tb["osq"][hp][pbase:pbase + 64, 0:128],
            self.ones_col16[pbase:pbase + 64, 0:1], start=True, stop=True))
        neps = self.inc("tensor", "pe")
        self.tbr("osq", hp, "pe")
        self.wait("vector", "pe", neps)
        self.wait("vector", "ac", nAexp)
        self.emit("vector", lambda c: c.tensor_scalar(
            self.col(hp, 3), peps[0][:, 0:1], EPS_RMS * DK,
            EPS_RMS * DK * 1e-6, ALU.mult, ALU.add))
        nepsv = self.inc("vector", "ve")
        self.bank_release(peps, "ve")
        yield
        self.wait("vector", "pe", nO2)
        self.wait("vector", "ac", no1)
        self.emit("vector", lambda c: c.tensor_tensor(
            tb["o1"][hp][:, :], tb["o1"][hp][:, :], pO2[0][:, 0:128],
            ALU.add))
        nOf = self.inc("vector", "ve")
        yield
        self.bank_release(pO2, "ve")
        if cch == 1 and h == 0:
            self.maybe_tap("c1o1", tb["o1"][hp][:, :])
        if cch == 0 and h == 0:
            self.maybe_tap("bo1", tb["o1"][hp][:, :])
        self.wait("scalar", "ve", nOf)
        self.tbw("osq", hp, "scalar")
        self.emit("scalar", lambda c: c.activation(
            tb["osq"][hp][:, :], tb["o1"][hp][:, :], AF.Square,
            accum_out=self.col(hp, 0)))
        self.selfsync("scalar")
        self.wait("scalar", "ve", nepsv)
        self.emit("scalar", lambda c: c.activation(
            self.col(hp, 1), self.col(hp, 0), AF.Ln,
            bias=self.col(hp, 3), scale=1.0 / DV))
        self.selfsync("scalar")
        self.emit("scalar", lambda c: c.activation(
            self.col(hp, 5), self.col(hp, 1), AF.Exp, scale=-0.5))
        nrms = self.inc("scalar", "ac")
        yield
        self.wait("vector", "ac", nrms)
        self.tbw("on", hp, "vector")
        self.emit("vector", lambda c: c.tensor_scalar(
            tb["on"][hp][:, :], tb["o1"][hp][:, :], self.col(hp, 5),
            None, ALU.mult))
        non = self.inc("vector", "ve")
        self.tbr("o1", hp, "ve")
        yield
        if cch == 0 and h == 0:
            self.maybe_tap("bon", tb["on"][hp][:, :])
        pot = self.bank(b16=True)
        self.wait("tensor", "ve", non)
        self.emit("tensor", lambda c: c.transpose(
            pot[0][:, 0:128], tb["on"][hp][:, :], self.ident16[:, :]))
        noT = self.inc("tensor", "pe")
        self.tbr("on", hp, "pe")
        self.wait("vector", "pe", noT)
        self.emit("vector", lambda c: c.tensor_copy(
            self.ar(A_OF + h * NT + coff, C), pot[0][:, 0:128]))
        self.inc("vector", "ve")
        self.bank_release(pot, "ve")
        self.wait("vector", "ac", max(nkin, prep_ac))
        self.tbw("ktl", hp, "vector")
        self.emit("vector", lambda c: c.tensor_scalar(
            tb["ktl"][hp][:, 0:64], tb["ktb"][hp][:, 0:64],
            self.col(hp, 2), one("edG"), ALU.mult, ALU.mult))
        nktl = self.inc("vector", "ve")
        self.tbr("ktb", hp, "ve")
        yield
        pdS = self.bank()
        self.wait("tensor", "ve", max(nktl, nU16))
        self.emit("tensor", lambda c: c.matmul(
            pdS[0][pbase:pbase + 64, 0:128], tb["ktl"][hp][:, 0:64],
            tb["U16"][hp][:, :], start=True, stop=True,
            tile_position=(0, pbase)))
        ndS = self.inc("tensor", "pe")
        self.tbr("ktl", hp, "pe")
        self.tbr("U16", hp, "pe")
        self.wait("vector", "pe", ndS)
        self.wait("vector", "ac", prep_ac)
        self.emit("vector", lambda c: c.tensor_scalar(
            stB_sl, stB_sl,
            self.colt["etdB"][pbase:pbase + 64,
                              p * 16 + h:p * 16 + h + 1],
            None, ALU.mult))
        self.emit("vector", lambda c: c.tensor_tensor(
            stB_sl, stB_sl, pdS[0][pbase:pbase + 64, 0:128], ALU.add))
        nst = self.inc("vector", "ve")
        self.bank_release(pdS, "ve")
        yield
        self.wait("scalar", "ve", nst)
        self.emit("scalar", lambda c: c.copy(stB16_sl, stB_sl))
        ncast = self.inc("scalar", "ac")
        self.st_ready_b[h] = ("ac", ncast)

    def attn_b_out(self):
        of_ve = self.n_of("ve")
        for m in range(16):
            self.wait("scalar", "ve", of_ve)
            self.emit("scalar", lambda c, m_=m: c.activation(
                self.ar(A_GOUT + m_ * NT, NT), self.ar(A_GOUT + m_ * NT, NT),
                AF.Silu))
            nsl = self.inc("scalar", "ac")
            self.wait("vector", "ac", nsl)
            self.emit("vector", lambda c, m_=m: c.tensor_tensor(
                self.ar(A_OF + m_ * NT, NT), self.ar(A_OF + m_ * NT, NT),
                self.ar(A_GOUT + m_ * NT, NT), ALU.mult))
            self.inc("vector", "ve")
        og_ve = self.n_of("ve")
        last_add = None
        for m in range(16):
            buf, slot, mir = self.next_group()
            pbk = self.bank()
            self.wait("tensor", "ve", og_ve)
            for k in range(KC):
                self.emit("tensor", lambda c, k_=k: c.matmul(
                    pbk[0][:, 0:NT], buf[:, k_ * 128:(k_ + 1) * 128],
                    self.ar(A_OF + k_ * NT, NT), start=(k_ == 0),
                    stop=(k_ == KC - 1), skip_group_check=True))
            self.inc("tensor", "pe")
            self.group_done(slot, mir)
            nd = self.n_of("pe")
            xsl = self.xf[:, m * NT:(m + 1) * NT]
            self.wait("vector", "pe", nd)
            self.emit("vector", lambda c, x_=xsl: c.tensor_tensor(
                x_, x_, pbk[0][:, 0:NT], ALU.add))
            last_add = self.inc("vector", "ve")
            self.bank_release(pbk, "ve")
        self.xver = ("ve", last_add)

    def build(self):
        st = ExitStack()
        with st:
            self.alloc(st.enter_context)
            with self.nc.Block() as block:
                @block.sync
                def _(cur):
                    self.reset_pass("sync", cur)
                    self.program()

                @block.gpsimd
                def _(cur):
                    self.reset_pass("gpsimd", cur)
                    self.program()

                @block.vector
                def _(cur):
                    self.reset_pass("vector", cur)
                    self.program()

                @block.tensor
                def _(cur):
                    self.reset_pass("tensor", cur)
                    self.program()

                @block.scalar
                def _(cur):
                    self.reset_pass("scalar", cur)
                    self.program()
        return self.nc


def arrange_w(wT, nm, nk):
    K_in, M_out = wT.shape
    assert K_in == nk * 128 and M_out == nm * 128
    return np.ascontiguousarray(
        wT.reshape(nk, 128, nm, 128).transpose(2, 1, 0, 3)
        .reshape(nm, 128, nk * 128))


def arrange_small(wT):
    return np.ascontiguousarray(
        wT.reshape(16, 128, 16).transpose(1, 0, 2).reshape(128, 256))


def arrange_conv(cw):
    ch = cw.shape[0]
    nch = ch // 128
    return np.ascontiguousarray(
        cw[:, 0, :].reshape(nch, 128, 4).transpose(1, 0, 2)
        .reshape(128, nch * 4))


def make_inputs(core, full):
    f = lambda k: np.asarray(full[k], np.float32)
    x = f("L0_attn_a_x")[2 * core:2 * core + 2]
    xa = np.ascontiguousarray(x.transpose(0, 2, 1).reshape(2, 16, 128, 2048))
    d = {
        "x": xa,
        "wq": arrange_w(f("L0_attn_a_Wq"), 16, 16),
        "wk": arrange_w(f("L0_attn_a_Wk"), 16, 16),
        "wv": arrange_w(f("L0_attn_a_Wv"), 16, 16),
        "wg": arrange_w(f("L0_ffn_w_gate").T, 64, 16),
        "wu": arrange_w(f("L0_ffn_w_up").T, 64, 16),
        "wd": arrange_w(f("L0_ffn_w_down").T, 16, 64),
        "wqk": arrange_w(f("L0_attn_b_qk_w").T, 16, 16),
        "wvg": arrange_w(f("L0_attn_b_vg_w").T, 32, 16),
        "wo": arrange_w(f("L0_attn_b_o_w").T, 16, 16),
        "wal": arrange_small(f("L0_attn_a_Walpha")),
        "wbe": arrange_small(f("L0_attn_a_Wbeta")),
        "wa": arrange_small(f("L0_attn_b_a_w").T),
        "wb": arrange_small(f("L0_attn_b_b_w").T),
        "cq": arrange_conv(f("L0_attn_b_q_conv_w")),
        "ck": arrange_conv(f("L0_attn_b_k_conv_w")),
        "cv": arrange_conv(f("L0_attn_b_v_conv_w")),
        "bq": np.ascontiguousarray(f("L0_attn_a_bq").reshape(16, 128).T),
        "bk": np.ascontiguousarray(f("L0_attn_a_bk").reshape(16, 128).T),
        "bv": np.ascontiguousarray(f("L0_attn_a_bv").reshape(16, 128).T),
        "bal": f("L0_attn_a_balpha").reshape(16, 1),
        "bbe": f("L0_attn_a_bbeta").reshape(16, 1),
        "bbb": f("L0_attn_b_b_bias").reshape(16, 1),
        "bbt": f("L0_attn_b_beta_bias").reshape(16, 1),
        "bdt": f("L0_attn_b_dt_bias").reshape(16, 1),
        "alog": f("L0_attn_b_A_log").reshape(16, 1),
    }
    return {k: np.ascontiguousarray(v) for k, v in d.items()}


_NC_CACHE = {}


def get_nc(n_sublayers=4, n_tiles=NTILES):
    key = (n_sublayers, n_tiles)
    if key not in _NC_CACHE:
        _NC_CACHE[key] = MK(n_sublayers, n_tiles).build()
    return _NC_CACHE[key]


def kernel(**inputs):
    nc = get_nc()
    in_maps = [make_inputs(c, inputs) for c in range(8)]
    res = run_bass_kernel_spmd(nc, in_maps, core_ids=list(range(8)))
    out = np.empty((16, 2048, 2048), np.float32)
    for c in range(8):
        o = np.asarray(res.results[c]["out"])
        out[2 * c:2 * c + 2] = o.transpose(0, 3, 1, 2).reshape(2, 2048, 2048)
    return out


# version 03
# speedup vs baseline: 1.3448x; 1.0445x over previous
import numpy as np
from contextlib import ExitStack

ENGS = ("sync", "gpsimd", "vector", "tensor", "scalar")

import concourse.bass as bass
import concourse.mybir as mybir
from concourse.bass_utils import run_bass_kernel_spmd

F32 = mybir.dt.float32
BF16 = mybir.dt.bfloat16
AF = mybir.ActivationFunctionType
ALU = mybir.AluOpType

D = 2048
KC = 16
NT = 512
C = 128
NCH = NT // C
NTILES = 8
HA = 16
HB = 16
DK = 64
DV = 128
NEG = -1.0e30
EPS_RMS = float(np.finfo(np.float32).eps)

A_QF, A_KF, A_VF = 0, 8192, 16384
A_H = 0
A_QC, A_KC, A_VC = 0, 4096, 8192
A_GOUT = 16384
A_OF = 24576
ARENA_N = 32768


class Sem:
    __slots__ = ("h", "n", "name")

    def __init__(self, h, name):
        self.h = h
        self.n = 0
        self.name = name


class MK:
    PREFETCH = 2

    def __init__(self, n_sublayers=4, n_tiles=NTILES, tap=None, serialize=0):
        self.n_sublayers = n_sublayers
        self.n_tiles = n_tiles
        self.tap_point = tap
        self.serialize = serialize
        self.nc = bass.Bass("TRN2", target_bir_lowering=False)
        self.eng = None
        self.cur = None
        self._last = {}
        self.sems = {}

    def emit(self, engine, fn):
        self._last_inc[engine] = None
        if self.eng == engine:
            ins = fn(self.cur)
            self._last[engine] = ins
            return ins
        return None

    def inc(self, engine, semname, k=1):
        s = self.sems[semname]
        if self.eng == engine:
            self._last[engine].then_inc(s.h, k)
        s.n += k
        self._last_inc[engine] = (semname, s.n)
        return s.n

    def n_of(self, semname):
        return self.sems[semname].n

    def wait(self, engine, semname, val=None):
        s = self.sems[semname]
        v = s.n if val is None else val
        assert v <= s.n, f"wait on future count {semname} {v}>{s.n}"
        cache = self._wcache.setdefault(engine, {})
        if cache.get(semname, 0) >= v:
            return
        cache[semname] = v
        if self.eng == engine and v > 0:
            self.cur.wait_ge(s.h, v)

    ENG_SEM = {"vector": "ve", "scalar": "ac", "gpsimd": "po",
               "tensor": "pe"}

    def selfsync(self, engine):
        li = self._last_inc.get(engine)
        if li is not None:
            sn, n = li
        else:
            sn = self.ENG_SEM[engine]
            n = self.inc(engine, sn)
        if self.eng == engine:
            self.cur.wait_ge(self.sems[sn].h, n)
        cache = self._wcache.setdefault(engine, {})
        cache[sn] = max(cache.get(sn, 0), n)

    def bank(self, b16=False):
        lst = self.banks_b16 if b16 else self.banks_f32
        n = len(lst)
        i0 = self.bi16 if b16 else self.bi32
        for k in range(n):
            i = (i0 + k) % n
            if not lst[i][3]:
                break
        else:
            raise RuntimeError("no free PSUM bank (all held)")
        if b16:
            self.bi16 = (i + 1) % n
        else:
            self.bi32 = (i + 1) % n
        b = lst[i]
        b[3] = True
        if b[1] is not None:
            self.wait("tensor", b[1], b[2])
        return b

    def bank_release(self, b, semname):
        b[1] = semname
        b[2] = self.n_of(semname)
        b[3] = False

    def ar(self, off, n):
        return self.arena[:, off:off + n]

    def alloc(self, E):
        nc = self.nc
        self.x_in = nc.dram_tensor("x", [2, KC, 128, 2048], F32,
                                   kind="ExternalInput")
        self.x_out = nc.dram_tensor("out", [2, KC, 128, 2048], F32,
                                    kind="ExternalOutput")
        self.w_shapes = [("wq", 16, 16), ("wk", 16, 16), ("wv", 16, 16),
                         ("wg", 64, 16), ("wu", 64, 16), ("wd", 16, 64),
                         ("wqk", 16, 16), ("wvg", 32, 16), ("wo", 16, 16)]
        self.w_in = {n: nc.dram_tensor(n, [m, 128, k * 128], F32,
                                       kind="ExternalInput")
                     for n, m, k in self.w_shapes}
        self.w_scr = {n: nc.dram_tensor(n + "_s", [m, 128, k * 128], BF16,
                                        kind="Internal")
                      for n, m, k in self.w_shapes}
        self.wsm_in = {n: nc.dram_tensor(n, [128, KC * 16], F32,
                                         kind="ExternalInput")
                       for n in ("wal", "wbe", "wa", "wb")}
        self.cv_in = {n: nc.dram_tensor(n, [128, c * 4], F32,
                                        kind="ExternalInput")
                      for n, c in [("cq", 8), ("ck", 8), ("cv", 16)]}
        self.bias_in = {}
        for n in ("bq", "bk", "bv"):
            self.bias_in[n] = nc.dram_tensor(n, [128, 16], F32,
                                             kind="ExternalInput")
        for n in ("bal", "bbe", "bbb", "bbt", "bdt"):
            self.bias_in[n] = nc.dram_tensor(n, [16, 1], F32,
                                             kind="ExternalInput")
        self.alog_in = nc.dram_tensor("alog", [16, 1], F32,
                                      kind="ExternalInput")

        self.xf = E(nc.sbuf_tensor("xf", [128, KC * NT], F32))
        self.xb = E(nc.sbuf_tensor("xbr", [128, KC * NT], BF16))
        self.arena = E(nc.sbuf_tensor("arena", [128, ARENA_N], BF16))
        self.wbuf = [E(nc.sbuf_tensor(f"wbuf{i}", [128, 2048], BF16))
                     for i in range(3)]
        self.rows = E(nc.sbuf_tensor("rows", [1, 1536], F32))
        self.ones_col16 = E(nc.sbuf_tensor("ones_col16", [128, 1], BF16))
        self.ones_row32 = E(nc.sbuf_tensor("ones_row32", [1, 128], F32))
        self.r_b = E(nc.sbuf_tensor("r_b", [128, NT], F32))
        self.m_b = E(nc.sbuf_tensor("m_b", [128, NT], F32))
        self.ident16 = E(nc.sbuf_tensor("ident16", [128, 128], BF16))
        self.ident32 = E(nc.sbuf_tensor("ident32", [128, 128], F32))
        self.tril32 = E(nc.sbuf_tensor("tril32", [128, 128], F32))
        self.slow32 = E(nc.sbuf_tensor("slow32", [128, 128], F32))
        self.addUC = E(nc.sbuf_tensor("addUC", [128, 128], F32))
        self.addSL = E(nc.sbuf_tensor("addSL", [128, 128], F32))
        self.bcol = {n: E(nc.sbuf_tensor("bc_" + n, [128, 16], F32))
                     for n in ("bq", "bk", "bv")}
        self.bsm = {n: E(nc.sbuf_tensor("bs_" + n, [16, 1], F32))
                    for n in ("bal", "bbe", "bbb", "bbt", "bdt", "bb2")}
        self.wsm16 = {n: E(nc.sbuf_tensor("w16_" + n, [128, KC * 16], BF16))
                      for n in ("wal", "wbe", "wa", "wb")}
        self.cvw = {n: E(nc.sbuf_tensor("cvw_" + n, [128, c * 4], F32))
                    for n, c in [("cq", 8), ("ck", 8), ("cv", 16)]}
        self.negexpA = E(nc.sbuf_tensor("negexpA", [16, 1], F32))
        self.alog_sb = E(nc.sbuf_tensor("alog_sb", [16, 1], F32))
        self.stA = E(nc.sbuf_tensor("stA", [128, HA * 128], F32))
        self.stA16 = E(nc.sbuf_tensor("stA16", [128, HA * 128], BF16))
        self.stB = E(nc.sbuf_tensor("stB", [128, (HB // 2) * 128], F32))
        self.stB16 = E(nc.sbuf_tensor("stB16", [128, (HB // 2) * 128], BF16))
        self.colt = {n: E(nc.sbuf_tensor("colt_" + n, [128, 64], F32))
                     for n in ("la", "lb", "cla", "dtx", "etd", "ecd",
                               "edte", "ebt", "bias", "eb2", "g", "G", "dG",
                               "eG", "edG", "etdB", "Glb", "nbG")}
        self.za = E(nc.sbuf_tensor("za", [16, NT], F32))
        self.zb = E(nc.sbuf_tensor("zb", [16, NT], F32))
        self.loga = E(nc.sbuf_tensor("loga", [16, NT], F32))
        self.logb = E(nc.sbuf_tensor("logb", [16, NT], F32))
        self.zt1 = E(nc.sbuf_tensor("zt1", [16, NT], F32))
        self.zt2 = E(nc.sbuf_tensor("zt2", [16, NT], F32))
        self.gbuf = E(nc.sbuf_tensor("gbuf", [16, NT], F32))
        self.stage = [E(nc.sbuf_tensor(f"stage{i}", [128, 515], BF16))
                      for i in range(3)]
        self.xsq = [self.stage[0][:, 0:NT], self.stage[1][:, 0:NT]]
        self.tails = E(nc.sbuf_tensor("tails", [128, 96], F32))
        self.tb = {}
        for name, w, dt, nb in [
            ("ktok", 128, BF16, 2), ("vtok", 128, BF16, 2),
            ("kw", 128, BF16, 2), ("attns", 128, BF16, 2),
            ("DT", 128, BF16, 2), ("X2", 128, F32, 2),
            ("ytok", 128, BF16, 2), ("yg", 128, F32, 2),
            ("Mb", 128, BF16, 2), ("Mpr", 128, BF16, 2),
            ("Mng", 128, BF16, 2),
            ("pwA", 128, BF16, 2), ("pwAT", 128, BF16, 2),
            ("pwB", 128, BF16, 2), ("pwBT", 128, BF16, 2),
            ("tj", 128, BF16, 2), ("U16", 128, BF16, 2),
            ("KS", 128, BF16, 2), ("Ab", 128, BF16, 2),
            ("dgb", 128, BF16, 2), ("dgb2", 128, BF16, 2),
            ("o1", 128, F32, 2), ("on", 128, BF16, 2), ("osq", 128, BF16, 2),
            ("ktb", 64, BF16, 2), ("ktl", 64, BF16, 2),
            ("kq2", 128, BF16, 2), ("col", 1, F32, 12), ("sg", NT, BF16, 2),
            ("cvt", NT, F32, 2),
        ]:
            self.tb[name] = [E(nc.sbuf_tensor(f"t_{name}{i}", [128, w], dt))
                             for i in range(nb)]
        self.tb["rw"] = [E(nc.sbuf_tensor(f"t_rw{i}", [1, 256], F32))
                         for i in range(2)]
        self.CONSTS = [1e-5, 1e-6, 1e-24, 1.0, -1.0, -0.5,
                       EPS_RMS, 1.0 / DV, float(np.log(2.0)), 0.5, 2.0]
        self.cst = E(nc.sbuf_tensor("cst", [128, len(self.CONSTS)], F32))
        for i, v in enumerate(self.CONSTS):
            nc.const_aps.register(F32, v, self.cst[:, i:i + 1])
        self.banks_f32 = [[E(nc.psum_tensor(f"pf{i}", [128, 512], F32)),
                           None, 0, False] for i in range(4)]
        self.solve_bank = [[E(nc.psum_tensor(f"psolve{i}", [128, 512], F32)),
                            None, 0, False] for i in range(2)]
        self.banks_b16 = [[E(nc.psum_tensor(f"pb{i}", [128, 512], BF16)),
                           None, 0, False] for i in range(2)]
        for n in ("dS", "dA", "dG", "pe", "ve", "ac", "po"):
            self.sems[n] = Sem(E(nc.semaphore(n)), n)
        self.tap_t = None
        if self.tap_point is not None:
            self.tap_t = nc.dram_tensor("tap", [128, 8192], F32,
                                        kind="ExternalOutput")

    def weight_order_tile(self, sublayers):
        order = []
        if 1 in sublayers:
            for w in ("wq", "wk", "wv"):
                order += [(w, i) for i in range(16)]
        if 2 in sublayers:
            for i in range(64):
                order += [("wg", i), ("wu", i)]
            order += [("wd", i) for i in range(64)]
        if 3 in sublayers:
            order += [("wqk", i) for i in range(16)]
            order += [("wvg", i) for i in range(32)]
            order += [("wo", i) for i in range(16)]
        if 4 in sublayers:
            for i in range(64):
                order += [("wg", i), ("wu", i)]
            order += [("wd", i) for i in range(64)]
        return order

    def stream_init(self):
        subl = list(range(1, self.n_sublayers + 1))
        per_tile = self.weight_order_tile(subl)
        self.g_order = []
        for t in range(self.n_tiles):
            for (w, gi) in per_tile:
                self.g_order.append((t, w, gi))
        self.iptr = 0
        self.cptr = 0
        self.s_slot = 0
        self.s_rel = [None] * len(self.wbuf)
        self.comp = {}
        self.mirrored = set()

    def w_src(self, w, gi, first):
        base = self.w_in[w] if first else self.w_scr[w]
        if w == "wd":
            m, q = gi // 4, gi % 4
            return base[m, :, q * 2048:(q + 1) * 2048]
        return base[gi, :, :]

    def issue_upto(self, target):
        while self.iptr <= target and self.iptr < len(self.g_order):
            idx = self.iptr
            t, w, gi = self.g_order[idx]
            first = (t == 0)
            slot = self.s_slot
            self.s_slot = (slot + 1) % len(self.wbuf)
            rel = self.s_rel[slot]
            buf = self.wbuf[slot]
            q = "gpsimd" if first else "sync"
            sem = "dG" if first else "dS"
            if rel is not None:
                self.wait(q, rel[0], rel[1])
                if len(rel) == 4:
                    self.wait(q, rel[2], rel[3])
            src = self.w_src(w, gi, first)
            self.emit(q, lambda c, d_=buf, s_=src: c.dma_start(d_[:, :], s_))
            n = self.inc(q, sem, 16)
            mkey = (w, gi)
            do_mirror = first and mkey not in self.mirrored
            if do_mirror:
                self.mirrored.add(mkey)
                self.wait("scalar", sem, n)
                dst = self.w_src(w, gi, False)
                self.emit("scalar", lambda c, d_=dst, b_=buf:
                          c.dma_start(d_, b_[:, :]))
                self.inc("scalar", "dA", 16)
            self.comp[idx] = (sem, n, slot,
                              ("dA", self.n_of("dA")) if do_mirror else None)
            self.iptr += 1

    def next_group(self):
        idx = self.cptr
        self.cptr += 1
        self.issue_upto(idx + self.PREFETCH)
        sem, n, slot, mir = self.comp[idx]
        self.wait("tensor", sem, n)
        return self.wbuf[slot], slot, mir

    def group_done(self, slot, mir):
        rel = ("pe", self.n_of("pe"))
        if mir is not None:
            rel = rel + mir
        self.s_rel[slot] = rel

    def reset_pass(self, engine, cur):
        self.eng = engine
        self.cur = cur
        self._last = {}
        self._wcache = {}
        self._last_inc = {}
        self.tb_rel = {}
        for s in self.sems.values():
            s.n = 0
        for b in self.banks_f32 + self.banks_b16 + self.solve_bank:
            b[1] = None
            b[2] = 0
            b[3] = False
        self.bi32 = 0
        self.bi16 = 0
        self.stream_init()
        self.chunk_ctr = 0
        self.head_ctr = 0
        self.st_ready_a = [("po", 0)] * HA
        self.st_ready_b = [("po", 0)] * HB
        self.xver = ("po", 0)

    def col(self, hp, i):
        return self.tb["col"][hp * 6 + i][:, 0:1]

    def tbw(self, name, hp, engine):
        rel = self.tb_rel.get((name, hp))
        if rel is not None:
            self.wait(engine, rel[0], rel[1])
        return self.tb[name][hp]

    def tbr(self, name, hp, sem):
        self.tb_rel[(name, hp)] = (sem, self.n_of(sem))

    def drive(self, *gens):
        gens = list(gens)
        while gens:
            done = []
            for g in gens:
                try:
                    next(g)
                except StopIteration:
                    done.append(g)
            for g in done:
                gens.remove(g)

    def barrier(self):
        for e in ENGS:
            for s in ("pe", "ve", "ac", "po", "dS", "dA", "dG"):
                self.wait(e, s)

    def maybe_tap(self, name, ap):
        if self.tap_point != name or getattr(self, "_tapped", None) == name:
            return
        self._tapped = name
        for s in ("pe", "ve", "ac", "po"):
            self.wait("gpsimd", s)
        P, W = ap.shape
        self.emit("gpsimd", lambda c: c.dma_start(
            self.tap_t[0:P, 0:W], ap, allow_slow_non_contiguous=True))
        self.inc("gpsimd", "dG", 16)
        for e in ("gpsimd", "vector", "scalar", "tensor", "sync"):
            self.wait(e, "dG")

    def program(self):
        self._tapped = None
        self.preamble()
        for t in range(self.n_tiles):
            self.tile(t)
        self.wait("scalar", "dA")
        self.wait("sync", "dS")

    def preamble(self):
        gp = lambda fn: self.emit("gpsimd", fn)
        for tn, cmp_, fill, cm, stp in [
            (self.ident16, ALU.not_equal, 1.0, 1, -1),
            (self.ident32, ALU.not_equal, 1.0, 1, -1),
            (self.tril32, ALU.is_gt, 1.0, 1, -1),
            (self.slow32, ALU.is_ge, 1.0, -1, 1),
            (self.addUC, ALU.is_ge, NEG, -1, 1),
            (self.addSL, ALU.is_gt, NEG, 1, -1),
        ]:
            gp(lambda c, t_=tn: c.memset(t_[:, :], 0.0))
            gp(lambda c, t_=tn, c_=cmp_, f=fill, m=cm, s=stp:
               c.affine_select(
                   out=t_[:, :], in_=t_[:, :], compare_op=c_, fill=f,
                   base=0, pattern=[[s, 128]], channel_multiplier=m))
        gp(lambda c: c.memset(self.ones_col16[:, :], 1.0))
        gp(lambda c: c.memset(self.ones_row32[:, :], 1.0))
        gp(lambda c: c.memset(self.stA[:, :], 0.0))
        gp(lambda c: c.memset(self.stA16[:, :], 0.0))
        gp(lambda c: c.memset(self.stB[:, :], 0.0))
        gp(lambda c: c.memset(self.stB16[:, :], 0.0))
        for i, v in enumerate(self.CONSTS):
            gp(lambda c, i_=i, v_=v: c.memset(self.cst[:, i_:i_ + 1], v_))
        self.inc("gpsimd", "po")
        self.emit("vector", lambda c: c.memset(self.tails[:, :], 0.0))
        self.inc("vector", "ve")
        self.po_consts = self.n_of("po")
        for n in ("wal", "wbe", "wa", "wb"):
            self.emit("gpsimd", lambda c, n_=n: c.dma_start(
                self.wsm16[n_][:, :], self.wsm_in[n_][:, :]))
            self.inc("gpsimd", "dG", 16)
        self.dg_small = self.n_of("dG")
        for n in ("bq", "bk", "bv"):
            self.emit("sync", lambda c, n_=n: c.dma_start(
                self.bcol[n_][:, :], self.bias_in[n_][:, :]))
            self.inc("sync", "dS", 16)
        for n in ("bal", "bbe", "bbb", "bbt", "bdt"):
            self.emit("sync", lambda c, n_=n: c.dma_start(
                self.bsm[n_][:, :], self.bias_in[n_][:, :]))
            self.inc("sync", "dS", 16)
        for n in ("cq", "ck", "cv"):
            self.emit("sync", lambda c, n_=n: c.dma_start(
                self.cvw[n_][:, :], self.cv_in[n_][:, :]))
            self.inc("sync", "dS", 16)
        self.emit("sync", lambda c: c.dma_start(self.alog_sb[:, :],
                                                self.alog_in[:, :]))
        self.inc("sync", "dS", 16)
        self.ds_small = self.n_of("dS")
        self.wait("scalar", "dS", self.ds_small)
        self.wait("scalar", "po", self.po_consts)
        self.emit("scalar", lambda c: c.activation(
            self.negexpA[:, :], self.alog_sb[:, :], AF.Exp))
        na = self.inc("scalar", "ac")
        self.wait("vector", "ac", na)
        self.emit("vector", lambda c: c.tensor_scalar(
            self.negexpA[:, :], self.negexpA[:, :], -1.0, None, ALU.mult))
        self.wait("vector", "dS", self.ds_small)
        self.emit("vector", lambda c: c.tensor_tensor(
            self.bsm["bb2"][:, :], self.bsm["bbb"][:, :],
            self.bsm["bbt"][:, :], ALU.add))
        self.negA_ready = self.inc("vector", "ve")
        self.emit("sync", lambda c: c.dma_start(
            self.xf[:, :].rearrange("p (k t) -> p k t", t=NT),
            self.x_in[0, :, :, 0:NT].rearrange("k p t -> p k t")))
        self.inc("sync", "dS", 16)
        self.xver = ("dS", self.n_of("dS"))

    def ln(self):
        xsem, xn = self.xver
        pb = self.bank()
        pbq = self.bank()
        pe_kc = []
        for kc in range(KC):
            self.wait("gpsimd", xsem, xn)
            self.emit("gpsimd", lambda c, k=kc: c.tensor_copy(
                self.xb[:, k * NT:(k + 1) * NT],
                self.xf[:, k * NT:(k + 1) * NT]))
            npo = self.inc("gpsimd", "po")
            self.wait("scalar", xsem, xn)
            if kc >= 2:
                self.wait("scalar", "pe", pe_kc[kc - 2])
            self.emit("scalar", lambda c, k=kc: c.activation(
                self.xsq[k % 2], self.xf[:, k * NT:(k + 1) * NT],
                AF.Square))
            nac = self.inc("scalar", "ac")
            self.wait("tensor", "po", npo)
            self.wait("tensor", "ac", nac)
            self.emit("tensor", lambda c, k=kc: c.matmul(
                pb[0][0:1, 0:NT], self.ones_col16[:, :],
                self.xb[:, k * NT:(k + 1) * NT],
                start=(k == 0), stop=False, skip_group_check=True))
            self.emit("tensor", lambda c, k=kc: c.matmul(
                pbq[0][0:1, 0:NT], self.ones_col16[:, :],
                self.xsq[kc % 2], start=(k == 0),
                stop=(k == KC - 1), skip_group_check=True))
            pe_kc.append(self.inc("tensor", "pe"))
        pe_stats = self.n_of("pe")
        R = self.rows
        self.wait("vector", "pe", pe_stats)
        self.emit("vector", lambda c: c.tensor_scalar(
            R[:, 0:512], pb[0][0:1, 0:NT], 1.0 / D, None, ALU.mult))
        self.emit("vector", lambda c: c.tensor_scalar(
            R[:, 1024:1536], pbq[0][0:1, 0:NT], 1.0 / D, None,
            ALU.mult))
        self.emit("vector", lambda c: c.tensor_tensor(
            R[:, 512:1024], R[:, 0:512], R[:, 0:512], ALU.mult))
        self.emit("vector", lambda c: c.tensor_tensor(
            R[:, 1024:1536], R[:, 1024:1536], R[:, 512:1024],
            ALU.subtract))
        nve = self.inc("vector", "ve")
        self.bank_release(pb, "ve")
        self.bank_release(pbq, "ve")
        self.wait("scalar", "ve", nve)
        self.emit("scalar", lambda c: c.activation(
            R[:, 512:1024], R[:, 1024:1536], AF.Ln, bias=1e-5))
        self.emit("scalar", lambda c: c.activation(
            R[:, 512:1024], R[:, 512:1024], AF.Exp, scale=-0.5))
        r_ready = self.inc("scalar", "ac")
        pb2 = self.bank()
        pb3 = self.bank()
        self.wait("tensor", "ac", r_ready)
        self.emit("tensor", lambda c: c.matmul(
            pb2[0][:, 0:NT], self.ones_row32[0:1, :], R[0:1, 0:512],
            start=True, stop=True))
        self.emit("tensor", lambda c: c.matmul(
            pb3[0][:, 0:NT], self.ones_row32[0:1, :], R[0:1, 512:1024],
            start=True, stop=True))
        nrb = self.inc("tensor", "pe")
        self.wait("vector", "pe", nrb)
        self.wait("vector", "po")
        self.emit("vector", lambda c: c.tensor_copy(
            self.m_b[:, :], pb2[0][:, 0:NT]))
        self.emit("vector", lambda c: c.tensor_copy(
            self.r_b[:, :], pb3[0][:, 0:NT]))
        rb_ready = self.inc("vector", "ve")
        self.bank_release(pb2, "ve")
        self.bank_release(pb3, "ve")
        self.xr_k = []
        for kc in range(KC):
            engn = "vector" if kc % 2 == 0 else "gpsimd"
            self.wait(engn, "pe", pe_stats)
            if engn == "gpsimd":
                self.wait(engn, "ve", rb_ready)
            self.emit(engn, lambda c, k=kc: c.tensor_tensor(
                self.xb[:, k * NT:(k + 1) * NT],
                self.xf[:, k * NT:(k + 1) * NT], self.m_b[:, :],
                ALU.subtract))
            self.emit(engn, lambda c, k=kc: c.tensor_tensor(
                self.xb[:, k * NT:(k + 1) * NT],
                self.xb[:, k * NT:(k + 1) * NT], self.r_b[:, :], ALU.mult))
            sn = "ve" if engn == "vector" else "po"
            self.xr_k.append((sn, self.inc(engn, sn)))
        self.xr_ve = self.n_of("ve")
        self.xr_po = self.n_of("po")
        self.maybe_tap("ln_xb", self.xb[:, :])
        self.maybe_tap("ln_rows", self.rows[0:1, :])

    def proj(self, n_m, wname, evac):
        for m in range(n_m):
            buf, slot, mir = self.next_group()
            pbk = self.bank()
            for k in range(KC):
                self.wait("tensor", *self.xr_k[k])
                self.emit("tensor", lambda c, k_=k: c.matmul(
                    pbk[0][:, 0:NT], buf[:, k_ * 128:(k_ + 1) * 128],
                    self.xb[:, k_ * NT:(k_ + 1) * NT], start=(k_ == 0),
                    stop=(k_ == KC - 1), skip_group_check=True))
            self.inc("tensor", "pe")
            self.group_done(slot, mir)
            evac(m, pbk)

    def proj_small(self, wname, out_t, bias_name):
        pbk = self.bank()
        self.wait("tensor", "dG", self.dg_small)
        for k in range(KC):
            self.wait("tensor", *self.xr_k[k])
            self.emit("tensor", lambda c, k_=k: c.matmul(
                pbk[0][0:16, 0:NT],
                self.wsm16[wname][:, k_ * 16:(k_ + 1) * 16],
                self.xb[:, k_ * NT:(k_ + 1) * NT],
                start=(k_ == 0), stop=(k_ == KC - 1),
                skip_group_check=True))
        self.inc("tensor", "pe")
        npe = self.n_of("pe")
        self.wait("vector", "pe", npe)
        self.wait("vector", "dS", self.ds_small)
        self.emit("vector", lambda c: c.tensor_scalar(
            out_t[:, :], pbk[0][0:16, 0:NT], self.bsm[bias_name][:, 0:1],
            None, ALU.add))
        self.inc("vector", "ve")
        self.bank_release(pbk, "ve")
        return self.n_of("ve")

    def tile(self, t):
        seq = t // 4
        tt = t % 4
        tok0 = tt * NT
        if tt == 0 and t > 0:
            npe = self.n_of("pe")
            self.wait("gpsimd", "pe", npe)
            self.emit("gpsimd", lambda c: c.memset(self.stA[:, :], 0.0))
            self.emit("gpsimd", lambda c: c.memset(self.stA16[:, :], 0.0))
            self.emit("gpsimd", lambda c: c.memset(self.stB[:, :], 0.0))
            self.emit("gpsimd", lambda c: c.memset(self.stB16[:, :], 0.0))
            npo = self.inc("gpsimd", "po")
            self.wait("vector", "pe", npe)
            self.emit("vector", lambda c: c.memset(self.tails[:, :], 0.0))
            self.inc("vector", "ve")
            self.st_ready_a = [("po", npo)] * HA
            self.st_ready_b = [("po", npo)] * HB
        elif t == 0:
            self.st_ready_a = [("po", self.po_consts)] * HA
            self.st_ready_b = [("po", self.po_consts)] * HB

        if self.n_sublayers >= 1:
            self.ln()
            self.attn_a(t)
        self.maybe_tap("x_after1", self.xf[:, :])
        if self.n_sublayers >= 2:
            self.ln()
            self.ffn()
        self.maybe_tap("x_after2", self.xf[:, :])
        if self.n_sublayers >= 3:
            self.ln()
            self.attn_b(t)
        if self.n_sublayers >= 4:
            self.ln()
            self.ffn()

        xsem, xn = self.xver
        self.wait("scalar", xsem, xn)
        self.emit("scalar", lambda c, s_=seq, t0=tok0: c.dma_start(
            self.x_out[s_, :, :, t0:t0 + NT].rearrange("k p t -> p k t"),
            self.xf[:, :].rearrange("p (k t) -> p k t", t=NT)))
        nst = self.inc("scalar", "dA", 16)
        if t + 1 < self.n_tiles:
            s2, t2 = (t + 1) // 4, (t + 1) % 4
            self.wait("sync", "dA", nst)
            self.emit("sync", lambda c, s_=s2, t0=t2 * NT: c.dma_start(
                self.xf[:, :].rearrange("p (k t) -> p k t", t=NT),
                self.x_in[s_, :, :, t0:t0 + NT].rearrange("k p t -> p k t")))
            self.inc("sync", "dS", 16)
            self.xver = ("dS", self.n_of("dS"))

    def attn_a(self, t):
        def mk_evac(base, bname):
            def evac(m, pbk):
                npe = self.n_of("pe")
                self.wait("vector", "pe", npe)
                self.wait("vector", "dS", self.ds_small)
                self.emit("vector", lambda c, m_=m: c.tensor_scalar(
                    self.ar(base + m_ * NT, NT), pbk[0][:, 0:NT],
                    self.bcol[bname][:, m_:m_ + 1], None, ALU.add))
                self.inc("vector", "ve")
                self.bank_release(pbk, "ve")
            return evac
        self.proj(16, "wq", mk_evac(A_QF, "bq"))
        self.proj(16, "wk", mk_evac(A_KF, "bk"))
        self.proj(16, "wv", mk_evac(A_VF, "bv"))
        qkv_ve = self.n_of("ve")
        self.maybe_tap("qf", self.ar(A_QF, 8192))
        self.maybe_tap("kf", self.ar(A_KF, 8192))
        self.maybe_tap("vf", self.ar(A_VF, 8192))
        za_ve = self.proj_small("wal", self.za, "bal")
        zb_ve = self.proj_small("wbe", self.zb, "bbe")
        self.wait("scalar", "ve", za_ve)
        self.emit("scalar", lambda c: c.activation(
            self.zt1[:, :], self.za[:, :], AF.Exp, scale=-1.0))
        n1 = self.inc("scalar", "ac")
        self.wait("vector", "ac", n1)
        self.emit("vector", lambda c: c.tensor_scalar(
            self.zt1[:, :], self.zt1[:, :], 1.0, None, ALU.add))
        self.emit("vector", lambda c: c.reciprocal(self.zt1[:, :],
                                                   self.zt1[:, :]))
        n2 = self.inc("vector", "ve")
        self.wait("scalar", "ve", n2)
        self.emit("scalar", lambda c: c.activation(
            self.loga[:, :], self.zt1[:, :], AF.Ln, bias=1e-6))
        self.emit("scalar", lambda c: c.activation(
            self.zt1[:, :], self.zb[:, :], AF.Abs))
        self.emit("scalar", lambda c: c.activation(
            self.zt1[:, :], self.zt1[:, :], AF.Exp, scale=-1.0))
        self.emit("scalar", lambda c: c.activation(
            self.zt1[:, :], self.zt1[:, :], AF.Ln, bias=1.0))
        n4 = self.inc("scalar", "ac")
        self.wait("vector", "ac", n4)
        self.wait("vector", "ve", zb_ve)
        self.emit("vector", lambda c: c.tensor_scalar(
            self.zt2[:, :], self.zb[:, :], 0.0, None, ALU.max))
        self.emit("vector", lambda c: c.tensor_tensor(
            self.zt2[:, :], self.zt2[:, :], self.zt1[:, :], ALU.add))
        n5 = self.inc("vector", "ve")
        self.wait("scalar", "ve", n5)
        self.emit("scalar", lambda c: c.activation(
            self.logb[:, :], self.zt2[:, :], AF.Ln))
        ab_ready = self.inc("scalar", "ac")
        self.maybe_tap("loga", self.loga[:, :])
        self.maybe_tap("logb", self.logb[:, :])
        preps = [self.attn_a_prep(cch, ab_ready) for cch in range(NCH)]
        for cch in range(NCH):
            self.attn_a_chunk(t, cch, qkv_ve, *preps[cch])
            self.maybe_tap(f"stA_c{cch}", self.stA[:, :])
        self.maybe_tap("stA", self.stA[:, :])

    def attn_a_prep(self, cch, ab_ready):
        tb = self.tb
        p = cch
        coff = cch * C
        csl = lambda n: self.colt[n][:, p * 16:(p + 1) * 16]
        pbt = self.bank()
        self.wait("tensor", "ac", ab_ready)
        self.wait("tensor", "po", self.po_consts)
        self.emit("tensor", lambda c: c.transpose(
            pbt[0][0:128, 0:16], self.loga[:, coff:coff + C],
            self.ident32[0:16, 0:16]))
        self.emit("tensor", lambda c: c.transpose(
            pbt[0][0:128, 16:32], self.logb[:, coff:coff + C],
            self.ident32[0:16, 0:16]))
        ntr = self.inc("tensor", "pe")
        self.wait("vector", "pe", ntr)
        self.emit("vector", lambda c: c.tensor_copy(csl("la"),
                                                    pbt[0][:, 0:16]))
        self.emit("vector", lambda c: c.tensor_copy(csl("lb"),
                                                    pbt[0][:, 16:32]))
        nla = self.inc("vector", "ve")
        self.bank_release(pbt, "ve")
        pbc = self.bank()
        self.wait("tensor", "ve", nla)
        self.emit("tensor", lambda c: c.matmul(
            pbc[0][0:128, 0:16], self.tril32[:, :], csl("la"),
            start=True, stop=False, skip_group_check=True))
        self.emit("tensor", lambda c: c.matmul(
            pbc[0][0:128, 16:32], self.slow32[:, :], csl("la"),
            start=False, stop=True, skip_group_check=True))
        ncl = self.inc("tensor", "pe")
        self.wait("vector", "pe", ncl)
        self.emit("vector", lambda c: c.tensor_copy(csl("cla"),
                                                    pbc[0][:, 0:16]))
        self.emit("vector", lambda c: c.tensor_copy(csl("dtx"),
                                                    pbc[0][:, 16:32]))
        self.selfsync("vector")
        self.emit("vector", lambda c: c.tensor_tensor(
            csl("etd"), csl("cla"), csl("dtx"), ALU.add))
        self.emit("vector", lambda c: c.tensor_tensor(
            csl("bias"), csl("lb"), csl("cla"), ALU.subtract))
        ncl2 = self.inc("vector", "ve")
        self.bank_release(pbc, "ve")
        self.wait("scalar", "ve", ncl2)
        self.emit("scalar", lambda c: c.activation(
            csl("ecd"), csl("cla"), AF.Exp))
        self.emit("scalar", lambda c: c.activation(
            csl("edte"), csl("dtx"), AF.Exp))
        self.emit("scalar", lambda c: c.activation(
            csl("etd"), csl("etd"), AF.Exp))
        self.emit("scalar", lambda c: c.activation(
            csl("ebt"), csl("lb"), AF.Exp))
        self.selfsync("scalar")
        self.emit("scalar", lambda c: c.activation(
            csl("eb2"), csl("edte"), AF.Copy))
        prep_ac0 = self.inc("scalar", "ac")
        self.wait("vector", "ac", prep_ac0)
        self.emit("vector", lambda c: c.tensor_tensor(
            csl("eb2"), csl("eb2"), csl("ebt"), ALU.mult))
        eb2_ve = self.inc("vector", "ve")
        return (nla, prep_ac0, eb2_ve)

    def attn_a_chunk(self, t, cch, qkv_ve, nla, prep_ac, eb2_ve):
        p = cch
        for h0 in range(0, HA, 2):
            g1 = self.attn_a_head(t, cch, h0, p, qkv_ve, nla, prep_ac,
                                  eb2_ve)
            g2 = self.attn_a_head(t, cch, h0 + 1, p, qkv_ve, nla, prep_ac,
                                  eb2_ve)
            self.drive(g1, g2)

    def attn_a_head(self, t, cch, h, p, qkv_ve, nla, prep_ac, eb2_ve):
        tb = self.tb
        hp = h % 2
        coff = cch * C
        one = lambda n: self.colt[n][:, p * 16 + h:p * 16 + h + 1]
        qsl = self.ar(A_QF + h * NT + coff, C)
        ksl = self.ar(A_KF + h * NT + coff, C)
        vsl = self.ar(A_VF + h * NT + coff, C)
        pb1 = self.bank(b16=True)
        self.wait("tensor", "ve", qkv_ve)
        self.emit("tensor", lambda c: c.transpose(
            pb1[0][:, 0:128], ksl, self.ident16[:, :]))
        self.emit("tensor", lambda c: c.transpose(
            pb1[0][:, 128:256], vsl, self.ident16[:, :]))
        ntr = self.inc("tensor", "pe")
        self.wait("vector", "pe", ntr)
        self.tbw("ktok", hp, "vector")
        yield
        self.tbw("vtok", hp, "vector")
        self.emit("vector", lambda c: c.tensor_copy(
            tb["ktok"][hp][:, :], pb1[0][:, 0:128]))
        self.emit("vector", lambda c: c.tensor_copy(
            tb["vtok"][hp][:, :], pb1[0][:, 128:256]))
        ktv = self.inc("vector", "ve")
        self.bank_release(pb1, "ve")
        yield
        self.wait("scalar", "ve", ktv)
        self.emit("scalar", lambda c: c.activation(
            tb["osq"][hp][:, :], tb["ktok"][hp][:, :], AF.Square,
            accum_out=self.col(hp, 0)))
        self.selfsync("scalar")
        self.emit("scalar", lambda c: c.activation(
            self.col(hp, 1), self.col(hp, 0), AF.Ln, bias=1e-24))
        self.selfsync("scalar")
        self.emit("scalar", lambda c: c.activation(
            self.col(hp, 2), self.col(hp, 1), AF.Exp, scale=-0.5))
        nk1 = self.inc("scalar", "ac")
        yield
        self.wait("vector", "ac", max(nk1, prep_ac))
        self.emit("vector", lambda c: c.scalar_tensor_tensor(
            self.col(hp, 3), self.col(hp, 1), -0.5,
            one("bias"), ALU.mult, ALU.add))
        nbias = self.inc("vector", "ve")
        yield
        pbr = self.bank()
        self.wait("tensor", "ve", nla)
        self.emit("tensor", lambda c: c.matmul(
            pbr[0][0:1, 0:128], self.colt["la"][:, p * 16 + h:p * 16 + h + 1],
            self.tril32[:, :], start=True, stop=True))
        nrow = self.inc("tensor", "pe")
        self.wait("vector", "pe", nrow)
        self.tbw("rw", hp, "vector")
        self.emit("vector", lambda c: c.tensor_copy(
            tb["rw"][hp][0:1, 0:128], pbr[0][0:1, 0:128]))
        nrow2 = self.inc("vector", "ve")
        self.bank_release(pbr, "ve")
        yield
        pbx = self.bank()
        self.wait("tensor", "ve", nrow2)
        self.emit("tensor", lambda c: c.matmul(
            pbx[0][:, 0:128], self.ones_row32[0:1, :],
            tb["rw"][hp][0:1, 0:128], start=True, stop=True))
        nX = self.inc("tensor", "pe")
        self.tbr("rw", hp, "pe")
        self.wait("vector", "pe", nX)
        self.tbw("X2", hp, "vector")
        self.emit("vector", lambda c: c.tensor_tensor(
            tb["X2"][hp][:, :], pbx[0][:, 0:128], self.addUC[:, :], ALU.add))
        nX2 = self.inc("vector", "ve")
        self.bank_release(pbx, "ve")
        yield
        self.wait("scalar", "ve", max(nX2, nbias))
        self.tbw("DT", hp, "scalar")
        self.emit("scalar", lambda c: c.activation(
            tb["DT"][hp][:, :], tb["X2"][hp][:, :], AF.Exp,
            bias=self.col(hp, 3)))
        nDT = self.inc("scalar", "ac")
        self.tbr("X2", hp, "ac")
        yield
        pba = self.bank()
        self.wait("tensor", "ve", qkv_ve)
        self.emit("tensor", lambda c: c.matmul(
            pba[0][:, 0:128], ksl, qsl, start=True, stop=True))
        nat = self.inc("tensor", "pe")
        self.wait("vector", "pe", nat)
        self.wait("vector", "ac", nDT)
        self.tbw("attns", hp, "vector")
        self.emit("vector", lambda c: c.tensor_tensor(
            tb["attns"][hp][:, :], pba[0][:, 0:128], tb["DT"][hp][:, :],
            ALU.mult))
        nats = self.inc("vector", "ve")
        self.tbr("DT", hp, "ve")
        yield
        self.bank_release(pba, "ve")
        pyg = self.bank()
        sra = self.st_ready_a[h]
        self.wait("tensor", sra[0], sra[1])
        self.emit("tensor", lambda c: c.matmul(
            pyg[0][:, 0:128], qsl, self.stA16[:, h * 128:(h + 1) * 128],
            start=True, stop=True))
        nyg = self.inc("tensor", "pe")
        self.wait("scalar", "pe", nyg)
        self.wait("scalar", "ac", prep_ac)
        self.tbw("yg", hp, "scalar")
        self.emit("scalar", lambda c: c.activation(
            tb["yg"][hp][:, :], pyg[0][:, 0:128], AF.Copy,
            scale=one("ecd")))
        nyg2 = self.inc("scalar", "ac")
        self.bank_release(pyg, "ac")
        yield
        pyl = self.bank()
        self.wait("tensor", "ve", nats)
        self.emit("tensor", lambda c: c.matmul(
            pyl[0][:, 0:128], tb["attns"][hp][:, :], tb["vtok"][hp][:, :],
            start=True, stop=True))
        nyl = self.inc("tensor", "pe")
        self.tbr("attns", hp, "pe")
        self.wait("vector", "pe", nyl)
        self.wait("vector", "ac", nyg2)
        self.tbw("ytok", hp, "vector")
        self.emit("vector", lambda c: c.tensor_tensor(
            tb["ytok"][hp][:, :], tb["yg"][hp][:, :], pyl[0][:, 0:128],
            ALU.add))
        nyt = self.inc("vector", "ve")
        self.tbr("yg", hp, "ve")
        yield
        self.bank_release(pyl, "ve")
        pyt = self.bank(b16=True)
        self.wait("tensor", "ve", nyt)
        self.emit("tensor", lambda c: c.transpose(
            pyt[0][:, 0:128], tb["ytok"][hp][:, :], self.ident16[:, :]))
        nyT = self.inc("tensor", "pe")
        self.tbr("ytok", hp, "pe")
        xslice = self.xf[:, h * NT + coff:h * NT + coff + C]
        self.wait("vector", "pe", nyT)
        self.emit("vector", lambda c: c.tensor_tensor(
            xslice, xslice, pyt[0][:, 0:128], ALU.add))
        nadd = self.inc("vector", "ve")
        self.bank_release(pyt, "ve")
        yield
        self.wait("vector", "ac", max(nk1, prep_ac))
        self.wait("vector", "ve", eb2_ve)
        self.tbw("kw", hp, "vector")
        self.emit("vector", lambda c: c.tensor_scalar(
            tb["kw"][hp][:, :], tb["ktok"][hp][:, :], self.col(hp, 2),
            one("eb2"), ALU.mult, ALU.mult))
        nkw = self.inc("vector", "ve")
        self.tbr("ktok", hp, "ve")
        yield
        if cch == 0 and h == 0:
            self.maybe_tap("edte_t", self.colt["edte"][:, :])
            self.maybe_tap("ebt_t", self.colt["ebt"][:, :])
            self.maybe_tap("cla_t", self.colt["cla"][:, :])
            self.maybe_tap("dtx_t", self.colt["dtx"][:, :])
            self.maybe_tap("ktok0", tb["ktok"][hp][:, :])
            self.maybe_tap("vtok0", tb["vtok"][hp][:, :])
            self.maybe_tap("kw0", tb["kw"][hp][:, :])
            self.maybe_tap("col5", self.col(hp, 5))
            self.maybe_tap("col2", self.col(hp, 2))
        pdel = self.bank()
        self.wait("tensor", "ve", nkw)
        self.emit("tensor", lambda c: c.matmul(
            pdel[0][:, 0:128], tb["kw"][hp][:, :], tb["vtok"][hp][:, :],
            start=True, stop=True))
        ndel = self.inc("tensor", "pe")
        self.tbr("kw", hp, "pe")
        self.tbr("vtok", hp, "pe")
        stsl = self.stA[:, h * 128:(h + 1) * 128]
        st16 = self.stA16[:, h * 128:(h + 1) * 128]
        self.wait("vector", "pe", ndel)
        self.wait("vector", "ac", prep_ac)
        self.emit("vector", lambda c: c.tensor_scalar(
            stsl, stsl, one("etd"), None, ALU.mult))
        self.emit("vector", lambda c: c.tensor_tensor(
            stsl, stsl, pdel[0][:, 0:128], ALU.add))
        nst = self.inc("vector", "ve")
        self.bank_release(pdel, "ve")
        yield
        self.wait("scalar", "ve", nst)
        self.emit("scalar", lambda c: c.copy(st16, stsl))
        ncast = self.inc("scalar", "ac")
        self.st_ready_a[h] = ("ac", ncast)
        if cch == NCH - 1 and h == HA - 1:
            self.xver = ("ve", nadd)

    def ffn(self):
        tb = self.tb
        sg_used = [0, 0]
        H_ve = []
        for hc in range(64):
            bufg, slg, mirg = self.next_group()
            pg = self.bank()
            for k in range(KC):
                self.wait("tensor", *self.xr_k[k])
                self.emit("tensor", lambda c, k_=k: c.matmul(
                    pg[0][:, 0:NT], bufg[:, k_ * 128:(k_ + 1) * 128],
                    self.xb[:, k_ * NT:(k_ + 1) * NT], start=(k_ == 0),
                    stop=(k_ == KC - 1), skip_group_check=True))
            ng = self.inc("tensor", "pe")
            self.group_done(slg, mirg)
            bufu, slu, miru = self.next_group()
            pu = self.bank()
            for k in range(KC):
                self.emit("tensor", lambda c, k_=k: c.matmul(
                    pu[0][:, 0:NT], bufu[:, k_ * 128:(k_ + 1) * 128],
                    self.xb[:, k_ * NT:(k_ + 1) * NT], start=(k_ == 0),
                    stop=(k_ == KC - 1), skip_group_check=True))
            nu = self.inc("tensor", "pe")
            self.group_done(slu, miru)
            sp = hc % 2
            self.wait("scalar", "pe", ng)
            if sg_used[sp]:
                self.wait("scalar", "ve", sg_used[sp])
            self.emit("scalar", lambda c, s_=sp: c.activation(
                tb["sg"][s_][:, :], pg[0][:, 0:NT], AF.Silu))
            nsg = self.inc("scalar", "ac")
            self.bank_release(pg, "ac")
            self.wait("vector", "pe", nu)
            self.wait("vector", "ac", nsg)
            self.emit("vector", lambda c, m_=hc, s_=sp: c.tensor_tensor(
                self.ar(A_H + m_ * NT, NT), tb["sg"][s_][:, :],
                pu[0][:, 0:NT], ALU.mult))
            nh = self.inc("vector", "ve")
            self.bank_release(pu, "ve")
            sg_used[sp] = nh
            H_ve.append(nh)
        last_add = None
        for m in range(KC):
            pd = self.bank()
            for quarter in range(4):
                buf, slot, mir = self.next_group()
                self.wait("tensor", "ve", H_ve[quarter * 16 + 15])
                for j in range(16):
                    k = quarter * 16 + j
                    self.emit("tensor", lambda c, j_=j, k_=k: c.matmul(
                        pd[0][:, 0:NT], buf[:, j_ * 128:(j_ + 1) * 128],
                        self.ar(A_H + k_ * NT, NT), start=(k_ == 0),
                        stop=(k_ == 63), skip_group_check=True))
                self.inc("tensor", "pe")
                self.group_done(slot, mir)
            nd = self.n_of("pe")
            xsl = self.xf[:, m * NT:(m + 1) * NT]
            self.wait("vector", "pe", nd)
            self.emit("vector", lambda c, x_=xsl: c.tensor_tensor(
                x_, x_, pd[0][:, 0:NT], ALU.add))
            last_add = self.inc("vector", "ve")
            self.bank_release(pd, "ve")
        self.xver = ("ve", last_add)

    def attn_b(self, t):
        tb = self.tb
        stage_rel = [None, None, None]
        scr_rel = [0, 0]
        self.stg_i = 0

        def conv_one(m, pbk, which):
            cw, obase, ch, engn = which
            sl = self.stg_i % 3
            sc = self.stg_i % 2
            self.stg_i += 1
            npe = self.n_of("pe")
            self.wait("vector", "pe", npe)
            if stage_rel[sl] is not None:
                self.wait("vector", stage_rel[sl][0], stage_rel[sl][1])
            self.emit("vector", lambda c: c.tensor_copy(
                self.stage[sl][:, 3:515], pbk[0][:, 0:NT]))
            nev = self.inc("vector", "ve")
            self.bank_release(pbk, "ve")
            toff = {"cq": 0, "ck": 24, "cv": 48}[cw] + ch * 3
            self.wait("vector", "dS", self.ds_small)
            self.emit("vector", lambda c: c.tensor_copy(
                self.stage[sl][:, 0:3], self.tails[:, toff:toff + 3]))
            self.selfsync("vector")
            if scr_rel[sc]:
                self.wait("vector", "ac", scr_rel[sc])
            scr = tb["cvt"][sc]
            wc = lambda j: self.cvw[cw][:, ch * 4 + j:ch * 4 + j + 1]
            self.emit("vector", lambda c: c.tensor_scalar(
                scr[:, :], self.stage[sl][:, 0:512], wc(0), None, ALU.mult))
            for j in (1, 2, 3):
                self.emit("vector", lambda c, j_=j: c.scalar_tensor_tensor(
                    scr[:, :], self.stage[sl][:, j_:j_ + 512], wc(j_),
                    scr[:, :], ALU.mult, ALU.add))
            self.emit("vector", lambda c: c.tensor_copy(
                self.tails[:, toff:toff + 3], self.stage[sl][:, 512:515]))
            ncv = self.inc("vector", "ve")
            stage_rel[sl] = ("ve", ncv)
            self.wait("scalar", "ve", ncv)
            self.emit("scalar", lambda c, ob=obase, ch_=ch: c.activation(
                self.ar(ob + ch_ * 512, 512), scr[:, :], AF.Silu))
            scr_rel[sc] = self.inc("scalar", "ac")

        def evac_qk(m, pbk):
            if m < 8:
                conv_one(m, pbk, ("cq", A_QC, m, "vector"))
            else:
                conv_one(m, pbk, ("ck", A_KC, m - 8, "vector"))

        def evac_vg(m, pbk):
            if m < 16:
                conv_one(m, pbk, ("cv", A_VC, m, "vector"))
            else:
                npe = self.n_of("pe")
                self.wait("vector", "pe", npe)
                self.emit("vector", lambda c, m_=m: c.tensor_copy(
                    self.ar(A_GOUT + (m_ - 16) * NT, NT), pbk[0][:, 0:NT]))
                self.inc("vector", "ve")
                self.bank_release(pbk, "ve")

        self.proj(16, "wqk", evac_qk)
        self.proj(32, "wvg", evac_vg)
        conv_ac = self.n_of("ac")
        self.maybe_tap("qc", self.ar(A_QC, 4096))
        self.maybe_tap("kc", self.ar(A_KC, 4096))
        self.maybe_tap("vc", self.ar(A_VC, 8192))
        self.maybe_tap("gout", self.ar(A_GOUT, 8192))
        zb_ve = self.proj_small("wb", self.zb, "bb2")
        za_ve = self.proj_small("wa", self.za, "bdt")
        self.wait("scalar", "ve", za_ve)
        self.emit("scalar", lambda c: c.activation(
            self.zt1[:, :], self.za[:, :], AF.Abs))
        self.emit("scalar", lambda c: c.activation(
            self.zt1[:, :], self.zt1[:, :], AF.Exp, scale=-1.0))
        self.emit("scalar", lambda c: c.activation(
            self.zt1[:, :], self.zt1[:, :], AF.Ln, bias=1.0))
        n1 = self.inc("scalar", "ac")
        self.wait("vector", "ac", n1)
        self.wait("vector", "ve", self.negA_ready)
        self.emit("vector", lambda c: c.tensor_scalar(
            self.zt2[:, :], self.za[:, :], 0.0, None, ALU.max))
        self.emit("vector", lambda c: c.tensor_tensor(
            self.zt2[:, :], self.zt2[:, :], self.zt1[:, :], ALU.add))
        self.emit("vector", lambda c: c.tensor_scalar(
            self.gbuf[:, :], self.zt2[:, :], self.negexpA[0:16, 0:1], None,
            ALU.mult))
        g_ve = self.inc("vector", "ve")
        self.wait("scalar", "ve", g_ve)
        self.emit("scalar", lambda c: c.activation(
            self.zt1[:, :], self.zb[:, :], AF.Abs))
        self.emit("scalar", lambda c: c.activation(
            self.zt1[:, :], self.zt1[:, :], AF.Exp, scale=-1.0))
        self.emit("scalar", lambda c: c.activation(
            self.zt1[:, :], self.zt1[:, :], AF.Ln, bias=1.0))
        n2 = self.inc("scalar", "ac")
        self.wait("vector", "ac", n2)
        self.emit("vector", lambda c: c.tensor_scalar(
            self.zt2[:, :], self.zb[:, :], -1.0, 0.0, ALU.mult, ALU.max))
        self.emit("vector", lambda c: c.tensor_tensor(
            self.zt2[:, :], self.zt2[:, :], self.zt1[:, :], ALU.add))
        self.emit("vector", lambda c: c.tensor_scalar(
            self.logb[:, :], self.zt2[:, :], -1.0, float(np.log(2.0)),
            ALU.mult, ALU.add))
        logb_ve = self.inc("vector", "ve")
        self.maybe_tap("gbuf", self.gbuf[:, :])
        self.maybe_tap("logbB", self.logb[:, :])
        prepsb = [self.attn_b_prep(cch, conv_ac, g_ve, logb_ve)
                  for cch in range(NCH)]
        for cch in range(NCH):
            self.attn_b_chunk(t, cch, conv_ac, *prepsb[cch])
            self.maybe_tap(f"stB_c{cch}", self.stB[:, :])
        self.maybe_tap("of", self.ar(A_OF, 8192))
        self.attn_b_out()

    def attn_b_prep(self, cch, conv_ac, g_ve, logb_ve):
        tb = self.tb
        p = cch
        coff = cch * C
        csl = lambda n: self.colt[n][:, p * 16:(p + 1) * 16]
        pbt = self.bank()
        self.wait("tensor", "ve", max(g_ve, logb_ve))
        self.wait("tensor", "po", self.po_consts)
        self.emit("tensor", lambda c: c.transpose(
            pbt[0][0:128, 0:16], self.gbuf[:, coff:coff + C],
            self.ident32[0:16, 0:16]))
        self.emit("tensor", lambda c: c.transpose(
            pbt[0][0:128, 16:32], self.logb[:, coff:coff + C],
            self.ident32[0:16, 0:16]))
        ntr = self.inc("tensor", "pe")
        self.wait("vector", "pe", ntr)
        self.emit("vector", lambda c: c.tensor_copy(csl("g"),
                                                    pbt[0][:, 0:16]))
        self.emit("vector", lambda c: c.tensor_copy(csl("lb"),
                                                    pbt[0][:, 16:32]))
        ng = self.inc("vector", "ve")
        self.bank_release(pbt, "ve")
        pbc = self.bank()
        self.wait("tensor", "ve", ng)
        self.emit("tensor", lambda c: c.matmul(
            pbc[0][0:128, 0:16], self.tril32[:, :], csl("g"),
            start=True, stop=False, skip_group_check=True))
        self.emit("tensor", lambda c: c.matmul(
            pbc[0][0:128, 16:32], self.slow32[:, :], csl("g"),
            start=False, stop=True, skip_group_check=True))
        ncl = self.inc("tensor", "pe")
        self.wait("vector", "pe", ncl)
        self.emit("vector", lambda c: c.tensor_copy(csl("G"),
                                                    pbc[0][:, 0:16]))
        self.emit("vector", lambda c: c.tensor_copy(csl("dG"),
                                                    pbc[0][:, 16:32]))
        self.selfsync("vector")
        self.emit("vector", lambda c: c.tensor_tensor(
            csl("etdB"), csl("G"), csl("dG"), ALU.add))
        self.emit("vector", lambda c: c.tensor_tensor(
            csl("Glb"), csl("G"), csl("lb"), ALU.add))
        nG = self.inc("vector", "ve")
        self.bank_release(pbc, "ve")
        self.wait("scalar", "ve", nG)
        self.emit("scalar", lambda c: c.activation(
            csl("eG"), csl("G"), AF.Exp))
        self.emit("scalar", lambda c: c.activation(
            csl("edG"), csl("dG"), AF.Exp))
        self.emit("scalar", lambda c: c.activation(
            csl("etdB"), csl("etdB"), AF.Exp))
        self.emit("scalar", lambda c: c.activation(
            csl("ebt"), csl("lb"), AF.Exp))
        prep_ac = self.inc("scalar", "ac")
        if self.serialize >= 1:
            self.barrier()
        self.wait("vector", "ac", prep_ac)
        self.emit("vector", lambda c: c.tensor_tensor(
            csl("nbG"), csl("ebt"), csl("eG"), ALU.mult))
        self.selfsync("vector")
        self.emit("vector", lambda c: c.tensor_scalar(
            csl("nbG"), csl("nbG"), -1.0, None, ALU.mult))
        nbG_ve = self.inc("vector", "ve")
        return (prep_ac, nG, nbG_ve)

    def attn_b_chunk(self, t, cch, conv_ac, prep_ac, nG, nbG_ve):
        p = cch
        for h0 in range(0, HB, 2):
            g1 = self.attn_b_head(t, cch, h0, p, conv_ac, prep_ac, nG,
                                  nbG_ve)
            g2 = self.attn_b_head(t, cch, h0 + 1, p, conv_ac, prep_ac, nG,
                                  nbG_ve)
            self.drive(g1, g2)

    def attn_b_head(self, t, cch, h, p, conv_ac, prep_ac, nG, nbG_ve):
        tb = self.tb
        hp = h % 2
        coff = cch * C
        one = lambda n: self.colt[n][:, p * 16 + h:p * 16 + h + 1]
        pbase = 64 * (h % 2)
        qsl = self.arena[pbase:pbase + 64,
                         A_QC + (h // 2) * NT + coff:
                         A_QC + (h // 2) * NT + coff + C]
        ksl = self.arena[pbase:pbase + 64,
                         A_KC + (h // 2) * NT + coff:
                         A_KC + (h // 2) * NT + coff + C]
        vsl = self.ar(A_VC + h * NT + coff, C)
        stB_sl = self.stB[pbase:pbase + 64,
                          (h // 2) * 128:(h // 2 + 1) * 128]
        stB16_sl = self.stB16[pbase:pbase + 64,
                              (h // 2) * 128:(h // 2 + 1) * 128]
        idsl = self.ident16[pbase:pbase + 64, pbase:pbase + 64]
        pkt = self.bank(b16=True)
        self.wait("tensor", "ac", conv_ac)
        self.emit("tensor", lambda c: c.transpose(
            pkt[0][:, 0:64], ksl, idsl))
        nkt = self.inc("tensor", "pe")
        self.wait("vector", "pe", nkt)
        self.tbw("ktb", hp, "vector")
        self.emit("vector", lambda c: c.tensor_copy(
            tb["ktb"][hp][:, 0:64], pkt[0][:, 0:64]))
        nktv = self.inc("vector", "ve")
        self.bank_release(pkt, "ve")
        yield
        self.wait("scalar", "ve", nktv)
        self.emit("scalar", lambda c: c.activation(
            tb["kq2"][hp][:, 0:64], tb["ktb"][hp][:, 0:64], AF.Square,
            accum_out=self.col(hp, 0)))
        self.selfsync("scalar")
        self.emit("scalar", lambda c: c.activation(
            self.col(hp, 1), self.col(hp, 0), AF.Ln, bias=1e-6))
        self.selfsync("scalar")
        self.emit("scalar", lambda c: c.activation(
            self.col(hp, 2), self.col(hp, 1), AF.Exp, scale=-0.5))
        nkin = self.inc("scalar", "ac")
        self.tbw("kq2", hp, "scalar")
        yield
        self.emit("scalar", lambda c: c.activation(
            tb["kq2"][hp][pbase:pbase + 64, 0:128], ksl, AF.Square))
        nksq = self.inc("scalar", "ac")
        yield
        prow = self.bank()
        self.wait("tensor", "ve", nG)
        self.emit("tensor", lambda c: c.matmul(
            prow[0][0:1, 0:128], self.colt["g"][:, p * 16 + h:p * 16 + h + 1],
            self.tril32[:, :], start=True, stop=True))
        self.wait("tensor", "ac", nksq)
        self.emit("tensor", lambda c: c.matmul(
            prow[0][0:1, 128:256], self.ones_col16[pbase:pbase + 64, 0:1],
            tb["kq2"][hp][pbase:pbase + 64, 0:128], start=True, stop=True,
            skip_group_check=True))
        nrow = self.inc("tensor", "pe")
        self.tbr("kq2", hp, "pe")
        self.wait("vector", "pe", nrow)
        self.tbw("rw", hp, "vector")
        self.emit("vector", lambda c: c.tensor_copy(
            tb["rw"][hp][0:1, 0:128], prow[0][0:1, 0:128]))
        self.emit("vector", lambda c: c.tensor_copy(
            tb["rw"][hp][0:1, 128:256], prow[0][0:1, 128:256]))
        nrows = self.inc("vector", "ve")
        self.bank_release(prow, "ve")
        yield
        self.wait("scalar", "ve", nrows)
        self.emit("scalar", lambda c: c.activation(
            tb["rw"][hp][0:1, 128:256], tb["rw"][hp][0:1, 128:256], AF.Ln,
            bias=1e-6))
        nlnk = self.inc("scalar", "ac")
        self.wait("vector", "ac", nlnk)
        self.emit("vector", lambda c: c.scalar_tensor_tensor(
            tb["rw"][hp][0:1, 128:256], tb["rw"][hp][0:1, 128:256], -0.5,
            tb["rw"][hp][0:1, 0:128], ALU.mult, ALU.subtract))
        nMrow = self.inc("vector", "ve")
        yield
        pkk = self.bank()
        self.wait("tensor", "ac", conv_ac)
        self.emit("tensor", lambda c: c.matmul(
            pkk[0][:, 0:128], ksl, ksl, start=True, stop=True,
            skip_group_check=True))
        self.emit("tensor", lambda c: c.matmul(
            pkk[0][:, 128:256], ksl, qsl, start=True, stop=True,
            skip_group_check=True))
        nkk = self.inc("tensor", "pe")
        yield
        self.wait("vector", "ac", max(nkin, prep_ac))
        self.emit("vector", lambda c: c.scalar_tensor_tensor(
            self.col(hp, 3), self.col(hp, 1), -0.5,
            one("Glb"), ALU.mult, ALU.add))
        nMb = self.inc("vector", "ve")
        yield
        pX = self.bank()
        self.wait("tensor", "ve", nMrow)
        self.emit("tensor", lambda c: c.matmul(
            pX[0][:, 0:128], self.ones_row32[0:1, :],
            tb["rw"][hp][0:1, 128:256], start=True, stop=True))
        nX = self.inc("tensor", "pe")
        self.wait("vector", "pe", nX)
        self.tbw("X2", hp, "vector")
        self.emit("vector", lambda c: c.tensor_tensor(
            tb["X2"][hp][:, :], pX[0][:, 0:128], self.addSL[:, :], ALU.add))
        nX2 = self.inc("vector", "ve")
        self.bank_release(pX, "ve")
        self.wait("scalar", "ve", max(nX2, nMb))
        self.tbw("DT", hp, "scalar")
        self.emit("scalar", lambda c: c.activation(
            tb["DT"][hp][:, :], tb["X2"][hp][:, :], AF.Exp,
            bias=self.col(hp, 3)))
        nMexp = self.inc("scalar", "ac")
        self.tbr("X2", hp, "ac")
        yield
        self.wait("vector", "pe", nkk)
        self.wait("vector", "ac", nMexp)
        self.tbw("Mb", hp, "vector")
        self.emit("vector", lambda c: c.tensor_tensor(
            tb["Mb"][hp][:, :], pkk[0][:, 0:128], tb["DT"][hp][:, :],
            ALU.mult))
        nM = self.inc("vector", "ve")
        yield
        self.wait("vector", "ac", max(nkin, prep_ac))
        self.selfsync("vector")
        self.emit("vector", lambda c: c.scalar_tensor_tensor(
            self.col(hp, 4), self.col(hp, 1), -0.5,
            one("G"), ALU.mult, ALU.subtract))
        nAb = self.inc("vector", "ve")
        yield
        pXa = self.bank()
        self.wait("tensor", "ve", nrows)
        self.emit("tensor", lambda c: c.matmul(
            pXa[0][:, 0:128], self.ones_row32[0:1, :],
            tb["rw"][hp][0:1, 0:128], start=True, stop=True))
        nXa = self.inc("tensor", "pe")
        self.tbr("rw", hp, "pe")
        self.wait("vector", "pe", nXa)
        self.tbw("X2", hp, "vector")
        self.emit("vector", lambda c: c.tensor_tensor(
            tb["X2"][hp][:, :], pXa[0][:, 0:128], self.addUC[:, :], ALU.add))
        nXa2 = self.inc("vector", "ve")
        self.bank_release(pXa, "ve")
        self.wait("scalar", "ve", max(nXa2, nAb))
        self.tbw("DT", hp, "scalar")
        self.emit("scalar", lambda c: c.activation(
            tb["DT"][hp][:, :], tb["X2"][hp][:, :], AF.Exp,
            bias=self.col(hp, 4)))
        nAexp = self.inc("scalar", "ac")
        self.tbr("X2", hp, "ac")
        yield
        self.wait("vector", "ac", nAexp)
        self.tbw("Ab", hp, "vector")
        self.emit("vector", lambda c: c.tensor_tensor(
            tb["Ab"][hp][:, :], pkk[0][:, 128:256], tb["DT"][hp][:, :],
            ALU.mult))
        nAbar = self.inc("vector", "ve")
        self.tbr("DT", hp, "ve")
        yield
        self.bank_release(pkk, "ve")
        pmt = self.bank(b16=True)
        self.wait("tensor", "ve", nM)
        self.emit("tensor", lambda c: c.transpose(
            pmt[0][:, 0:128], tb["Mb"][hp][:, :], self.ident16[:, :]))
        nmt = self.inc("tensor", "pe")
        self.wait("vector", "pe", nmt)
        self.tbw("Mpr", hp, "vector")
        self.tbw("Mng", hp, "vector")
        self.emit("vector", lambda c: c.tensor_copy(
            tb["Mpr"][hp][:, :], pmt[0][:, 0:128]))
        self.emit("vector", lambda c: c.tensor_scalar(
            tb["Mng"][hp][:, :], pmt[0][:, 0:128], -1.0, None, ALU.mult))
        nmpr = self.inc("vector", "ve")
        self.bank_release(pmt, "ve")
        yield
        pks = self.bank()
        srb = self.st_ready_b[h]
        self.wait("tensor", srb[0], srb[1])
        self.wait("tensor", "ac", conv_ac)
        self.emit("tensor", lambda c: c.matmul(
            pks[0][:, 0:128], ksl, stB16_sl, start=True, stop=True))
        nks = self.inc("tensor", "pe")
        self.wait("vector", "pe", nks)
        self.tbw("KS", hp, "vector")
        self.emit("vector", lambda c: c.tensor_copy(
            tb["KS"][hp][:, :], pks[0][:, 0:128]))
        nksv = self.inc("vector", "ve")
        self.bank_release(pks, "ve")
        yield
        pvt = self.bank(b16=True)
        self.wait("tensor", "ac", conv_ac)
        self.emit("tensor", lambda c: c.transpose(
            pvt[0][:, 0:128], vsl, self.ident16[:, :]))
        nvt = self.inc("tensor", "pe")
        self.wait("vector", "pe", nvt)
        self.tbw("vtok", hp, "vector")
        self.emit("vector", lambda c: c.tensor_copy(
            tb["vtok"][hp][:, :], pvt[0][:, 0:128]))
        nvtv = self.inc("vector", "ve")
        self.bank_release(pvt, "ve")
        yield
        self.wait("vector", "ac", max(prep_ac, nkin))
        self.wait("vector", "ve", nbG_ve)
        self.wait("vector", "po", self.po_consts)
        self.tbw("dgb", hp, "vector")
        self.tbw("dgb2", hp, "vector")
        self.emit("vector", lambda c: c.tensor_scalar(
            tb["dgb"][hp][:, :], self.ident16[:, :], one("ebt"), None,
            ALU.mult))
        self.emit("vector", lambda c: c.tensor_scalar(
            tb["dgb2"][hp][:, :], self.ident16[:, :], one("nbG"),
            self.col(hp, 2), ALU.mult, ALU.mult))
        ndg = self.inc("vector", "ve")
        yield
        pR = self.solve_bank[hp]
        if pR[1] is not None:
            self.wait("tensor", pR[1], pR[2])
        self.wait("tensor", "ve", max(ndg, nvtv, nksv))
        self.emit("tensor", lambda c: c.matmul(
            pR[0][:, 0:128], tb["dgb"][hp][:, :], tb["vtok"][hp][:, :],
            start=True, stop=False, skip_group_check=True))
        self.emit("tensor", lambda c: c.matmul(
            pR[0][:, 0:128], tb["dgb2"][hp][:, :], tb["KS"][hp][:, :],
            start=False, stop=True, skip_group_check=True))
        nR = self.inc("tensor", "pe")
        self.tbr("KS", hp, "pe")
        self.tbr("vtok", hp, "pe")
        self.tbr("dgb", hp, "pe")
        self.tbr("dgb2", hp, "pe")
        self.wait("vector", "pe", nR)
        self.tbw("tj", hp, "vector")
        self.emit("vector", lambda c: c.tensor_copy(
            tb["tj"][hp][:, :], pR[0][:, 0:128]))
        ntj = self.inc("vector", "ve")
        yield
        prime_list = [tb["Mng"][hp]]
        curP, curPT = tb["Mb"][hp], tb["Mpr"][hp]
        sq_ve = nmpr
        bufs = [(tb["pwA"][hp], tb["pwAT"][hp]),
                (tb["pwB"][hp], tb["pwBT"][hp])]
        for lvl in range(1, 7):
            dst, dstT = bufs[lvl % 2]
            psq = self.bank()
            self.wait("tensor", "ve", sq_ve)
            if lvl < 6:
                self.emit("tensor", lambda c, a=curPT, b=curP: c.matmul(
                    psq[0][:, 0:128], a[:, 0:128], b[:, 0:128],
                    start=True, stop=False, skip_group_check=True))
                self.emit("tensor", lambda c, a=curP, b=curPT: c.matmul(
                    psq[0][:, 128:256], a[:, 0:128], b[:, 0:128],
                    start=True, stop=True, skip_group_check=True))
            else:
                self.emit("tensor", lambda c, a=curP, b=curPT: c.matmul(
                    psq[0][:, 0:128], a[:, 0:128], b[:, 0:128],
                    start=True, stop=True, skip_group_check=True))
            nsq = self.inc("tensor", "pe")
            self.wait("vector", "pe", nsq)
            dnames = {id(self.tb["pwA"][hp]): ("pwA", hp),
                      id(self.tb["pwAT"][hp]): ("pwAT", hp),
                      id(self.tb["pwB"][hp]): ("pwB", hp),
                      id(self.tb["pwBT"][hp]): ("pwBT", hp)}
            if lvl < 6:
                self.tbw(*dnames[id(dst)][:1], hp, "vector") if False else None
                self.tbw(dnames[id(dst)][0], hp, "vector")
                self.tbw(dnames[id(dstT)][0], hp, "vector")
                self.emit("vector", lambda c, d=dst: c.tensor_copy(
                    d[:, 0:128], psq[0][:, 0:128]))
                self.emit("vector", lambda c, d=dstT: c.tensor_copy(
                    d[:, 0:128], psq[0][:, 128:256]))
            else:
                self.tbw(dnames[id(dstT)][0], hp, "vector")
                self.emit("vector", lambda c, d=dstT: c.tensor_copy(
                    d[:, 0:128], psq[0][:, 0:128]))
            sq_ve = self.inc("vector", "ve")
            self.bank_release(psq, "ve")
            prime_list.append(dstT)
            curP, curPT = dst, dstT
            yield
        app_ve = ntj
        for lvl in range(6, 0, -1):
            Pp = prime_list[lvl]
            self.wait("tensor", "ve", app_ve)
            self.emit("tensor", lambda c, a=Pp: c.matmul(
                pR[0][:, 0:128], a[:, 0:128], tb["tj"][hp][:, :],
                start=False, stop=False, skip_group_check=True))
            napp = self.inc("tensor", "pe")
            self.wait("vector", "pe", napp)
            self.emit("vector", lambda c: c.tensor_copy(
                tb["tj"][hp][:, :], pR[0][:, 0:128]))
            app_ve = self.inc("vector", "ve")
            yield
        self.wait("tensor", "ve", app_ve)
        self.emit("tensor", lambda c: c.matmul(
            pR[0][:, 0:128], tb["Mng"][hp][:, :], tb["tj"][hp][:, :],
            start=False, stop=True, skip_group_check=True))
        nU = self.inc("tensor", "pe")
        for nm in ("Mb", "Mpr", "Mng", "pwA", "pwAT", "pwB", "pwBT", "tj"):
            self.tbr(nm, hp, "pe")
        self.wait("vector", "pe", nU)
        self.tbw("U16", hp, "vector")
        self.emit("vector", lambda c: c.tensor_copy(
            tb["U16"][hp][:, :], pR[0][:, 0:128]))
        nU16 = self.inc("vector", "ve")
        yield
        self.bank_release(pR, "ve")
        if cch == 1 and h == 0:
            self.maybe_tap("c1U", tb["U16"][hp][:, :])
            self.maybe_tap("c1KS", tb["KS"][hp][:, :])
        if cch == 0 and h == 0:
            self.maybe_tap("bU16", tb["U16"][hp][:, :])
            self.maybe_tap("bMb", tb["Mb"][hp][:, :])
            self.maybe_tap("bMpr", tb["Mpr"][hp][:, :])
            self.maybe_tap("bAb", tb["Ab"][hp][:, :])
            self.maybe_tap("bKS", tb["KS"][hp][:, :])
            self.maybe_tap("bktb", tb["ktb"][hp][:, :])
            self.maybe_tap("bvtok", tb["vtok"][hp][:, :])
            self.maybe_tap("bdgb", tb["dgb"][hp][:, :])
            self.maybe_tap("bdgb2", tb["dgb2"][hp][:, :])
        pO = self.bank()
        self.wait("tensor", srb[0], srb[1])
        self.emit("tensor", lambda c: c.matmul(
            pO[0][:, 0:128], qsl, stB16_sl, start=True, stop=True))
        nqs = self.inc("tensor", "pe")
        self.wait("scalar", "pe", nqs)
        self.wait("scalar", "ac", prep_ac)
        self.tbw("o1", hp, "scalar")
        self.emit("scalar", lambda c: c.activation(
            tb["o1"][hp][:, :], pO[0][:, 0:128], AF.Copy, scale=one("eG")))
        no1 = self.inc("scalar", "ac")
        self.bank_release(pO, "ac")
        yield
        pO2 = self.bank()
        self.wait("tensor", "ve", max(nAbar, nU16))
        self.emit("tensor", lambda c: c.matmul(
            pO2[0][:, 0:128], tb["Ab"][hp][:, :], tb["U16"][hp][:, :],
            start=True, stop=True))
        nO2 = self.inc("tensor", "pe")
        self.tbr("Ab", hp, "pe")
        self.wait("scalar", "ac", nksq)
        self.tbw("osq", hp, "scalar")
        self.emit("scalar", lambda c: c.activation(
            tb["osq"][hp][pbase:pbase + 64, 0:128], qsl, AF.Square))
        nqsq = self.inc("scalar", "ac")
        peps = self.bank()
        self.wait("tensor", "ac", nqsq)
        self.emit("tensor", lambda c: c.matmul(
            peps[0][:, 0:1], tb["osq"][hp][pbase:pbase + 64, 0:128],
            self.ones_col16[pbase:pbase + 64, 0:1], start=True, stop=True))
        neps = self.inc("tensor", "pe")
        self.tbr("osq", hp, "pe")
        self.wait("vector", "pe", neps)
        self.wait("vector", "ac", nAexp)
        self.emit("vector", lambda c: c.tensor_scalar(
            self.col(hp, 3), peps[0][:, 0:1], EPS_RMS * DK,
            EPS_RMS * DK * 1e-6, ALU.mult, ALU.add))
        nepsv = self.inc("vector", "ve")
        self.bank_release(peps, "ve")
        yield
        self.wait("vector", "pe", nO2)
        self.wait("vector", "ac", no1)
        self.emit("vector", lambda c: c.tensor_tensor(
            tb["o1"][hp][:, :], tb["o1"][hp][:, :], pO2[0][:, 0:128],
            ALU.add))
        nOf = self.inc("vector", "ve")
        yield
        self.bank_release(pO2, "ve")
        if cch == 1 and h == 0:
            self.maybe_tap("c1o1", tb["o1"][hp][:, :])
        if cch == 0 and h == 0:
            self.maybe_tap("bo1", tb["o1"][hp][:, :])
        self.wait("scalar", "ve", nOf)
        self.tbw("osq", hp, "scalar")
        self.emit("scalar", lambda c: c.activation(
            tb["osq"][hp][:, :], tb["o1"][hp][:, :], AF.Square,
            accum_out=self.col(hp, 0)))
        self.selfsync("scalar")
        self.wait("scalar", "ve", nepsv)
        self.emit("scalar", lambda c: c.activation(
            self.col(hp, 1), self.col(hp, 0), AF.Ln,
            bias=self.col(hp, 3), scale=1.0 / DV))
        self.selfsync("scalar")
        self.emit("scalar", lambda c: c.activation(
            self.col(hp, 5), self.col(hp, 1), AF.Exp, scale=-0.5))
        nrms = self.inc("scalar", "ac")
        yield
        self.wait("vector", "ac", nrms)
        self.tbw("on", hp, "vector")
        self.emit("vector", lambda c: c.tensor_scalar(
            tb["on"][hp][:, :], tb["o1"][hp][:, :], self.col(hp, 5),
            None, ALU.mult))
        non = self.inc("vector", "ve")
        self.tbr("o1", hp, "ve")
        yield
        if cch == 0 and h == 0:
            self.maybe_tap("bon", tb["on"][hp][:, :])
        pot = self.bank(b16=True)
        self.wait("tensor", "ve", non)
        self.emit("tensor", lambda c: c.transpose(
            pot[0][:, 0:128], tb["on"][hp][:, :], self.ident16[:, :]))
        noT = self.inc("tensor", "pe")
        self.tbr("on", hp, "pe")
        self.wait("vector", "pe", noT)
        self.emit("vector", lambda c: c.tensor_copy(
            self.ar(A_OF + h * NT + coff, C), pot[0][:, 0:128]))
        self.inc("vector", "ve")
        self.bank_release(pot, "ve")
        self.wait("vector", "ac", max(nkin, prep_ac))
        self.tbw("ktl", hp, "vector")
        self.emit("vector", lambda c: c.tensor_scalar(
            tb["ktl"][hp][:, 0:64], tb["ktb"][hp][:, 0:64],
            self.col(hp, 2), one("edG"), ALU.mult, ALU.mult))
        nktl = self.inc("vector", "ve")
        self.tbr("ktb", hp, "ve")
        yield
        pdS = self.bank()
        self.wait("tensor", "ve", max(nktl, nU16))
        self.emit("tensor", lambda c: c.matmul(
            pdS[0][pbase:pbase + 64, 0:128], tb["ktl"][hp][:, 0:64],
            tb["U16"][hp][:, :], start=True, stop=True,
            tile_position=(0, pbase)))
        ndS = self.inc("tensor", "pe")
        self.tbr("ktl", hp, "pe")
        self.tbr("U16", hp, "pe")
        self.wait("vector", "pe", ndS)
        self.wait("vector", "ac", prep_ac)
        self.emit("vector", lambda c: c.tensor_scalar(
            stB_sl, stB_sl,
            self.colt["etdB"][pbase:pbase + 64,
                              p * 16 + h:p * 16 + h + 1],
            None, ALU.mult))
        self.emit("vector", lambda c: c.tensor_tensor(
            stB_sl, stB_sl, pdS[0][pbase:pbase + 64, 0:128], ALU.add))
        nst = self.inc("vector", "ve")
        self.bank_release(pdS, "ve")
        yield
        self.wait("scalar", "ve", nst)
        self.emit("scalar", lambda c: c.copy(stB16_sl, stB_sl))
        ncast = self.inc("scalar", "ac")
        self.st_ready_b[h] = ("ac", ncast)

    def attn_b_out(self):
        of_ve = self.n_of("ve")
        for m in range(16):
            self.wait("scalar", "ve", of_ve)
            self.emit("scalar", lambda c, m_=m: c.activation(
                self.ar(A_GOUT + m_ * NT, NT), self.ar(A_GOUT + m_ * NT, NT),
                AF.Silu))
            nsl = self.inc("scalar", "ac")
            self.wait("vector", "ac", nsl)
            self.emit("vector", lambda c, m_=m: c.tensor_tensor(
                self.ar(A_OF + m_ * NT, NT), self.ar(A_OF + m_ * NT, NT),
                self.ar(A_GOUT + m_ * NT, NT), ALU.mult))
            self.inc("vector", "ve")
        og_ve = self.n_of("ve")
        last_add = None
        for m in range(16):
            buf, slot, mir = self.next_group()
            pbk = self.bank()
            self.wait("tensor", "ve", og_ve)
            for k in range(KC):
                self.emit("tensor", lambda c, k_=k: c.matmul(
                    pbk[0][:, 0:NT], buf[:, k_ * 128:(k_ + 1) * 128],
                    self.ar(A_OF + k_ * NT, NT), start=(k_ == 0),
                    stop=(k_ == KC - 1), skip_group_check=True))
            self.inc("tensor", "pe")
            self.group_done(slot, mir)
            nd = self.n_of("pe")
            xsl = self.xf[:, m * NT:(m + 1) * NT]
            self.wait("vector", "pe", nd)
            self.emit("vector", lambda c, x_=xsl: c.tensor_tensor(
                x_, x_, pbk[0][:, 0:NT], ALU.add))
            last_add = self.inc("vector", "ve")
            self.bank_release(pbk, "ve")
        self.xver = ("ve", last_add)

    def build(self):
        st = ExitStack()
        with st:
            self.alloc(st.enter_context)
            with self.nc.Block() as block:
                @block.sync
                def _(cur):
                    self.reset_pass("sync", cur)
                    self.program()

                @block.gpsimd
                def _(cur):
                    self.reset_pass("gpsimd", cur)
                    self.program()

                @block.vector
                def _(cur):
                    self.reset_pass("vector", cur)
                    self.program()

                @block.tensor
                def _(cur):
                    self.reset_pass("tensor", cur)
                    self.program()

                @block.scalar
                def _(cur):
                    self.reset_pass("scalar", cur)
                    self.program()
        return self.nc


def arrange_w(wT, nm, nk):
    K_in, M_out = wT.shape
    assert K_in == nk * 128 and M_out == nm * 128
    return np.ascontiguousarray(
        wT.reshape(nk, 128, nm, 128).transpose(2, 1, 0, 3)
        .reshape(nm, 128, nk * 128))


def arrange_small(wT):
    return np.ascontiguousarray(
        wT.reshape(16, 128, 16).transpose(1, 0, 2).reshape(128, 256))


def arrange_conv(cw):
    ch = cw.shape[0]
    nch = ch // 128
    return np.ascontiguousarray(
        cw[:, 0, :].reshape(nch, 128, 4).transpose(1, 0, 2)
        .reshape(128, nch * 4))


def make_inputs(core, full):
    f = lambda k: np.asarray(full[k], np.float32)
    x = f("L0_attn_a_x")[2 * core:2 * core + 2]
    xa = np.ascontiguousarray(x.transpose(0, 2, 1).reshape(2, 16, 128, 2048))
    d = {
        "x": xa,
        "wq": arrange_w(f("L0_attn_a_Wq"), 16, 16),
        "wk": arrange_w(f("L0_attn_a_Wk"), 16, 16),
        "wv": arrange_w(f("L0_attn_a_Wv"), 16, 16),
        "wg": arrange_w(f("L0_ffn_w_gate").T, 64, 16),
        "wu": arrange_w(f("L0_ffn_w_up").T, 64, 16),
        "wd": arrange_w(f("L0_ffn_w_down").T, 16, 64),
        "wqk": arrange_w(f("L0_attn_b_qk_w").T, 16, 16),
        "wvg": arrange_w(f("L0_attn_b_vg_w").T, 32, 16),
        "wo": arrange_w(f("L0_attn_b_o_w").T, 16, 16),
        "wal": arrange_small(f("L0_attn_a_Walpha")),
        "wbe": arrange_small(f("L0_attn_a_Wbeta")),
        "wa": arrange_small(f("L0_attn_b_a_w").T),
        "wb": arrange_small(f("L0_attn_b_b_w").T),
        "cq": arrange_conv(f("L0_attn_b_q_conv_w")),
        "ck": arrange_conv(f("L0_attn_b_k_conv_w")),
        "cv": arrange_conv(f("L0_attn_b_v_conv_w")),
        "bq": np.ascontiguousarray(f("L0_attn_a_bq").reshape(16, 128).T),
        "bk": np.ascontiguousarray(f("L0_attn_a_bk").reshape(16, 128).T),
        "bv": np.ascontiguousarray(f("L0_attn_a_bv").reshape(16, 128).T),
        "bal": f("L0_attn_a_balpha").reshape(16, 1),
        "bbe": f("L0_attn_a_bbeta").reshape(16, 1),
        "bbb": f("L0_attn_b_b_bias").reshape(16, 1),
        "bbt": f("L0_attn_b_beta_bias").reshape(16, 1),
        "bdt": f("L0_attn_b_dt_bias").reshape(16, 1),
        "alog": f("L0_attn_b_A_log").reshape(16, 1),
    }
    return {k: np.ascontiguousarray(v) for k, v in d.items()}


_NC_CACHE = {}


def get_nc(n_sublayers=4, n_tiles=NTILES):
    key = (n_sublayers, n_tiles)
    if key not in _NC_CACHE:
        _NC_CACHE[key] = MK(n_sublayers, n_tiles).build()
    return _NC_CACHE[key]


def kernel(**inputs):
    nc = get_nc()
    in_maps = [make_inputs(c, inputs) for c in range(8)]
    res = run_bass_kernel_spmd(nc, in_maps, core_ids=list(range(8)))
    out = np.empty((16, 2048, 2048), np.float32)
    for c in range(8):
        o = np.asarray(res.results[c]["out"])
        out[2 * c:2 * c + 2] = o.transpose(0, 3, 1, 2).reshape(2, 2048, 2048)
    return out
